# version 03
# baseline (speedup 1.0000x reference)
import math
from contextlib import ExitStack

import numpy as np
import concourse.bass as bass
import concourse.mybir as mybir
from concourse.bass_utils import run_bass_kernel_spmd

F32 = mybir.dt.float32
BF16 = mybir.dt.bfloat16
U8 = mybir.dt.uint8
AF = mybir.ActivationFunctionType
ALU = mybir.AluOpType
AX = mybir.AxisListType

NCORES = 8
T = 4096
D = 1024
DEBUG_TMPS = False
L3F_HEADS = 4
L3F_BLKS = 8
ENGINES = ("sync", "tensor", "vector", "scalar", "gpsimd")
ESEM = {"sync": "y", "tensor": "t", "vector": "v", "scalar": "s", "gpsimd": "g"}


class Prog:
    def __init__(self, name):
        self.name = name
        self.nc = bass.Bass("TRN2", target_bir_lowering=False)
        self.ops = {e: [] for e in ENGINES}
        self.stack = ExitStack()
        self.sem_handles = {}
        self.count = {}
        self.host_consts = {}
        self._nsb = 0
        for e in ENGINES:
            self.sem(ESEM[e])
        self.NDMASEM = 12
        self._dmarot = {e: 0 for e in ENGINES}
        for e in ENGINES:
            for i in range(self.NDMASEM):
                self.sem(f"q{ESEM[e]}{i}")

    def sem(self, name):
        if name not in self.sem_handles:
            self.sem_handles[name] = self.stack.enter_context(
                self.nc.semaphore(name))
            self.count[name] = 0
        return self.sem_handles[name]

    def sb(self, shape, dtype=F32, name=None):
        self._nsb += 1
        return self.stack.enter_context(
            self.nc.sbuf_tensor(name or f"sb{self._nsb}", shape, dtype))

    def psum(self, shape, dtype=F32, name=None):
        self._nsb += 1
        return self.stack.enter_context(
            self.nc.psum_tensor(name or f"ps{self._nsb}", shape, dtype))

    def dram_in(self, name, shape, dtype=F32):
        return self.nc.dram_tensor(name, list(shape), dtype,
                                   kind="ExternalInput").ap()

    def dram_out(self, name, shape, dtype=F32):
        return self.nc.dram_tensor(name, list(shape), dtype,
                                   kind="ExternalOutput").ap()

    def dram_tmp(self, name, shape, dtype=F32):
        kind = "ExternalOutput" if DEBUG_TMPS else "Internal"
        return self.nc.dram_tensor(name, list(shape), dtype, kind=kind).ap()

    def const(self, name, arr):
        arr = np.ascontiguousarray(arr, dtype=np.float32)
        self.host_consts[name] = arr
        return self.dram_in(name, arr.shape, F32)

    def op(self, eng, fn, after=(), inc=True):
        s = ESEM[eng]
        incs = []
        if inc:
            self.count[s] += 1
            incs = [(s, 1)]
        waits = [w for w in after if w is not None]
        for w in waits:
            self.sem(w[0])
        self.ops[eng].append((waits, fn, incs))
        return (s, self.count[s])

    def dma(self, q, out_ap, in_ap, sem=None, after=()):
        s = f"q{ESEM[q]}{self._dmarot[q] % self.NDMASEM}"
        self._dmarot[q] += 1
        prev = self.count[s]
        waits = [w for w in after if w is not None]
        if prev > 0:
            waits = waits + [(s, prev)]
        self.count[s] += 16
        self.ops[q].append(
            (waits, (lambda e, o=out_ap, i=in_ap: e.dma_start(out=o, in_=i)),
             [(s, 16)]))
        return (s, self.count[s])

    def at(self, sem):
        return (sem, self.count[sem])

    def eng_at(self, eng):
        return self.at(ESEM[eng])

    def bar(self, engines=ENGINES, name=None):
        toks = [(s, v) for s, v in self.count.items() if v > 0]
        for e in engines:
            waits = [(s, v) for (s, v) in toks if s != ESEM[e]]
            if waits:
                self.ops[e].append((waits, None, []))

    def finalize(self):
        nc = self.nc
        sems = self.sem_handles

        def run_engine(eng_obj, name):
            waited = {}
            for waits, fn, incs in self.ops[name]:
                for s, v in waits:
                    if waited.get(s, -1) < v:
                        eng_obj.wait_ge(sems[s], v)
                        waited[s] = v
                if fn is None:
                    continue
                inst = fn(eng_obj)
                for s, amt in incs:
                    inst.then_inc(sems[s], amt)

        with nc.Block() as block:
            @block.sync
            def _(e):
                run_engine(e, "sync")

            @block.tensor
            def _(e):
                run_engine(e, "tensor")

            @block.vector
            def _(e):
                run_engine(e, "vector")

            @block.scalar
            def _(e):
                run_engine(e, "scalar")

            @block.gpsimd
            def _(e):
                run_engine(e, "gpsimd")
        self.stack.close()
        return nc


from contextlib import contextmanager


@contextmanager
def scope(p):
    old = p.stack
    p.stack = ExitStack()
    try:
        yield
    finally:
        p.stack.close()
        p.stack = old


def ceil_div(a, b):
    return -(-a // b)


class Pool:

    def __init__(self, p, n, shape, dtype=F32, tag=""):
        n = max(n, 3)
        self.tiles = [p.sb(shape, dtype, name=f"{tag}{i}" if tag else None)
                      for i in range(n)]
        self.hist = []
        self.i = -1
        self.nacq = 0

    def acquire(self):
        self.i = (self.i + 1) % len(self.tiles)
        idx = self.nacq - len(self.tiles) + 1
        self.nacq += 1
        tok = self.hist[idx] if idx >= 0 else None
        return self.tiles[self.i], tok

    def used(self, tok):
        self.hist.append(tok)



def emit_ln_fm(p, x_hbm, xn_bf_hbm, xn_f32_hbm=None, Tt=T, eps=1e-5, name="ln"):
    DC = D // 128
    dm = name + "d"
    NBT = Tt // 512
    with scope(p):
        xp = Pool(p, 3, [128, 512], F32)
        x2p = Pool(p, 2, [128, 512], F32)
        obp = Pool(p, 2, [128, 512], BF16)
        ofp = Pool(p, 2, [128, 512], F32)
        nb = p.sb([128, 512], F32)
        onec = p.sb([128, 1], F32)
        ones1f = p.sb([1, 128], F32)
        stats_x = p.sb([1, Tt], F32)
        stats_x2 = p.sb([1, Tt], F32)
        arow = p.sb([1, Tt], F32)
        brow = p.sb([1, Tt], F32)
        mean = p.sb([1, Tt], F32)
        var = p.sb([1, Tt], F32)
        tmpr = p.sb([1, Tt], F32)
        pb_sx = p.psum([1, 512], F32)
        pb_sx2 = p.psum([1, 512], F32)
        pb_a = p.psum([128, 512], F32)
        pb_b = p.psum([128, 512], F32)
        p.op("vector", lambda e: e.memset(onec[:], 1.0))
        p.op("vector", lambda e: e.memset(ones1f[:], 1.0))
        onetok = p.eng_at("vector")
        evtok = None
        for tb in range(NBT):
            for c in range(DC):
                xt, w1 = xp.acquire()
                tload = p.dma("sync", xt[:],
                              x_hbm[c * 128:(c + 1) * 128,
                                    tb * 512:(tb + 1) * 512],
                              sem=dm, after=[w1])
                x2, w2 = x2p.acquire()
                sq = p.op("vector",
                          (lambda e, xt=xt, x2=x2: e.tensor_mul(
                              x2[:], xt[:], xt[:])),
                          after=[tload, w2])
                mm1 = p.op("tensor",
                           (lambda e, xt=xt, st=(c == 0): e.matmul(
                               pb_sx[:], onec[:], xt[:], start=st,
                               stop=(c == DC - 1))),
                           after=[tload, onetok] + ([evtok] if c == 0 else []),
                           inc=False)
                mm2 = p.op("tensor",
                           (lambda e, x2=x2, st=(c == 0), sp=(c == DC - 1):
                            e.matmul(pb_sx2[:], onec[:], x2[:], start=st,
                                     stop=sp)),
                           after=[sq])
                xp.used(mm2)
                x2p.used(mm2)
            mmt = p.eng_at("tensor")
            p.op("vector",
                 (lambda e, tb=tb: e.tensor_copy(
                     out=stats_x[0:1, tb * 512:(tb + 1) * 512], in_=pb_sx[:])),
                 after=[mmt])
            evtok = p.op("vector",
                         (lambda e, tb=tb: e.tensor_copy(
                             out=stats_x2[0:1, tb * 512:(tb + 1) * 512],
                             in_=pb_sx2[:])))
        stok = p.eng_at("vector")
        p.op("vector", lambda e: e.tensor_scalar_mul(mean[:], stats_x[:],
                                                     1.0 / D), after=[stok])
        p.op("vector", lambda e: e.tensor_scalar_mul(var[:], stats_x2[:],
                                                     1.0 / D))
        vt = p.eng_at("vector")
        sq2 = p.op("scalar", lambda e: e.activation(out=tmpr[:], in_=mean[:],
                                                    func=AF.Square), after=[vt])
        p.op("vector", lambda e: e.tensor_sub(var[:], var[:], tmpr[:]),
             after=[sq2])
        p.op("vector", lambda e: e.tensor_scalar_add(var[:], var[:],
                                                     float(eps)))
        vt = p.eng_at("vector")
        sq3 = p.op("scalar", lambda e: e.activation(out=tmpr[:], in_=var[:],
                                                    func=AF.Sqrt), after=[vt])
        p.op("vector", lambda e: e.reciprocal(out=arow[:], in_=tmpr[:]),
             after=[sq3])
        p.op("vector", lambda e: e.tensor_mul(brow[:], mean[:], arow[:]))
        p.op("vector", lambda e: e.tensor_scalar_mul(brow[:], brow[:], -1.0))
        abtok = p.eng_at("vector")
        for tb in range(NBT):
            tsl = slice(tb * 512, (tb + 1) * 512)
            p.op("tensor",
                 (lambda e, tsl=tsl: e.matmul(pb_a[:], ones1f[:], arow[:, tsl],
                                              start=True, stop=True)),
                 after=[abtok, p.eng_at("vector")], inc=False)
            bt = p.op("tensor",
                      (lambda e, tsl=tsl: e.matmul(pb_b[:], ones1f[:],
                                                   brow[:, tsl], start=True,
                                                   stop=True)))
            for c in range(DC):
                xt, w1 = xp.acquire()
                tload = p.dma("sync", xt[:], x_hbm[c * 128:(c + 1) * 128, tsl],
                              sem=dm, after=[w1])
                m1 = p.op("vector",
                          (lambda e, xt=xt: e.tensor_mul(nb[:], xt[:], pb_a[:])),
                          after=[tload, bt])
                xp.used(m1)
                ob, w3 = obp.acquire()
                a2 = p.op("vector",
                          (lambda e, ob=ob: e.tensor_tensor(
                              out=ob[:], in0=nb[:], in1=pb_b[:], op=ALU.add)),
                          after=[w3] if w3 else ())
                st1 = p.dma("gpsimd", xn_bf_hbm[c * 128:(c + 1) * 128, tsl],
                            ob[:], sem="st", after=[a2])
                obp.used(st1)
                if xn_f32_hbm is not None:
                    of, w4 = ofp.acquire()
                    a3 = p.op("vector",
                              (lambda e, of=of: e.tensor_tensor(
                                  out=of[:], in0=nb[:], in1=pb_b[:],
                                  op=ALU.add)),
                              after=[w4] if w4 else ())
                    st2 = p.dma("gpsimd", xn_f32_hbm[c * 128:(c + 1) * 128, tsl],
                                of[:], sem="st", after=[a3])
                    ofp.used(st2)



def gemm_fm(p, w_hbm, x_get, epilogue, K, NO, Tt, name, TBLK=4096,
            w_after=(), nbuf=2, w_bf16=False):
    with scope(p):
        return _gemm_fm(p, w_hbm, x_get, epilogue, K, NO, Tt, name, TBLK,
                        w_after, nbuf, w_bf16)


def _gemm_fm(p, w_hbm, x_get, epilogue, K, NO, Tt, name, TBLK=4096,
             w_after=(), nbuf=2, w_bf16=False):
    KC = K // 128
    NCH = NO // 128
    wd = name + "w"
    wst = None if w_bf16 else Pool(p, nbuf, [128, KC * 128], F32,
                                   tag=name + "ws")
    wbf = Pool(p, nbuf, [128, KC * 128], BF16, tag=name + "wb")
    NB = min(8, TBLK // 512)
    banks = [p.psum([128, 512], F32) for _ in range(NB)]
    banks_tok = None
    for ts0 in range(0, Tt, TBLK):
        tblk = min(TBLK, Tt - ts0)
        nbk = ceil_div(tblk, 512)
        for no in range(NCH):
            if w_bf16:
                wb, wbwar = wbf.acquire()
                for c in range(KC):
                    tc = p.dma("sync", wb[:, c * 128:(c + 1) * 128],
                               w_hbm[c * 128:(c + 1) * 128,
                                     no * 128:(no + 1) * 128], sem=wd,
                               after=([wbwar] + list(w_after))
                               if c == 0 else ())
            else:
                wt, wwar = wst.acquire()
                for c in range(KC):
                    tw = p.dma("sync", wt[:, c * 128:(c + 1) * 128],
                               w_hbm[c * 128:(c + 1) * 128,
                                     no * 128:(no + 1) * 128], sem=wd,
                               after=([wwar] + list(w_after))
                               if c == 0 else ())
                wb, wbwar = wbf.acquire()
                tc = p.op("vector",
                          (lambda e, wb=wb, wt=wt: e.tensor_copy(out=wb[:],
                                                                 in_=wt[:])),
                          after=[tw, wbwar])
                wst.used(tc)
            mmfirst_extra = [banks_tok] if banks_tok else []
            last = None
            for k in range(KC):
                xap, xtok = x_get(k, ts0, tblk)
                for b in range(nbk):
                    t0 = b * 512
                    bw = min(512, tblk - t0)
                    first = (k == 0 and b == 0)
                    aft = ([tc, xtok] + mmfirst_extra) if first else \
                        ([xtok] if b == 0 else ())
                    last = p.op(
                        "tensor",
                        (lambda e, bank=banks[b], wb=wb, k=k, xap=xap, t0=t0,
                         bw=bw, st=(k == 0), sp=(k == KC - 1):
                         e.matmul(bank[:, :bw], wb[:, k * 128:(k + 1) * 128],
                                  xap[:, t0:t0 + bw], start=st, stop=sp)),
                        after=aft, inc=(k == KC - 1 and b == nbk - 1))
            wbf.used(last)
            banks_tok = epilogue(
                no, ts0, [(banks[b], ts0 + b * 512, min(512, tblk - b * 512))
                          for b in range(nbk)], last)
    return banks_tok



def gemm_tm(p, w_bf_get, x_get, epilogue, K, NO, Tt, name):
    with scope(p):
        return _gemm_tm(p, w_bf_get, x_get, epilogue, K, NO, Tt, name)


def _gemm_tm(p, w_bf_get, x_get, epilogue, K, NO, Tt, name):
    KC = K // 128
    NB = ceil_div(NO, 512)
    assert NB <= 8
    banks = [p.psum([128, 512], F32) for _ in range(NB)]
    banks_tok = None
    TC = Tt // 128
    for tch in range(TC):
        last = None
        for k in range(KC):
            xap, xtok = x_get(k, 0, Tt)
            wap, wtok = w_bf_get(k)
            for b in range(NB):
                n0 = b * 512
                nw = min(512, NO - n0)
                first = (k == 0 and b == 0)
                aft = ([xtok, wtok] + ([banks_tok] if banks_tok else [])) \
                    if first else ()
                last = p.op(
                    "tensor",
                    (lambda e, bank=banks[b], xap=xap, wap=wap, tch=tch, n0=n0,
                     nw=nw, st=(k == 0), sp=(k == KC - 1):
                     e.matmul(bank[:, :nw],
                              xap[:, tch * 128:(tch + 1) * 128],
                              wap[:, n0:n0 + nw], start=st, stop=sp)),
                    after=aft, inc=(k == KC - 1 and b == NB - 1))
        banks_tok = epilogue(
            tch, [(banks[b], b * 512, min(512, NO - b * 512))
                  for b in range(NB)], last)
    return banks_tok



def load_fm_resident(p, hbm_bf, Dk, Tt, name, after=()):
    KC = Dk // 128
    buf = p.sb([128, KC * Tt], BF16, name=name)
    toks = []
    for c in range(KC):
        toks.append(p.dma("sync", buf[:, c * Tt:(c + 1) * Tt],
                          hbm_bf[c * 128:(c + 1) * 128, :], sem=name + "d",
                          after=after))
    tok = toks[-1]

    def get(k, t0, tw):
        return buf[:, k * Tt + t0:k * Tt + t0 + tw], tok
    return get, tok, buf



def emit_colsq(p, src_bf_hbm, Dk, Tt, out_row, name):
    DC = Dk // 128
    p.bar()
    with scope(p):
        xp = Pool(p, 2, [128, 512], BF16, tag=name + "x")
        x2p = Pool(p, 2, [128, 512], F32, tag=name + "q")
        onec = p.sb([128, 1], F32)
        fence = p.sb([1, 8], F32)
        pb = p.psum([1, 512], F32)
        p.op("vector", lambda e: e.memset(onec[:], 1.0))
        onetok = p.eng_at("vector")
        evtok = None
        for tb in range(Tt // 512):
            for c in range(DC):
                xt, w1 = xp.acquire()
                tl = p.dma("sync", xt[:],
                           src_bf_hbm[c * 128:(c + 1) * 128,
                                      tb * 512:(tb + 1) * 512],
                           sem=name + "d", after=[w1])
                x2, w2 = x2p.acquire()
                p.op("vector",
                     (lambda e, xt=xt, x2=x2: e.tensor_mul(
                         x2[:], xt[:], xt[:])),
                     after=[tl, w2])
                sq = p.op("vector",
                          (lambda e, x2=x2: e.tensor_copy(
                              out=fence[0:1, 0:8], in_=x2[0:1, 504:512])))
                mm = p.op("tensor",
                          (lambda e, x2=x2, st=(c == 0), sp=(c == DC - 1):
                           e.matmul(pb[:], onec[:], x2[:], start=st, stop=sp)),
                          after=[sq, onetok] + ([evtok] if c == 0 and evtok
                                                else []))
                xp.used(sq)
                x2p.used(mm)
            evtok = p.op("vector",
                         (lambda e, tb=tb: e.tensor_copy(
                             out=out_row[0:1, tb * 512:(tb + 1) * 512],
                             in_=pb[:])),
                         after=[p.eng_at("tensor")])
        return evtok



def emit_l0_attn(p, io):
    x_in = io["x"]
    x_out = io["xo"]
    w_attn = io["w_attn"]
    b_attn = io["b_attn_t"]
    w_proj = io["w_proj"]
    b_proj = io["b_proj_t"]
    mask4 = io["l0a_mask"]
    xn_bf = p.dram_tmp("l0a_xn", [D, T], BF16)
    q_h = p.dram_tmp("l0a_q", [D, T], BF16)
    k_h = p.dram_tmp("l0a_k", [D, T], BF16)
    v_h = p.dram_tmp("l0a_v", [T, D], BF16)
    o_h = p.dram_tmp("l0a_o", [D, T], BF16)

    emit_ln_fm(p, x_in, xn_bf, None, name="l0aln")
    p.bar()

    with scope(p):
        xn_get, xn_tok, _ = load_fm_resident(p, xn_bf, D, T, "l0axn")
        btile = p.sb([128, 24], F32)
        btok = p.dma("sync", btile[:], b_attn, sem="bq")
        obp = Pool(p, 3, [128, 512], BF16, tag="l0aqke")

        def qk_epi(no, ts0, banks, mmtok):
            last = None
            for (bank, t0, tw) in banks:
                ob, war = obp.acquire()
                is_q = no < 8
                dst = q_h if is_q else k_h
                c = no if is_q else no - 8
                if is_q:
                    last = p.op("vector",
                                (lambda e, bank=bank, ob=ob, no=no, btile=btile:
                                 e.tensor_scalar(
                                     out=ob[:, :512], in0=bank[:, :512],
                                     scalar1=btile[:, no:no + 1],
                                     scalar2=1.0 / 32.0,
                                     op0=ALU.add, op1=ALU.mult)),
                                after=[mmtok, btok, war])
                else:
                    last = p.op("vector",
                                (lambda e, bank=bank, ob=ob, no=no, btile=btile:
                                 e.tensor_scalar(
                                     out=ob[:, :512], in0=bank[:, :512],
                                     scalar1=btile[:, no:no + 1], scalar2=None,
                                     op0=ALU.add)),
                                after=[mmtok, btok, war])
                stok = p.dma("gpsimd", dst[c * 128:(c + 1) * 128, t0:t0 + tw],
                             ob[:, :tw], sem="st", after=[last])
                obp.used(stok)
            return last

        gemm_fm(p, w_attn[:, 0:2048],
                lambda k, t0, tw: xn_get(k, t0, tw),
                qk_epi, K=D, NO=2048, Tt=T, name="l0aqk")

        wv_st = p.sb([128, 1024], F32)
        wv_bf = p.sb([128, 8 * 1024], BF16)
        wv_toks = []
        for k in range(8):
            tl = p.dma("sync", wv_st[:],
                       w_attn[k * 128:(k + 1) * 128, 2048:3072], sem="wv",
                       after=[p.eng_at("vector")])
            wv_toks.append(p.op("vector",
                                (lambda e, k=k: e.tensor_copy(
                                    out=wv_bf[:, k * 1024:(k + 1) * 1024],
                                    in_=wv_st[:])),
                                after=[tl]))
        wvtok = wv_toks[-1]
        ones1f = p.sb([1, 128], F32)
        bvrow = p.sb([1, 1024], F32)
        bv_bc = p.sb([128, 1024], F32)
        pb_bc = p.psum([128, 512], F32)
        p.op("vector", lambda e: e.memset(ones1f[:], 1.0))
        bvt = p.dma("sync", bvrow[:], io["b_attn_v"], sem="bq")
        for h in range(2):
            mm = p.op("tensor",
                      (lambda e, h=h: e.matmul(
                          pb_bc[:], ones1f[:],
                          bvrow[:, h * 512:(h + 1) * 512],
                          start=True, stop=True)),
                      after=[bvt, p.eng_at("vector")])
            p.op("vector",
                 (lambda e, h=h: e.tensor_copy(
                     out=bv_bc[:, h * 512:(h + 1) * 512], in_=pb_bc[:])),
                 after=[mm])
        bvbc_tok = p.eng_at("vector")

        def v_epi(tch, banks, mmtok):
            last = None
            for (bank, n0, nw) in banks:
                ob, war = obp.acquire()
                last = p.op("vector",
                            (lambda e, bank=bank, ob=ob, n0=n0, nw=nw, bv_bc=bv_bc:
                             e.tensor_tensor(out=ob[:, :nw], in0=bank[:, :nw],
                                             in1=bv_bc[:, n0:n0 + nw],
                                             op=ALU.add)),
                            after=[mmtok, bvbc_tok, war])
                stok = p.dma("gpsimd", v_h[tch * 128:(tch + 1) * 128,
                                           n0:n0 + nw],
                             ob[:, :nw], sem="st", after=[last])
                obp.used(stok)
            return last

        gemm_tm(p, (lambda k: (wv_bf[:, k * 1024:(k + 1) * 1024], wvtok)),
                lambda k, t0, tw: xn_get(k, t0, tw),
                v_epi, K=D, NO=1024, Tt=T, name="l0av")
    p.bar()

    with scope(p):
        negu_bf = p.sb([1, T], BF16)
        with scope(p):
            qn = p.sb([1, T], F32)
            kn = p.sb([1, T], F32)
            kmax = p.sb([1, 1], F32)
            urow = p.sb([1, T], F32)
            emit_colsq(p, q_h, D, T, qn, "l0aqn")
            emit_colsq(p, k_h, D, T, kn, "l0akn")
            vt = p.eng_at("vector")
            p.op("vector", lambda e: e.tensor_reduce(out=kmax[:], in_=kn[:],
                                                     axis=AX.X, op=ALU.max),
                 after=[vt])
            vt = p.eng_at("vector")
            s1 = p.op("scalar", lambda e: e.activation(out=kmax[:], in_=kmax[:],
                                                       func=AF.Sqrt), after=[vt])
            s2 = p.op("scalar", lambda e: e.activation(out=urow[:], in_=qn[:],
                                                       func=AF.Sqrt), after=[vt])
            p.op("vector",
                 lambda e: e.tensor_scalar(out=urow[:], in0=urow[:],
                                           scalar1=kmax[0:1, 0:1], scalar2=None,
                                           op0=ALU.mult),
                 after=[s1, s2])
            p.op("vector",
                 lambda e: e.tensor_scalar(out=negu_bf[:], in0=urow[:],
                                           scalar1=-1.0, scalar2=None,
                                           op0=ALU.mult))
            negu_tok = p.eng_at("vector")
            if DEBUG_TMPS:
                dbg_negu = p.dram_out("dbg_negu", [1, T], BF16)
                p.dma("gpsimd", dbg_negu, negu_bf[:], sem="st",
                      after=[negu_tok])
                dbg_qn = p.dram_out("dbg_qn", [1, T], F32)
                p.dma("gpsimd", dbg_qn, qn[:], sem="st", after=[negu_tok])
                dbg_kn = p.dram_out("dbg_kn", [1, T], F32)
                p.dma("gpsimd", dbg_kn, kn[:], sem="st", after=[negu_tok])

        p.bar()
        k_get, k_tok, k_all = load_fm_resident(p, k_h, D, T, "l0akr")
        v_all = p.sb([128, 32 * 1024], BF16, name="l0avr")
        vtoks = []
        for c in range(32):
            vtoks.append(p.dma("sync", v_all[:, c * 1024:(c + 1) * 1024],
                               v_h[c * 128:(c + 1) * 128, :], sem="l0avd"))
        v_tok = vtoks[-1]
        mask_sb = p.sb([128, 4 * 512], F32)
        mtok = p.dma("sync", mask_sb[:], mask4, sem="bq")
        ones1b = p.sb([1, 128], BF16)
        onecb = p.sb([128, 1], BF16)
        ones1f2 = p.sb([1, 128], F32)
        p.op("vector", lambda e: e.memset(ones1b[:], 1.0))
        p.op("vector", lambda e: e.memset(onecb[:], 1.0))
        p.op("vector", lambda e: e.memset(ones1f2[:], 1.0))
        ones_tok = p.eng_at("vector")
        E = p.sb([128, 32 * 512], BF16, name="l0aE")
        qblk = p.sb([128, 8 * 512], BF16, name="l0aqb")
        rb_sb = p.sb([128, 512], F32)
        den_row = p.sb([1, 512], F32)

        for B in range(8):
            qt0 = B * 512
            nkt = (qt0 + 512) // 128
            with scope(p):
                pb_s = [p.psum([128, 512], F32) for _ in range(2)]
                pb_den = p.psum([1, 512], F32)
                qtoks = []
                for c in range(8):
                    qtoks.append(p.dma(
                        "sync", qblk[:, c * 512:(c + 1) * 512],
                        q_h[c * 128:(c + 1) * 128, qt0:qt0 + 512],
                        sem="l0aqd", after=[p.eng_at("tensor")]))
                q_tok = qtoks[-1]
                etoks = []
                for ktc in range(nkt):
                    bank = pb_s[ktc % 2]
                    for c in range(8):
                        p.op("tensor",
                             (lambda e, bank=bank, c=c, ktc=ktc:
                              e.matmul(bank[:],
                                       k_all[:, c * T + ktc * 128:
                                             c * T + (ktc + 1) * 128],
                                       qblk[:, c * 512:(c + 1) * 512],
                                       start=(c == 0), stop=False)),
                             after=([k_tok, q_tok, ones_tok, negu_tok,
                                     mtok, v_tok]
                                    + ([etoks[ktc - 2]] if ktc >= 2 else []))
                             if c == 0 else (), inc=False)
                    mmt = p.op("tensor",
                               (lambda e, bank=bank, qt0=qt0: e.matmul(
                                   bank[:], ones1b[:],
                                   negu_bf[0:1, qt0:qt0 + 512],
                                   start=False, stop=True)))
                    j = ktc * 128 - qt0
                    if j >= 0:
                        mi = j // 128
                        tmpf = p.sb([128, 512], F32)
                        a1 = p.op("vector",
                                  (lambda e, bank=bank, tmpf=tmpf, mi=mi:
                                   e.tensor_tensor(
                                       out=tmpf[:], in0=bank[:],
                                       in1=mask_sb[:, mi * 512:(mi + 1) * 512],
                                       op=ALU.add)),
                                  after=[mmt])
                        et = p.op("scalar",
                                  (lambda e, tmpf=tmpf, ktc=ktc:
                                   e.activation(
                                       out=E[:, ktc * 512:(ktc + 1) * 512],
                                       in_=tmpf[:], func=AF.Exp)),
                                  after=[a1])
                    else:
                        et = p.op("scalar",
                                  (lambda e, bank=bank, ktc=ktc: e.activation(
                                      out=E[:, ktc * 512:(ktc + 1) * 512],
                                      in_=bank[:], func=AF.Exp)),
                                  after=[mmt])
                    etoks.append(et)
                    p.op("tensor",
                         (lambda e, ktc=ktc, st=(ktc == 0),
                          sp=(ktc == nkt - 1): e.matmul(
                             pb_den[:], onecb[:],
                             E[:, ktc * 512:(ktc + 1) * 512],
                             start=st, stop=sp)),
                         after=[et], inc=(ktc == nkt - 1))
                dent = p.eng_at("tensor")
                d1 = p.op("vector", lambda e: e.tensor_copy(out=den_row[:],
                                                            in_=pb_den[:]),
                          after=[dent])
                d2 = p.op("vector", (lambda e: e.reciprocal(
                    out=den_row[:], in_=den_row[:])), after=[d1])
                d3 = p.op("vector", (lambda e: e.tensor_copy(
                    out=rb_sb[0:1, 0:8], in_=den_row[0:1, 0:8])),
                    after=[d2])
                pb_r = p.psum([128, 512], F32)
                mr = p.op("tensor",
                          (lambda e, pb_r=pb_r: e.matmul(
                              pb_r[:], ones1f2[:], den_row[:],
                              start=True, stop=True)),
                          after=[d3])
                rbt = p.op("vector", lambda e: e.tensor_copy(out=rb_sb[:],
                                                             in_=pb_r[:]),
                           after=[mr])
                if DEBUG_TMPS and B == 0:
                    dbg_E = p.dram_out("dbg_E", [128, 32 * 512], BF16)
                    p.dma("gpsimd", dbg_E, E[:], sem="st", after=[rbt])
                    dbg_den = p.dram_out("dbg_den", [1, 512], F32)
                    p.dma("gpsimd", dbg_den, den_row[:], sem="st", after=[rbt])
            with scope(p):
                pb_av = [p.psum([128, 512], F32) for _ in range(8)]
                obp2 = Pool(p, 3, [128, 512], BF16, tag=f"l0aav{B}_")
                avt = None
                for dv in range(8):
                    for ktc in range(nkt):
                        avt = p.op(
                            "tensor",
                            (lambda e, dv=dv, ktc=ktc: e.matmul(
                                pb_av[dv][:],
                                v_all[:, ktc * 1024 + dv * 128:
                                      ktc * 1024 + (dv + 1) * 128],
                                E[:, ktc * 512:(ktc + 1) * 512],
                                start=(ktc == 0), stop=(ktc == nkt - 1))),
                            after=[rbt] if (dv == 0 and ktc == 0) else (),
                            inc=(ktc == nkt - 1))
                    ob, war = obp2.acquire()
                    ot = p.op("vector",
                              (lambda e, dv=dv, ob=ob: e.tensor_mul(
                                  ob[:], pb_av[dv][:], rb_sb[:])),
                              after=[avt, war])
                    stok = p.dma("gpsimd", o_h[dv * 128:(dv + 1) * 128,
                                               qt0:qt0 + 512],
                                 ob[:], sem="st", after=[ot])
                    obp2.used(stok)
            p.bar(("tensor", "vector", "scalar"))
    p.bar()

    with scope(p):
        o_get, o_tok, _ = load_fm_resident(p, o_h, D, T, "l0aor")
        btile = p.sb([128, 8], F32)
        btok = p.dma("sync", btile[:], b_proj, sem="bq")
        xrp = Pool(p, 2, [128, 512], F32, tag="l0axr")
        ofp = Pool(p, 2, [128, 512], F32, tag="l0aof")

        def proj_epi(no, ts0, banks, mmtok):
            last = None
            for (bank, t0, tw) in banks:
                xr, w1 = xrp.acquire()
                tl = p.dma("sync", xr[:],
                           x_in[no * 128:(no + 1) * 128, t0:t0 + tw],
                           sem="l0axd", after=[w1])
                of, w2 = ofp.acquire()
                a1 = p.op("vector",
                          (lambda e, bank=bank, of=of, no=no, btile=btile: e.tensor_scalar(
                              out=of[:], in0=bank[:, :512],
                              scalar1=btile[:, no:no + 1], scalar2=None,
                              op0=ALU.add)),
                          after=[mmtok, btok, w2])
                last = p.op("vector",
                            (lambda e, of=of, xr=xr: e.tensor_add(
                                of[:], of[:], xr[:])),
                            after=[tl])
                xrp.used(last)
                st1 = p.dma("gpsimd", x_out[no * 128:(no + 1) * 128,
                                            t0:t0 + tw],
                            of[:], sem="st", after=[last])
                ofp.used(st1)
            return last

        gemm_fm(p, w_proj, lambda k, t0, tw: o_get(k, t0, tw), proj_epi,
                K=D, NO=D, Tt=T, name="l0ap")
    p.bar()


def l0_attn_host_inputs(inp):
    w_attn = np.ascontiguousarray(inp["L0_attn_w_attn"])
    b = np.asarray(inp["L0_attn_b_attn"])
    b_qk = np.ascontiguousarray(b[:2048].reshape(16, 128).T)
    pad = np.zeros((128, 24), np.float32)
    pad[:, :16] = b_qk
    b_v = np.ascontiguousarray(b[2048:].reshape(1, 1024))
    w_proj = np.ascontiguousarray(inp["L0_attn_w_proj"])
    b_proj = np.ascontiguousarray(inp["L0_attn_b_proj"].reshape(8, 128).T)
    m = np.zeros((128, 4 * 512), np.float32)
    for mi, j in enumerate((0, 128, 256, 384)):
        kt = np.arange(128)[:, None] + j
        qt = np.arange(512)[None, :]
        m[:, mi * 512:(mi + 1) * 512] = np.where(kt <= qt, 0.0, -1e10)
    return {"w_attn": w_attn, "b_attn_t": pad, "b_attn_v": b_v,
            "w_proj": w_proj, "b_proj_t": b_proj, "l0a_mask": m}


def build_l0_attn_stage():
    p = Prog("l0_attn")
    io = {}
    io["x"] = p.dram_in("x", [D, T], F32)
    io["xo"] = p.dram_out("xo", [D, T], F32)
    io["w_attn"] = p.dram_in("w_attn", [1024, 3072], F32)
    io["b_attn_t"] = p.dram_in("b_attn_t", [128, 24], F32)
    io["b_attn_v"] = p.dram_in("b_attn_v", [1, 1024], F32)
    io["w_proj"] = p.dram_in("w_proj", [1024, 1024], F32)
    io["b_proj_t"] = p.dram_in("b_proj_t", [128, 8], F32)
    io["l0a_mask"] = p.dram_in("l0a_mask", [128, 4 * 512], F32)
    emit_l0_attn(p, io)
    return p



def emit_l2_attn(p, io):
    x_in = io["x"]
    x_out = io["xo"]
    Wqkv = io["l2a_wqkv"]
    bqkv = io["l2a_bqkv"]
    Wproj = io["l2a_wproj"]
    bproj = io["l2a_bproj"]
    mask4 = io["l0a_mask"]
    xn_bf = p.dram_tmp("l2a_xn", [D, T], BF16)
    q_h = p.dram_tmp("l2a_q", [T, D], BF16)
    k_h = p.dram_tmp("l2a_k", [T, D], BF16)
    v_h = p.dram_tmp("l2a_v", [T, D], BF16)
    o_h = p.dram_tmp("l2a_o", [D, T], BF16)

    emit_ln_fm(p, x_in, xn_bf, None, name="l2aln")
    p.bar()

    with scope(p):
        xn_get, xn_tok, _ = load_fm_resident(p, xn_bf, D, T, "l2axn")
        wst = p.sb([128, 3072], F32)
        w_bf = p.sb([128, 8 * 3072], BF16)
        wtoks = []
        for k in range(8):
            tl = p.dma("sync", wst[:], Wqkv[k * 128:(k + 1) * 128, :],
                       after=[p.eng_at("vector")] if k else ())
            wtoks.append(p.op("vector",
                              (lambda e, k=k, wst=wst, w_bf=w_bf: e.tensor_copy(
                                  out=w_bf[:, k * 3072:(k + 1) * 3072],
                                  in_=wst[:])),
                              after=[tl]))
        wtok = wtoks[-1]
        ones1f = p.sb([1, 128], F32)
        brow = p.sb([1, 3072], F32)
        b_bc = p.sb([128, 3072], F32)
        pb_bc = p.psum([128, 512], F32)
        p.op("vector", lambda e: e.memset(ones1f[:], 1.0))
        bt = p.dma("sync", brow[:], bqkv)
        for h in range(6):
            mm = p.op("tensor",
                      (lambda e, h=h: e.matmul(pb_bc[:], ones1f[:],
                                               brow[:, h * 512:(h + 1) * 512],
                                               start=True, stop=True)),
                      after=[bt, p.eng_at("vector")])
            p.op("vector",
                 (lambda e, h=h: e.tensor_copy(out=b_bc[:, h * 512:(h + 1) * 512],
                                               in_=pb_bc[:])),
                 after=[mm])
        bbct = p.eng_at("vector")
        obp = Pool(p, 3, [128, 512], BF16, tag="l2aqkv")

        def qkv_epi(tch, banks, mmtok):
            last = None
            for (bank, n0, nw) in banks:
                ob, war = obp.acquire()
                last = p.op("vector",
                            (lambda e, bank=bank, ob=ob, n0=n0, nw=nw:
                             e.tensor_tensor(out=ob[:, :nw], in0=bank[:, :nw],
                                             in1=b_bc[:, n0:n0 + nw],
                                             op=ALU.add)),
                            after=[mmtok, bbct, war])
                which = n0 // 1024
                dst = (q_h, k_h, v_h)[which]
                c0 = n0 % 1024
                stok = p.dma("gpsimd",
                             dst[tch * 128:(tch + 1) * 128, c0:c0 + nw],
                             ob[:, :nw], sem="st", after=[last])
                obp.used(stok)
            return last

        gemm_tm(p, (lambda k: (w_bf[:, k * 3072:(k + 1) * 3072], wtok)),
                lambda k, t0, tw: xn_get(k, t0, tw),
                qkv_epi, K=D, NO=3072, Tt=T, name="l2aqkv")
    p.bar()

    q_heads = q_h.rearrange("(h u) d -> h (u d)", h=16)
    k_heads = k_h.rearrange("(h u) d -> h (u d)", h=16)
    v_heads = v_h.rearrange("(h u) d -> h (u d)", h=16)
    with scope(p):
        ident = p.sb([128, 128], BF16)
        itok = p.dma("sync", ident[:], io["ident_bf"])
        ones1f2 = p.sb([1, 128], F32)
        onecb = p.sb([128, 1], BF16)
        onecf64 = p.sb([64, 1], F32)
        p.op("vector", lambda e: e.memset(ones1f2[:], 1.0))
        p.op("vector", lambda e: e.memset(onecb[:], 1.0))
        p.op("vector", lambda e: e.memset(onecf64[:], 1.0))
        ones_tok = p.eng_at("vector")
        mask_sb = p.sb([128, 4 * 512], F32)
        mtok = p.dma("sync", mask_sb[:], mask4)
        qT = p.sb([64, T], BF16, name="l2aqT")
        kT = p.sb([64, T], BF16, name="l2akT")
        vres = p.sb([128, 32 * 64], BF16, name="l2avr")
        E = p.sb([128, 32 * 512], BF16, name="l2aE")
        rb_sb = p.sb([128, 512], F32)
        den_row = p.sb([1, 512], F32)
        nu_col = p.sb([128, 1], F32)
        for h in range(16):
            with scope(p):
                pb_t = p.psum([128, 128], BF16, name=f"l2att{h}")
                ldp = Pool(p, 3, [128, 64], BF16, tag=f"l2ald{h}_")
                for (src, dstT, scale) in ((q_heads, qT, 0.125),
                                           (k_heads, kT, None)):
                    for c in range(32):
                        tile, war = ldp.acquire()
                        tl = p.dma("sync", tile[:],
                                   src[h:h + 1,
                                       c * 128 * 64:(c + 1) * 128 * 64]
                                   .rearrange("o (t d) -> (o t) d", d=64),
                                   after=[war])
                        mm = p.op("tensor",
                                  (lambda e, tile=tile: e.transpose(
                                      pb_t[0:64, 0:128], tile[:], ident[:])),
                                  after=[tl, itok, p.eng_at("vector")])
                        ldp.used(mm)
                        if scale is None:
                            p.op("vector",
                                 (lambda e, dstT=dstT, c=c: e.tensor_copy(
                                     out=dstT[:, c * 128:(c + 1) * 128],
                                     in_=pb_t[0:64, 0:128])),
                                 after=[mm])
                        else:
                            p.op("vector",
                                 (lambda e, dstT=dstT, c=c, scale=scale:
                                  e.tensor_scalar(
                                      out=dstT[:, c * 128:(c + 1) * 128],
                                      in0=pb_t[0:64, 0:128], scalar1=scale,
                                      scalar2=None, op0=ALU.mult)),
                                 after=[mm])
                tqk = p.eng_at("vector")
                pb_n = p.psum([1, 512], F32)
                sq_t = p.sb([64, 512], F32)
                nrow = p.sb([1, T], F32)
                numax = p.sb([1, 2], F32)
                for (srcT, col) in ((qT, 0), (kT, 1)):
                    evt = None
                    for tb in range(8):
                        sqo = p.op("vector",
                                   (lambda e, srcT=srcT, tb=tb: e.tensor_mul(
                                       sq_t[:], srcT[:, tb * 512:(tb + 1) * 512],
                                       srcT[:, tb * 512:(tb + 1) * 512])),
                                   after=[tqk] + ([evt] if evt else []))
                        mm = p.op("tensor",
                                  (lambda e: e.matmul(pb_n[:], onecf64[:],
                                                      sq_t[:], start=True,
                                                      stop=True)),
                                  after=[sqo])
                        evt = p.op("vector",
                                   (lambda e, tb=tb: e.tensor_copy(
                                       out=nrow[0:1, tb * 512:(tb + 1) * 512],
                                       in_=pb_n[:])),
                                   after=[mm])
                    p.op("vector",
                         (lambda e, col=col: e.tensor_reduce(
                             out=numax[0:1, col:col + 1], in_=nrow[:],
                             axis=AX.X, op=ALU.max)),
                         after=[evt])
                s1 = p.op("scalar", lambda e: e.activation(
                    out=numax[:], in_=numax[:], func=AF.Sqrt),
                    after=[p.eng_at("vector")])
                p.op("vector", lambda e: e.tensor_scalar(
                    out=numax[0:1, 0:1], in0=numax[0:1, 0:1],
                    scalar1=numax[0:1, 1:2], scalar2=-1.0, op0=ALU.mult,
                    op1=ALU.mult), after=[s1])
                pb_u = p.psum([128, 1], F32)
                mu = p.op("tensor", lambda e: e.matmul(
                    pb_u[:], ones1f2[:], numax[0:1, 0:1], start=True,
                    stop=True), after=[p.eng_at("vector")])
                nut = p.op("vector", lambda e: e.tensor_copy(out=nu_col[:],
                                                             in_=pb_u[:]),
                           after=[mu])
                vtoks = []
                vldp_war = p.eng_at("tensor")
                for c in range(32):
                    vtoks.append(p.dma(
                        "sync", vres[:, c * 64:(c + 1) * 64],
                        v_heads[h:h + 1, c * 128 * 64:(c + 1) * 128 * 64]
                        .rearrange("o (t d) -> (o t) d", d=64),
                        after=[p.eng_at("vector")] if c == 0 else ()))
                v_tok = vtoks[-1]
                for B in range(8):
                    qt0 = B * 512
                    nkt = (qt0 + 512) // 128
                    with scope(p):
                        pb_s = [p.psum([128, 512], F32) for _ in range(2)]
                        pb_den = p.psum([1, 512], F32)
                        etoks = []
                        for ktc in range(nkt):
                            bank = pb_s[ktc % 2]
                            mmt = p.op(
                                "tensor",
                                (lambda e, bank=bank, ktc=ktc, qt0=qt0:
                                 e.matmul(bank[:],
                                          kT[:, ktc * 128:(ktc + 1) * 128],
                                          qT[:, qt0:qt0 + 512],
                                          start=True, stop=True)),
                                after=[tqk, v_tok, nut, mtok, ones_tok]
                                + ([etoks[ktc - 2]] if ktc >= 2 else []))
                            j = ktc * 128 - qt0
                            if j >= 0:
                                mi = j // 128
                                tmpf = p.sb([128, 512], F32)
                                a1 = p.op("vector",
                                          (lambda e, bank=bank, tmpf=tmpf,
                                           mi=mi: e.tensor_tensor(
                                              out=tmpf[:], in0=bank[:],
                                              in1=mask_sb[:, mi * 512:
                                                          (mi + 1) * 512],
                                              op=ALU.add)),
                                          after=[mmt])
                                et = p.op("scalar",
                                          (lambda e, tmpf=tmpf, ktc=ktc:
                                           e.activation(
                                              out=E[:, ktc * 512:(ktc + 1) * 512],
                                              in_=tmpf[:], func=AF.Exp,
                                              bias=nu_col[:, 0:1])),
                                          after=[a1])
                            else:
                                et = p.op("scalar",
                                          (lambda e, bank=bank, ktc=ktc:
                                           e.activation(
                                              out=E[:, ktc * 512:(ktc + 1) * 512],
                                              in_=bank[:], func=AF.Exp,
                                              bias=nu_col[:, 0:1])),
                                          after=[mmt])
                            etoks.append(et)
                            p.op("tensor",
                                 (lambda e, ktc=ktc, st=(ktc == 0),
                                  sp=(ktc == nkt - 1): e.matmul(
                                     pb_den[:], onecb[:],
                                     E[:, ktc * 512:(ktc + 1) * 512],
                                     start=st, stop=sp)),
                                 after=[et])
                        dent = p.eng_at("tensor")
                        d1 = p.op("vector", lambda e: e.tensor_copy(
                            out=den_row[:], in_=pb_den[:]), after=[dent])
                        d2 = p.op("vector", (lambda e: e.reciprocal(
                            out=den_row[:], in_=den_row[:])), after=[d1])
                        d3 = p.op("vector", (lambda e: e.tensor_copy(
                            out=rb_sb[0:1, 0:8], in_=den_row[0:1, 0:8])),
                            after=[d2])
                        pb_r = p.psum([128, 512], F32)
                        mr = p.op("tensor",
                                  (lambda e, pb_r=pb_r: e.matmul(
                                      pb_r[:], ones1f2[:],
                                      den_row[:], start=True,
                                      stop=True)),
                                  after=[d3])
                        rbt = p.op("vector", lambda e: e.tensor_copy(
                            out=rb_sb[:], in_=pb_r[:]), after=[mr])
                    with scope(p):
                        pb_av = p.psum([64, 512], F32)
                        obp2 = Pool(p, 3, [64, 512], BF16, tag=f"l2ao{h}_{B}_")
                        avt = None
                        for ktc in range(nkt):
                            avt = p.op(
                                "tensor",
                                (lambda e, ktc=ktc: e.matmul(
                                    pb_av[:],
                                    vres[:, ktc * 64:(ktc + 1) * 64],
                                    E[:, ktc * 512:(ktc + 1) * 512],
                                    start=(ktc == 0), stop=(ktc == nkt - 1))),
                                after=[rbt] if ktc == 0 else ())
                        ob, war = obp2.acquire()
                        ot = p.op("vector",
                                  (lambda e, ob=ob: e.tensor_mul(
                                      ob[:], pb_av[:], rb_sb[0:64, :])),
                                  after=[avt, war])
                        stok = p.dma("gpsimd",
                                     o_h[h * 64:(h + 1) * 64, qt0:qt0 + 512],
                                     ob[:], sem="st", after=[ot])
                        obp2.used(stok)
                    p.bar(("tensor", "vector", "scalar"))
    p.bar()

    with scope(p):
        o_get, o_tok, _ = load_fm_resident(p, o_h, D, T, "l2aor")
        btile = p.sb([128, 8], F32)
        btok = p.dma("sync", btile[:], bproj)
        xrp = Pool(p, 3, [128, 512], F32, tag="l2axr")
        ofp = Pool(p, 3, [128, 512], F32, tag="l2aof")

        def proj_epi(no, ts0, banks, mmtok):
            last = None
            for (bank, t0, tw) in banks:
                xr, w1 = xrp.acquire()
                tl = p.dma("sync", xr[:],
                           x_in[no * 128:(no + 1) * 128, t0:t0 + tw],
                           after=[w1])
                of, w2 = ofp.acquire()
                p.op("vector",
                     (lambda e, bank=bank, of=of, no=no, btile=btile:
                      e.tensor_scalar(out=of[:], in0=bank[:, :512],
                                      scalar1=btile[:, no:no + 1],
                                      scalar2=None, op0=ALU.add)),
                     after=[mmtok, btok, w2])
                last = p.op("vector",
                            (lambda e, of=of, xr=xr: e.tensor_add(
                                of[:], of[:], xr[:])),
                            after=[tl])
                xrp.used(last)
                st1 = p.dma("gpsimd", x_out[no * 128:(no + 1) * 128,
                                            t0:t0 + tw],
                            of[:], sem="st", after=[last])
                ofp.used(st1)
            return last

        gemm_fm(p, Wproj, lambda k, t0, tw: o_get(k, t0, tw), proj_epi,
                K=D, NO=D, Tt=T, name="l2ap")
    p.bar()


def l2_attn_host_inputs(inp):
    m = np.zeros((128, 4 * 512), np.float32)
    for mi, j in enumerate((0, 128, 256, 384)):
        kt = np.arange(128)[:, None] + j
        qt = np.arange(512)[None, :]
        m[:, mi * 512:(mi + 1) * 512] = np.where(kt <= qt, 0.0, -1e10)
    import ml_dtypes
    ident = np.eye(128, dtype=ml_dtypes.bfloat16)
    return {"l2a_wqkv": np.ascontiguousarray(inp["L2_attn_Wqkv"]),
            "l2a_bqkv": np.ascontiguousarray(
                inp["L2_attn_bqkv"].reshape(1, 3072)),
            "l2a_wproj": np.ascontiguousarray(inp["L2_attn_Wproj"]),
            "l2a_bproj": np.ascontiguousarray(
                inp["L2_attn_bproj"].reshape(8, 128).T),
            "l0a_mask": m, "ident_bf": ident}


def build_l2_attn_stage():
    p = Prog("l2_attn")
    io = {}
    io["x"] = p.dram_in("x", [D, T], F32)
    io["xo"] = p.dram_out("xo", [D, T], F32)
    io["l2a_wqkv"] = p.dram_in("l2a_wqkv", [1024, 3072], F32)
    io["l2a_bqkv"] = p.dram_in("l2a_bqkv", [1, 3072], F32)
    io["l2a_wproj"] = p.dram_in("l2a_wproj", [1024, 1024], F32)
    io["l2a_bproj"] = p.dram_in("l2a_bproj", [128, 8], F32)
    io["l0a_mask"] = p.dram_in("l0a_mask", [128, 4 * 512], F32)
    io["ident_bf"] = p.dram_in("ident_bf", [128, 128], BF16)
    emit_l2_attn(p, io)
    return p



def emit_l1_attn(p, io):
    x_in, x_out = io["x"], io["xo"]
    WqT, WkT, WvT, WoT = (io["l1a_wqT"], io["l1a_wkT"], io["l1a_wvT"],
                          io["l1a_woT"])
    bq, bk, bo = io["l1a_bq"], io["l1a_bk"], io["l1a_bo"]
    bv_row = io["l1a_bv"]
    xn_bf = p.dram_tmp("l1a_xn", [D, T], BF16)
    q_hb = p.dram_tmp("l1a_q", [D, T], BF16)
    k_hb = p.dram_tmp("l1a_k", [D, T], BF16)
    v_hb = p.dram_tmp("l1a_v", [T, D], BF16)
    o_hb = p.dram_tmp("l1a_o", [D, T], BF16)

    emit_ln_fm(p, x_in, xn_bf, None, name="l1aln")
    p.bar()

    with scope(p):
        xn_get, xn_tok, _ = load_fm_resident(p, xn_bf, D, T, "l1axn")
        bqt = p.sb([128, 8], F32)
        bkt = p.sb([128, 8], F32)
        t1 = p.dma("sync", bqt[:], bq)
        t2 = p.dma("sync", bkt[:], bk)
        obp = Pool(p, 3, [128, 512], BF16, tag="l1aqk")

        def mk_epi(dst, btile, btok, scale2):
            def epi(no, ts0, banks, mmtok):
                last = None
                for (bank, t0, tw) in banks:
                    ob, war = obp.acquire()
                    if scale2 is None:
                        last = p.op("vector",
                                    (lambda e, bank=bank, ob=ob, no=no,
                                     btile=btile: e.tensor_scalar(
                                        out=ob[:, :512], in0=bank[:, :512],
                                        scalar1=btile[:, no:no + 1],
                                        scalar2=None, op0=ALU.add)),
                                    after=[mmtok, btok, war])
                    else:
                        last = p.op("vector",
                                    (lambda e, bank=bank, ob=ob, no=no,
                                     btile=btile: e.tensor_scalar(
                                        out=ob[:, :512], in0=bank[:, :512],
                                        scalar1=btile[:, no:no + 1],
                                        scalar2=scale2, op0=ALU.add,
                                        op1=ALU.mult)),
                                    after=[mmtok, btok, war])
                    stok = p.dma("gpsimd",
                                 dst[no * 128:(no + 1) * 128, t0:t0 + tw],
                                 ob[:, :tw], sem="st", after=[last])
                    obp.used(stok)
                return last
            return epi

        gemm_fm(p, WqT, lambda k, t0, tw: xn_get(k, t0, tw),
                mk_epi(q_hb, bqt, t1, 1.0 / 32.0), K=D, NO=D, Tt=T,
                name="l1aq")
        gemm_fm(p, WkT, lambda k, t0, tw: xn_get(k, t0, tw),
                mk_epi(k_hb, bkt, t2, None), K=D, NO=D, Tt=T, name="l1ak")
        wv_st = p.sb([128, 1024], F32)
        wv_bf = p.sb([128, 8 * 1024], BF16)
        for k in range(8):
            tl = p.dma("sync", wv_st[:], WvT[k * 128:(k + 1) * 128, :],
                       after=[p.eng_at("vector")] if k else ())
            wvtok = p.op("vector",
                         (lambda e, k=k: e.tensor_copy(
                             out=wv_bf[:, k * 1024:(k + 1) * 1024],
                             in_=wv_st[:])),
                         after=[tl])
        ones1f = p.sb([1, 128], F32)
        bvrow = p.sb([1, 1024], F32)
        bv_bc = p.sb([128, 1024], F32)
        pb_bc = p.psum([128, 512], F32)
        p.op("vector", lambda e: e.memset(ones1f[:], 1.0))
        bvt = p.dma("sync", bvrow[:], bv_row)
        for h in range(2):
            mm = p.op("tensor",
                      (lambda e, h=h: e.matmul(
                          pb_bc[:], ones1f[:], bvrow[:, h * 512:(h + 1) * 512],
                          start=True, stop=True)),
                      after=[bvt, p.eng_at("vector")])
            p.op("vector",
                 (lambda e, h=h: e.tensor_copy(
                     out=bv_bc[:, h * 512:(h + 1) * 512], in_=pb_bc[:])),
                 after=[mm])
        bvbc_tok = p.eng_at("vector")

        def v_epi(tch, banks, mmtok):
            last = None
            for (bank, n0, nw) in banks:
                ob, war = obp.acquire()
                last = p.op("vector",
                            (lambda e, bank=bank, ob=ob, n0=n0, nw=nw:
                             e.tensor_tensor(out=ob[:, :nw], in0=bank[:, :nw],
                                             in1=bv_bc[:, n0:n0 + nw],
                                             op=ALU.add)),
                            after=[mmtok, bvbc_tok, war])
                stok = p.dma("gpsimd", v_hb[tch * 128:(tch + 1) * 128,
                                            n0:n0 + nw],
                             ob[:, :nw], sem="st", after=[last])
                obp.used(stok)
            return last

        gemm_tm(p, (lambda k: (wv_bf[:, k * 1024:(k + 1) * 1024], wvtok)),
                lambda k, t0, tw: xn_get(k, t0, tw),
                v_epi, K=D, NO=1024, Tt=T, name="l1av")
    p.bar()

    with scope(p):
        k_get, k_tok, k_all = load_fm_resident(p, k_hb, D, T, "l1akr")
        ident = p.sb([128, 128], BF16)
        itok = p.dma("sync", ident[:], io["ident_bf"])
        S0 = p.sb([128, T], F32, name="l1aS0")
        S1 = p.sb([128, T], F32, name="l1aS1")
        PT = p.sb([128, 32 * 512], BF16, name="l1aPT")
        qblk = p.sb([128, 8 * 512], BF16)
        m8 = p.sb([128, 8], F32)
        negmax = p.sb([128, 1], F32)
        rden = p.sb([128, 1], F32)
        Ew = p.sb([128, T], F32)
        msk = p.sb([128, T], F32)
        Pbf = p.sb([128, T], BF16)
        vp = Pool(p, 3, [128, 1024], BF16, tag="l1avp")
        for B in range(8):
            qt0 = B * 512
            qtoks = []
            for c in range(8):
                qtoks.append(p.dma("sync", qblk[:, c * 512:(c + 1) * 512],
                                   q_hb[c * 128:(c + 1) * 128, qt0:qt0 + 512],
                                   after=[p.eng_at("tensor")]))
            q_tok = qtoks[-1]
            for qt in range(4):
                with scope(p):
                    banks = [p.psum([128, 512], F32) for _ in range(8)]
                    pb_tr2 = [banks[0][:].bitcast(BF16),
                              banks[1][:].bitcast(BF16)]
                    last = None
                    for c in range(8):
                        for b in range(8):
                            aft = ()
                            if c == 0 and b == 0:
                                aft = [k_tok, q_tok, itok,
                                       p.eng_at("vector")]
                            last = p.op(
                                "tensor",
                                (lambda e, c=c, b=b, qt=qt: e.matmul(
                                    banks[b][:],
                                    qblk[:, c * 512 + qt * 128:
                                         c * 512 + (qt + 1) * 128],
                                    k_all[:, c * T + b * 512:
                                          c * T + (b + 1) * 512],
                                    start=(c == 0), stop=(c == 7))),
                                after=aft, inc=(c == 7))
                    mmt = p.eng_at("tensor")
                    evs = []
                    for b in range(8):
                        evs.append(p.op(
                            "vector",
                            (lambda e, b=b: e.tensor_copy(
                                out=S0[:, b * 512:(b + 1) * 512],
                                in_=banks[b][:])),
                            after=[mmt]))
                    ev = evs[-1]
                    mx0 = p.op("vector",
                               (lambda e: e.max(out=m8[:], in_=S0[:])),
                               after=[ev])
                    ng = p.op("vector",
                              (lambda e: e.tensor_scalar(
                                  out=negmax[:], in0=m8[:, 0:1],
                                  scalar1=-1.0, scalar2=None, op0=ALU.mult)),
                              after=[mx0])
                    ex = p.op("scalar",
                              (lambda e: e.activation(
                                  out=Ew[:], in_=S0[:], func=AF.Exp,
                                  bias=negmax[:, 0:1])),
                              after=[ng])
                    src, dst = S0, S1
                    mr = ev
                    for rnd in range(4):
                        mx = p.op("vector",
                                  (lambda e, src=src: e.max(out=m8[:],
                                                            in_=src[:])),
                                  after=[mr] + ([ex] if rnd == 0 else []))
                        mr = p.op("vector",
                                  (lambda e, src=src, dst=dst: e.match_replace(
                                      out=dst[:], in_to_replace=m8[:],
                                      in_values=src[:], imm_value=-1e30)),
                                  after=[mx] + ([ex] if rnd <= 1 else []))
                        src, dst = dst, src
                    mk = p.op("vector",
                              (lambda e, src=src: e.tensor_scalar(
                                  out=msk[:], in0=src[:], scalar1=-1e30,
                                  scalar2=None, op0=ALU.is_equal)),
                              after=[mr])
                    w1 = p.op("vector",
                              (lambda e: e.tensor_mul(Ew[:], Ew[:], msk[:])),
                              after=[mk, ex])
                    rs = p.op("vector",
                              (lambda e: e.tensor_reduce(
                                  out=rden[:], in_=Ew[:], axis=AX.X,
                                  op=ALU.add)))
                    rc = p.op("vector", (lambda e: e.reciprocal(
                        out=rden[:], in_=rden[:])), after=[rs])
                    pw = p.op("vector",
                              (lambda e: e.tensor_scalar(
                                  out=Pbf[:], in0=Ew[:],
                                  scalar1=rden[:, 0:1], scalar2=None,
                                  op0=ALU.mult)),
                              after=[rc])
                    cptoks = []
                    for ktc in range(32):
                        trb = pb_tr2[ktc % 2]
                        tt = p.op("tensor",
                                  (lambda e, ktc=ktc, trb=trb: e.transpose(
                                      trb[:, 0:128],
                                      Pbf[:, ktc * 128:(ktc + 1) * 128],
                                      ident[:])),
                                  after=[pw, ev]
                                  + ([cptoks[ktc - 2]] if ktc >= 2 else []))
                        cptoks.append(p.op(
                            "vector",
                            (lambda e, ktc=ktc, qt=qt, trb=trb: e.tensor_copy(
                                out=PT[:, ktc * 512 + qt * 128:
                                       ktc * 512 + (qt + 1) * 128],
                                in_=trb[:, 0:128])),
                            after=[tt]))
                p.bar(("tensor", "vector", "scalar"))
            with scope(p):
                pb_av = [p.psum([128, 512], F32) for _ in range(8)]
                obp2 = Pool(p, 3, [128, 512], BF16, tag=f"l1ao{B}_")
                ptok = p.eng_at("vector")
                vtoks = {}
                for ktc in range(32):
                    vt_, war = vp.acquire()
                    tl = p.dma("sync", vt_[:],
                               v_hb[ktc * 128:(ktc + 1) * 128, :],
                               after=[war])
                    avt = None
                    for dv in range(8):
                        avt = p.op(
                            "tensor",
                            (lambda e, vt_=vt_, dv=dv, ktc=ktc: e.matmul(
                                pb_av[dv][:],
                                vt_[:, dv * 128:(dv + 1) * 128],
                                PT[:, ktc * 512:(ktc + 1) * 512],
                                start=(ktc == 0), stop=(ktc == 31))),
                            after=[tl, ptok] if dv == 0 else ())
                    vp.used(avt)
                mmt = p.eng_at("tensor")
                for dv in range(8):
                    ob, war = obp2.acquire()
                    ot = p.op("vector",
                              (lambda e, dv=dv, ob=ob: e.tensor_copy(
                                  ob[:], pb_av[dv][:])),
                              after=[mmt, war])
                    stok = p.dma("gpsimd",
                                 o_hb[dv * 128:(dv + 1) * 128, qt0:qt0 + 512],
                                 ob[:], sem="st", after=[ot])
                    obp2.used(stok)
            p.bar(("tensor", "vector", "scalar"))
    p.bar()

    with scope(p):
        o_get, o_tok, _ = load_fm_resident(p, o_hb, D, T, "l1aor")
        btile = p.sb([128, 8], F32)
        btok = p.dma("sync", btile[:], bo)
        xrp = Pool(p, 3, [128, 512], F32, tag="l1axr")
        ofp = Pool(p, 3, [128, 512], F32, tag="l1aof")

        def proj_epi(no, ts0, banks, mmtok):
            last = None
            for (bank, t0, tw) in banks:
                xr, w1 = xrp.acquire()
                tl = p.dma("sync", xr[:],
                           x_in[no * 128:(no + 1) * 128, t0:t0 + tw],
                           after=[w1])
                of, w2 = ofp.acquire()
                p.op("vector",
                     (lambda e, bank=bank, of=of, no=no, btile=btile:
                      e.tensor_scalar(out=of[:], in0=bank[:, :512],
                                      scalar1=btile[:, no:no + 1],
                                      scalar2=None, op0=ALU.add)),
                     after=[mmtok, btok, w2])
                last = p.op("vector",
                            (lambda e, of=of, xr=xr: e.tensor_add(
                                of[:], of[:], xr[:])),
                            after=[tl])
                xrp.used(last)
                st1 = p.dma("gpsimd", x_out[no * 128:(no + 1) * 128,
                                            t0:t0 + tw],
                            of[:], sem="st", after=[last])
                ofp.used(st1)
            return last

        gemm_fm(p, WoT, lambda k, t0, tw: o_get(k, t0, tw), proj_epi,
                K=D, NO=D, Tt=T, name="l1ap")
    p.bar()


def l1_attn_host_inputs(inp):
    import ml_dtypes
    return {
        "l1a_wqT": np.ascontiguousarray(inp["L1_attn_Wq"].T),
        "l1a_wkT": np.ascontiguousarray(inp["L1_attn_Wk"].T),
        "l1a_wvT": np.ascontiguousarray(inp["L1_attn_Wv"].T),
        "l1a_woT": np.ascontiguousarray(inp["L1_attn_Wo"].T),
        "l1a_bq": np.ascontiguousarray(inp["L1_attn_bq"].reshape(8, 128).T),
        "l1a_bk": np.ascontiguousarray(inp["L1_attn_bk"].reshape(8, 128).T),
        "l1a_bo": np.ascontiguousarray(inp["L1_attn_bo"].reshape(8, 128).T),
        "l1a_bv": np.ascontiguousarray(inp["L1_attn_bv"].reshape(1, 1024)),
        "ident_bf": np.eye(128, dtype=ml_dtypes.bfloat16),
    }


def build_l1_attn_stage():
    p = Prog("l1_attn")
    io = {"x": p.dram_in("x", [D, T], F32),
          "xo": p.dram_out("xo", [D, T], F32)}
    for nm, shape in (("l1a_wqT", [D, D]), ("l1a_wkT", [D, D]),
                      ("l1a_wvT", [D, D]), ("l1a_woT", [D, D]),
                      ("l1a_bq", [128, 8]), ("l1a_bk", [128, 8]),
                      ("l1a_bo", [128, 8]), ("l1a_bv", [1, 1024])):
        io[nm] = p.dram_in(nm, shape, F32)
    io["ident_bf"] = p.dram_in("ident_bf", [128, 128], BF16)
    emit_l1_attn(p, io)
    return p





def load_fm_blocked(p, hbm_bf, Dk, TBLK, name):
    KC = Dk // 128
    buf = p.sb([128, KC * TBLK], BF16, name=name)
    state = {"ts0": None, "tok": None}

    def get(k, ts0, tblk):
        assert tblk <= TBLK
        if state["ts0"] != ts0:
            war = p.eng_at("tensor")
            toks = []
            for c in range(KC):
                toks.append(p.dma("sync", buf[:, c * TBLK:c * TBLK + tblk],
                                  hbm_bf[c * 128:(c + 1) * 128,
                                         ts0:ts0 + tblk],
                                  after=[war] if c == 0 else ()))
            state["ts0"] = ts0
            state["tok"] = toks[-1]
        return buf[:, k * TBLK:k * TBLK + tblk], state["tok"]
    return get



def emit_l2_ffn(p, io):
    x_in, x_out = io["x"], io["xo"]
    xn_bf = p.dram_tmp("l2f_xn", [D, T], BF16)
    acc_h = p.dram_tmp("l2f_acc", [D, T], F32)
    h_hb = p.dram_tmp("l2f_h", [2048, T], BF16)
    h1_hb = p.dram_tmp("l2f_h1", [2048, T], BF16)
    emit_ln_fm(p, x_in, xn_bf, None, name="l2fln")
    p.bar()

    with scope(p):
        xn_get = load_fm_blocked(p, xn_bf, D, 2048, "l2fxn")
        ones1f = p.sb([1, 128], F32)
        onesc_b = p.sb([1, 128], BF16)
        p.op("vector", lambda e: e.memset(ones1f[:], 1.0))
        p.op("vector", lambda e: e.memset(onesc_b[:], 1.0))
        otok = p.eng_at("vector")
        ident = p.sb([128, 128], BF16)
        itok = p.dma("sync", ident[:], io["ident_bf"])

        comb_T = p.sb([8, T], F32, name="l2fcombT")
        comb_hbm = p.dram_tmp("l2f_comb", [8, T], F32)
        with scope(p):
            wg_st = p.sb([128, 8], F32)
            wg_bf = p.sb([128, 8 * 8], BF16)
            for k in range(8):
                tl = p.dma("sync", wg_st[:],
                           io["l2f_wg"][k * 128:(k + 1) * 128, :],
                           after=[p.eng_at("vector")] if k else ())
                wgtok = p.op("vector",
                             (lambda e, k=k: e.tensor_copy(
                                 out=wg_bf[:, k * 8:(k + 1) * 8],
                                 in_=wg_st[:])),
                             after=[tl])
            bg_row = p.sb([1, 8], F32)
            bg_bc = p.sb([128, 8], F32)
            pb_b = p.psum([128, 8], F32)
            bgt = p.dma("sync", bg_row[:], io["l2f_bg"])
            mm = p.op("tensor", lambda e: e.matmul(pb_b[:], ones1f[:],
                                                   bg_row[:], start=True,
                                                   stop=True),
                      after=[bgt, otok])
            bgbc = p.op("vector", lambda e: e.tensor_copy(out=bg_bc[:],
                                                          in_=pb_b[:]),
                        after=[mm])
            pb_g = p.psum([128, 8], F32)
            pb_t = p.psum([8, 128], F32)
            gt = p.sb([128, 8], F32)
            gmax = p.sb([128, 1], F32)
            gsum = p.sb([128, 1], F32)
            m2c = p.sb([128, 8], F32)
            msk_g = p.sb([128, 8], F32)
            identf = p.sb([128, 128], F32)
            idf = p.dma("sync", identf[:], io["ident_f32"])
            for tch in range(32):
                ts0 = (tch // 16) * 2048
                xaps = [xn_get(k, ts0, 2048) for k in range(8)]
                last = None
                for k in range(8):
                    xap, xtok = xaps[k]
                    last = p.op("tensor",
                                (lambda e, k=k, tch=tch, xap=xap: e.matmul(
                                    pb_g[:],
                                    xap[:, (tch % 16) * 128:
                                        (tch % 16 + 1) * 128],
                                    wg_bf[:, k * 8:(k + 1) * 8],
                                    start=(k == 0), stop=(k == 7))),
                                after=[xtok, wgtok, p.eng_at("vector")]
                                if k == 0 else (), inc=(k == 7))
                g1 = p.op("vector", lambda e: e.tensor_add(gt[:], pb_g[:],
                                                           bg_bc[:]),
                          after=[last, bgbc])
                g2 = p.op("vector", (lambda e: e.tensor_reduce(
                    out=gmax[:], in_=gt[:], axis=AX.X, op=ALU.max,
                    negate=True)), after=[g1])
                g3 = p.op("scalar", (lambda e: e.activation(
                    out=gt[:], in_=gt[:], func=AF.Exp,
                    bias=gmax[:, 0:1])), after=[g2])
                g4 = p.op("vector", (lambda e: e.tensor_reduce(
                    out=gsum[:], in_=gt[:], axis=AX.X, op=ALU.add)),
                    after=[g3])
                g5 = p.op("vector", (lambda e: e.reciprocal(
                    out=gsum[:], in_=gsum[:])), after=[g4])
                g6 = p.op("vector", (lambda e: e.tensor_scalar(
                    out=gt[:], in0=gt[:], scalar1=gsum[:, 0:1],
                    scalar2=None, op0=ALU.mult)), after=[g5])
                g7 = p.op("vector", (lambda e: e.max(out=m2c[:], in_=gt[:])),
                          after=[g6])
                g8 = p.op("vector", (lambda e: e.tensor_scalar(
                    out=msk_g[:], in0=gt[:], scalar1=m2c[:, 1:2],
                    scalar2=None, op0=ALU.is_ge)), after=[g7])
                g9 = p.op("vector", (lambda e: e.tensor_mul(
                    gt[:], gt[:], msk_g[:])), after=[g8])
                g10 = p.op("tensor", (lambda e: e.transpose(
                    pb_t[:], gt[:], identf[:])),
                    after=[g9, idf] + ([prev_evt] if tch else []))
                prev_evt = p.op("vector", (lambda e, tch=tch: e.tensor_copy(
                    out=comb_T[:, tch * 128:(tch + 1) * 128], in_=pb_t[:])),
                    after=[g10])
        ctok = p.eng_at("vector")
        p.dma("gpsimd", comb_hbm, comb_T[:], sem="st", after=[ctok])
        p.bar()

        zp = Pool(p, 3, [128, 512], F32, tag="l2fz")
        for c in range(8):
            for tb in range(8):
                zt, war = zp.acquire()
                zo = p.op("vector", (lambda e, zt=zt: e.memset(zt[:], 0.0)),
                          after=[war])
                st1 = p.dma("gpsimd",
                            acc_h[c * 128:(c + 1) * 128,
                                  tb * 512:(tb + 1) * 512],
                            zt[:], sem="st", after=[zo])
                zp.used(st1)
        p.bar()

        b1t = p.sb([128, 16 * 8], F32)
        b3t = p.sb([128, 16 * 8], F32)
        b2t = p.sb([128, 8 * 8], F32)
        t1 = p.dma("sync", b1t[:], io["l2f_b1t"])
        t2 = p.dma("sync", b3t[:], io["l2f_b3t"])
        t3 = p.dma("sync", b2t[:], io["l2f_b2t"])
        btok3 = t3
        obp = Pool(p, 3, [128, 512], BF16, tag="l2fe")
        h1p = Pool(p, 3, [128, 512], BF16, tag="l2fh1")
        hfp = Pool(p, 3, [128, 512], F32, tag="l2fhf")
        acp = Pool(p, 3, [128, 512], F32, tag="l2fac")
        cbp = Pool(p, 3, [128, 512], F32, tag="l2fcb")
        crp = Pool(p, 3, [1, 512], F32, tag="l2fcr")
        pb_cb = p.psum([128, 512], F32)
        pbcb_reader = [None]
        for ex in range(8):
            def h1_epi(no, ts0, banks, mmtok, ex=ex):
                last = None
                for (bank, t0, tw) in banks:
                    ob, war = obp.acquire()
                    last = p.op("vector",
                                (lambda e, bank=bank, ob=ob, no=no, ex=ex:
                                 e.tensor_scalar(
                                     out=ob[:, :512], in0=bank[:, :512],
                                     scalar1=b1t[:, ex * 16 + no:
                                                 ex * 16 + no + 1],
                                     scalar2=None, op0=ALU.add)),
                                after=[mmtok, t1, war])
                    stok = p.dma("gpsimd",
                                 h1_hb[no * 128:(no + 1) * 128, t0:t0 + tw],
                                 ob[:, :tw], sem="st", after=[last])
                    obp.used(stok)
                return last
            gemm_fm(p, io[f"l2f_w1_{ex}"], lambda k, t0, tw: xn_get(k, t0, tw),
                    h1_epi, K=D, NO=2048, Tt=T, name=f"l2fh1_{ex}",
                    TBLK=2048)
            p.bar()

            def h3_epi(no, ts0, banks, mmtok, ex=ex):
                last = None
                for (bank, t0, tw) in banks:
                    h1td, war1 = h1p.acquire()
                    tl = p.dma("sync", h1td[:],
                               h1_hb[no * 128:(no + 1) * 128, t0:t0 + tw],
                               after=[war1])
                    hf, war2 = hfp.acquire()
                    v1 = p.op("vector",
                              (lambda e, bank=bank, hf=hf, no=no, ex=ex:
                               e.tensor_scalar(
                                   out=hf[:], in0=bank[:, :512],
                                   scalar1=b3t[:, ex * 16 + no:
                                               ex * 16 + no + 1],
                                   scalar2=None, op0=ALU.add)),
                              after=[mmtok, t2, war2])
                    v2 = p.op("vector",
                              (lambda e, hf=hf, h1td=h1td: e.tensor_mul(
                                  hf[:], hf[:], h1td[:])),
                              after=[v1, tl])
                    h1p.used(v2)
                    ob, war3 = obp.acquire()
                    last = p.op("scalar",
                                (lambda e, hf=hf, ob=ob: e.activation(
                                    out=ob[:, :512], in_=hf[:],
                                    func=AF.Silu)),
                                after=[v2, war3])
                    hfp.used(last)
                    stok = p.dma("gpsimd",
                                 h_hb[no * 128:(no + 1) * 128, t0:t0 + tw],
                                 ob[:, :512], sem="st", after=[last])
                    obp.used(stok)
                return last
            gemm_fm(p, io[f"l2f_w3_{ex}"], lambda k, t0, tw: xn_get(k, t0, tw),
                    h3_epi, K=D, NO=2048, Tt=T, name=f"l2fh3_{ex}",
                    TBLK=2048)
            p.bar()


            def eo_epi(no, ts0, banks, mmtok, ex=ex):
                last = None
                for (bank, t0, tw) in banks:
                    cr, warr = crp.acquire()
                    tlc = p.dma("sync", cr[:],
                                comb_hbm[ex:ex + 1, t0:t0 + 512],
                                after=[warr])
                    cb, warc = cbp.acquire()
                    mmc = p.op("tensor",
                               (lambda e, cr=cr: e.matmul(
                                   pb_cb[:], ones1f[:], cr[:],
                                   start=True, stop=True)),
                               after=[tlc, warc, mmtok,
                                      pbcb_reader[0]])
                    cbt = p.op("vector",
                               (lambda e, cb=cb: e.tensor_copy(
                                   out=cb[:], in_=pb_cb[:])),
                               after=[mmc])
                    pbcb_reader[0] = cbt
                    act, wara = acp.acquire()
                    tl = p.dma("sync", act[:],
                               acc_h[no * 128:(no + 1) * 128, t0:t0 + tw],
                               after=[wara])
                    v1 = p.op("vector",
                              (lambda e, bank=bank, no=no, ex=ex, cb=cb:
                               e.tensor_scalar(
                                   out=bank[:, :512], in0=bank[:, :512],
                                   scalar1=b2t[:, ex * 8 + no:
                                               ex * 8 + no + 1],
                                   scalar2=None, op0=ALU.add)),
                              after=[mmtok, btok3, cbt])
                    v2 = p.op("vector",
                              (lambda e, bank=bank, cb=cb: e.tensor_mul(
                                  bank[:, :512], bank[:, :512], cb[:])),
                              after=[v1])
                    cbp.used(v2)
                    crp.used(mmc)
                    last = p.op("vector",
                                (lambda e, act=act, bank=bank: e.tensor_add(
                                    act[:], act[:], bank[:, :512])),
                                after=[v2, tl])
                    stok = p.dma("gpsimd",
                                 acc_h[no * 128:(no + 1) * 128, t0:t0 + tw],
                                 act[:], sem="st", after=[last])
                    acp.used(stok)
                return last
            with scope(p):
                h_get = load_fm_blocked(p, h_hb, 2048, 2048, f"l2fhr{ex}")
                gemm_fm(p, io[f"l2f_w2_{ex}"],
                        lambda k, t0, tw: h_get(k, t0, tw),
                        eo_epi, K=2048, NO=D, Tt=T, name=f"l2feo_{ex}",
                        TBLK=2048)
            p.bar()
    p.bar()

    with scope(p):
        acc_bf = p.dram_tmp("l2f_accbf", [D, T], BF16)
        cvp = Pool(p, 3, [128, 512], F32, tag="l2fcv")
        cvo = Pool(p, 3, [128, 512], BF16, tag="l2fcvo")
        for c in range(8):
            for tb in range(8):
                t_, war = cvp.acquire()
                tl = p.dma("sync", t_[:],
                           acc_h[c * 128:(c + 1) * 128,
                                 tb * 512:(tb + 1) * 512], after=[war])
                o_, war2 = cvo.acquire()
                cv = p.op("vector", (lambda e, t_=t_, o_=o_: e.tensor_copy(
                    out=o_[:], in_=t_[:])), after=[tl, war2])
                cvp.used(cv)
                st1 = p.dma("gpsimd",
                            acc_bf[c * 128:(c + 1) * 128,
                                   tb * 512:(tb + 1) * 512],
                            o_[:], sem="st", after=[cv])
                cvo.used(st1)
        p.bar()
        a_get, a_tok, _ = load_fm_resident(p, acc_bf, D, T, "l2far")
        btile = p.sb([128, 8], F32)
        btok = p.dma("sync", btile[:], io["l2f_bft"])
        xrp = Pool(p, 3, [128, 512], F32, tag="l2fxr")
        ofp = Pool(p, 3, [128, 512], F32, tag="l2fof")

        def fin_epi(no, ts0, banks, mmtok):
            last = None
            for (bank, t0, tw) in banks:
                xr, w1 = xrp.acquire()
                tl = p.dma("sync", xr[:],
                           x_in[no * 128:(no + 1) * 128, t0:t0 + tw],
                           after=[w1])
                of, w2 = ofp.acquire()
                p.op("vector",
                     (lambda e, bank=bank, of=of, no=no, btile=btile:
                      e.tensor_scalar(out=of[:], in0=bank[:, :512],
                                      scalar1=btile[:, no:no + 1],
                                      scalar2=None, op0=ALU.add)),
                     after=[mmtok, btok, w2])
                last = p.op("vector",
                            (lambda e, of=of, xr=xr: e.tensor_add(
                                of[:], of[:], xr[:])),
                            after=[tl])
                xrp.used(last)
                st1 = p.dma("gpsimd", x_out[no * 128:(no + 1) * 128,
                                            t0:t0 + tw],
                            of[:], sem="st", after=[last])
                ofp.used(st1)
            return last

        gemm_fm(p, io["l2f_wf"], lambda k, t0, tw: a_get(k, t0, tw), fin_epi,
                K=D, NO=D, Tt=T, name="l2ff")
    p.bar()


def l2_ffn_host_inputs(inp):
    import ml_dtypes
    d = {
        "l2f_wg": np.ascontiguousarray(inp["L2_ffn_Wg"]),
        "l2f_bg": np.ascontiguousarray(inp["L2_ffn_bg"].reshape(1, 8)),
        "l2f_b1t": np.ascontiguousarray(
            inp["L2_ffn_b1"].reshape(8, 16, 128).transpose(2, 0, 1)
            .reshape(128, 128)),
        "l2f_b3t": np.ascontiguousarray(
            inp["L2_ffn_b3"].reshape(8, 16, 128).transpose(2, 0, 1)
            .reshape(128, 128)),
        "l2f_b2t": np.ascontiguousarray(
            inp["L2_ffn_b2"].reshape(8, 8, 128).transpose(2, 0, 1)
            .reshape(128, 64)),
        "l2f_bft": np.ascontiguousarray(inp["L2_ffn_bf"].reshape(8, 128).T),
        "l2f_wf": np.ascontiguousarray(inp["L2_ffn_Wf"]),
        "ident_bf": np.eye(128, dtype=ml_dtypes.bfloat16),
        "ident_f32": np.eye(128, dtype=np.float32),
    }
    for ex in range(8):
        d[f"l2f_w1_{ex}"] = np.ascontiguousarray(inp["L2_ffn_W1"][ex])
        d[f"l2f_w3_{ex}"] = np.ascontiguousarray(inp["L2_ffn_W3"][ex])
        d[f"l2f_w2_{ex}"] = np.ascontiguousarray(inp["L2_ffn_W2"][ex])
    return d


def build_l2_ffn_stage():
    p = Prog("l2_ffn")
    io = {"x": p.dram_in("x", [D, T], F32),
          "xo": p.dram_out("xo", [D, T], F32)}
    io["l2f_wg"] = p.dram_in("l2f_wg", [1024, 8], F32)
    io["l2f_bg"] = p.dram_in("l2f_bg", [1, 8], F32)
    io["l2f_b1t"] = p.dram_in("l2f_b1t", [128, 128], F32)
    io["l2f_b3t"] = p.dram_in("l2f_b3t", [128, 128], F32)
    io["l2f_b2t"] = p.dram_in("l2f_b2t", [128, 64], F32)
    io["l2f_bft"] = p.dram_in("l2f_bft", [128, 8], F32)
    io["l2f_wf"] = p.dram_in("l2f_wf", [1024, 1024], F32)
    io["ident_bf"] = p.dram_in("ident_bf", [128, 128], BF16)
    io["ident_f32"] = p.dram_in("ident_f32", [128, 128], F32)
    for ex in range(8):
        io[f"l2f_w1_{ex}"] = p.dram_in(f"l2f_w1_{ex}", [1024, 2048], F32)
        io[f"l2f_w3_{ex}"] = p.dram_in(f"l2f_w3_{ex}", [1024, 2048], F32)
        io[f"l2f_w2_{ex}"] = p.dram_in(f"l2f_w2_{ex}", [2048, 1024], F32)
    emit_l2_ffn(p, io)
    return p



_QDQ_MID = (0.25, 0.75, 1.25, 1.75, 2.5, 3.5, 5.0)
_QDQ_DELTA = (0.5, 0.5, 0.5, 0.5, 1.0, 1.0, 2.0)


def emit_qdq(p, src, dst, W, qs, after=()):
    G = W // 16
    bm, rec, v, sg, acc, tmp = (qs["bm"], qs["rec"], qs["v"], qs["sg"],
                                qs["acc"], qs["tmp"])
    t0 = p.op("vector", (lambda e: e.tensor_reduce(
        out=bm[:, :G], in_=src.rearrange("p (g b) -> p g b", b=16),
        axis=AX.X, op=ALU.max, apply_absolute_value=True)), after=list(after))
    t1 = p.op("vector", (lambda e: e.tensor_scalar(
        out=bm[:, :G], in0=bm[:, :G], scalar1=1.0 / 6.0, scalar2=1e-12,
        op0=ALU.mult, op1=ALU.max)), after=[t0])
    t2 = p.op("vector", (lambda e: e.reciprocal(out=rec[:, :G],
                                                in_=bm[:, :G])), after=[t1])
    rec_x = rec[:, :G].unsqueeze(2).broadcast_to([128, G, 16])
    bm_x = bm[:, :G].unsqueeze(2).broadcast_to([128, G, 16])
    t3 = p.op("vector", (lambda e: e.tensor_tensor(
        out=v[:, :W].rearrange("p (g b) -> p g b", b=16),
        in0=src.rearrange("p (g b) -> p g b", b=16), in1=rec_x,
        op=ALU.mult)), after=[t2])
    s1 = p.op("scalar", (lambda e: e.activation(out=sg[:, :W], in_=v[:, :W],
                                                func=AF.Sign)), after=[t3])
    s2 = p.op("scalar", (lambda e: e.activation(out=v[:, :W], in_=v[:, :W],
                                                func=AF.Abs)), after=[s1])
    lt = p.op("vector", (lambda e: e.tensor_scalar(
        out=acc[:, :W], in0=v[:, :W], scalar1=float(_QDQ_MID[0]),
        scalar2=float(_QDQ_DELTA[0]), op0=ALU.is_gt, op1=ALU.mult)),
        after=[s2])
    for kk in range(1, 7):
        p.op("vector", (lambda e, kk=kk: e.tensor_scalar(
            out=tmp[:, :W], in0=v[:, :W], scalar1=float(_QDQ_MID[kk]),
            scalar2=float(_QDQ_DELTA[kk]), op0=ALU.is_gt, op1=ALU.mult)))
        lt = p.op("vector", (lambda e: e.tensor_add(acc[:, :W], acc[:, :W],
                                                    tmp[:, :W])))
    m1 = p.op("vector", (lambda e: e.tensor_mul(acc[:, :W], acc[:, :W],
                                                sg[:, :W])), after=[lt])
    m2 = p.op("vector", (lambda e, dst=dst: e.tensor_tensor(
        out=dst.rearrange("p (g b) -> p g b", b=16),
        in0=acc[:, :W].rearrange("p (g b) -> p g b", b=16), in1=bm_x,
        op=ALU.mult)), after=[m1])
    return m2


def mk_qdq_scratch(p, W):
    return {"bm": p.sb([128, W // 16], F32), "rec": p.sb([128, W // 16], F32),
            "v": p.sb([128, W], F32), "sg": p.sb([128, W], F32),
            "acc": p.sb([128, W], F32), "tmp": p.sb([128, W], F32)}



def emit_transpose_hbm(p, src, dst, R, C, dtype, name, ident, itok):
    with scope(p):
        ldp = Pool(p, 3, [128, 128], dtype, tag=name + "l")
        evp = Pool(p, 3, [128, 128], dtype, tag=name + "e")
        pb = p.psum([128, 128], dtype)
        prev_ev = None
        for r in range(R // 128):
            for c in range(C // 128):
                t_, war = ldp.acquire()
                tl = p.dma("sync", t_[:],
                           src[r * 128:(r + 1) * 128, c * 128:(c + 1) * 128],
                           after=[war])
                tt = p.op("tensor",
                          (lambda e, t_=t_: e.transpose(pb[:], t_[:],
                                                        ident[:])),
                          after=([tl, itok] + ([prev_ev] if prev_ev else [])))
                ldp.used(tt)
                o_, war2 = evp.acquire()
                prev_ev = p.op("vector",
                               (lambda e, o_=o_: e.tensor_copy(out=o_[:],
                                                               in_=pb[:])),
                               after=[tt, war2])
                st1 = p.dma("gpsimd",
                            dst[c * 128:(c + 1) * 128, r * 128:(r + 1) * 128],
                            o_[:], sem="st", after=[prev_ev])
                evp.used(st1)



def emit_l1_ffn(p, io):
    x_in, x_out = io["x"], io["xo"]
    I = 2816
    xn_f = p.dram_tmp("l1f_xnf", [D, T], F32)
    xn_dummy = p.dram_tmp("l1f_xnb", [D, T], BF16)
    xq_fm = p.dram_tmp("l1f_xq", [D, T], BF16)
    wgq_T = p.dram_tmp("l1f_wgq", [D, I], BF16)
    wuq_T = p.dram_tmp("l1f_wuq", [D, I], BF16)
    wdq_T = p.dram_tmp("l1f_wdq", [I, D], BF16)
    gate_tm = p.dram_tmp("l1f_gate", [T, I], BF16)
    hq_fm = p.dram_tmp("l1f_hq", [I, T], BF16)
    emit_ln_fm(p, x_in, xn_dummy, xn_f, name="l1fln")
    p.bar()

    identb = None
    with scope(p):
        identb = p.sb([128, 128], BF16, name="l1fident")
        identf = p.sb([128, 128], F32, name="l1fidentf")
        itok = p.dma("sync", identb[:], io["ident_bf"])
        iftok = p.dma("sync", identf[:], io["ident_f32"])

        with scope(p):
            qs = mk_qdq_scratch(p, 1024)
            xtm = p.sb([128, 1024], F32)
            xqb = p.sb([128, 1024], BF16)
            pb_f = p.psum([128, 128], F32)
            pb_b = p.psum([128, 128], BF16)
            ldp = Pool(p, 3, [128, 128], F32, tag="l1fx")
            evb = Pool(p, 3, [128, 128], BF16, tag="l1fxq")
            prev1 = None
            prev2 = None
            for tt in range(32):
                for dc in range(8):
                    t_, war = ldp.acquire()
                    tl = p.dma("sync", t_[:],
                               xn_f[dc * 128:(dc + 1) * 128,
                                    tt * 128:(tt + 1) * 128], after=[war])
                    tr = p.op("tensor",
                              (lambda e, t_=t_: e.transpose(
                                  pb_f[:], t_[:], identf[:])),
                              after=[tl, iftok] + ([prev1] if prev1 else []))
                    ldp.used(tr)
                    prev1 = p.op("vector",
                                 (lambda e, dc=dc: e.tensor_copy(
                                     out=xtm[:, dc * 128:(dc + 1) * 128],
                                     in_=pb_f[:])),
                                 after=[tr])
                qt = emit_qdq(p, xtm[:], xtm[:], 1024, qs, after=[prev1])
                cv = p.op("vector", (lambda e: e.tensor_copy(out=xqb[:],
                                                             in_=xtm[:])),
                          after=[qt])
                for dc in range(8):
                    tr2 = p.op("tensor",
                               (lambda e, dc=dc: e.transpose(
                                   pb_b[:], xqb[:, dc * 128:(dc + 1) * 128],
                                   identb[:])),
                               after=[cv, itok] + ([prev2] if prev2 else []))
                    o_, war2 = evb.acquire()
                    prev2 = p.op("vector",
                                 (lambda e, o_=o_: e.tensor_copy(
                                     out=o_[:], in_=pb_b[:])),
                                 after=[tr2, war2])
                    st1 = p.dma("gpsimd",
                                xq_fm[dc * 128:(dc + 1) * 128,
                                      tt * 128:(tt + 1) * 128],
                                o_[:], sem="st", after=[prev2])
                    evb.used(st1)
        p.bar()

        with scope(p):
            qs = mk_qdq_scratch(p, 2816)
            wt = p.sb([128, 2816], F32)
            wq = p.sb([128, 2816], BF16)
            pb_b2 = p.psum([128, 128], BF16)
            evb2 = Pool(p, 3, [128, 128], BF16, tag="l1fwq")
            for (w_src, w_dst, RR, CC) in ((io["l1f_wg"], wgq_T, 2816, 1024),
                                           (io["l1f_wu"], wuq_T, 2816, 1024),
                                           (io["l1f_wd"], wdq_T, 1024, 2816)):
                prev3 = None
                for r in range(RR // 128):
                    tl = p.dma("sync", wt[:, :CC],
                               w_src[r * 128:(r + 1) * 128, :],
                               after=[p.eng_at("tensor"),
                                      p.eng_at("vector")])
                    qt = emit_qdq(p, wt[:, :CC], wt[:, :CC], CC, qs,
                                  after=[tl])
                    cv = p.op("vector", (lambda e, CC=CC: e.tensor_copy(
                        out=wq[:, :CC], in_=wt[:, :CC])), after=[qt])
                    for c in range(CC // 128):
                        tr = p.op("tensor",
                                  (lambda e, c=c: e.transpose(
                                      pb_b2[:], wq[:, c * 128:(c + 1) * 128],
                                      identb[:])),
                                  after=[cv, itok]
                                  + ([prev3] if prev3 else []))
                        o_, war2 = evb2.acquire()
                        prev3 = p.op("vector",
                                     (lambda e, o_=o_: e.tensor_copy(
                                         out=o_[:], in_=pb_b2[:])),
                                     after=[tr, war2])
                        st1 = p.dma("gpsimd",
                                    w_dst[c * 128:(c + 1) * 128,
                                          r * 128:(r + 1) * 128],
                                    o_[:], sem="st", after=[prev3])
                        evb2.used(st1)
        p.bar()

        xq_get, xq_tok, _ = load_fm_resident(p, xq_fm, D, T, "l1fxqr")
        with scope(p):
            wgq = p.sb([128, 8 * I], BF16, name="l1fwgr")
            wtoks = []
            for k in range(8):
                wtoks.append(p.dma("sync", wgq[:, k * I:(k + 1) * I],
                                   wgq_T[k * 128:(k + 1) * 128, :]))
            wtok = wtoks[-1]
            obp = Pool(p, 3, [128, 512], BF16, tag="l1fg")

            def gate_epi(tch, banks, mmtok):
                last = None
                for (bank, n0, nw) in banks:
                    ob, war = obp.acquire()
                    last = p.op("vector",
                                (lambda e, bank=bank, ob=ob, nw=nw:
                                 e.tensor_copy(out=ob[:, :nw],
                                               in_=bank[:, :nw])),
                                after=[mmtok, war])
                    st1 = p.dma("gpsimd",
                                gate_tm[tch * 128:(tch + 1) * 128,
                                        n0:n0 + nw],
                                ob[:, :nw], sem="st", after=[last])
                    obp.used(st1)
                return last

            gemm_tm(p, (lambda k: (wgq[:, k * I:(k + 1) * I], wtok)),
                    lambda k, t0, tw: xq_get(k, t0, tw),
                    gate_epi, K=D, NO=I, Tt=T, name="l1fgg")
        p.bar()

        with scope(p):
            wuq = p.sb([128, 8 * I], BF16, name="l1fwur")
            wtoks = []
            for k in range(8):
                wtoks.append(p.dma("sync", wuq[:, k * I:(k + 1) * I],
                                   wuq_T[k * 128:(k + 1) * 128, :]))
            wtok = wtoks[-1]
            qs = mk_qdq_scratch(p, 512)
            gp_ = Pool(p, 3, [128, 512], BF16, tag="l1fgt")
            sgp = Pool(p, 3, [128, 512], F32, tag="l1fsg")
            hfp = Pool(p, 3, [128, 512], F32, tag="l1fhf")
            hqp = Pool(p, 3, [128, 512], BF16, tag="l1fhq")
            evb3 = Pool(p, 3, [128, 128], BF16, tag="l1fhqT")
            pb_b3 = p.psum([128, 128], BF16)
            prev4 = [None]

            def up_epi(tch, banks, mmtok):
                last = None
                for (bank, n0, nw) in banks:
                    gt_, war1 = gp_.acquire()
                    tl = p.dma("sync", gt_[:, :nw],
                               gate_tm[tch * 128:(tch + 1) * 128,
                                       n0:n0 + nw], after=[war1])
                    sg_, war2 = sgp.acquire()
                    s1 = p.op("scalar",
                              (lambda e, gt_=gt_, sg_=sg_, nw=nw:
                               e.activation(out=sg_[:, :nw], in_=gt_[:, :nw],
                                            func=AF.Silu)),
                              after=[tl, war2])
                    gp_.used(s1)
                    hf, war3 = hfp.acquire()
                    v1 = p.op("vector",
                              (lambda e, bank=bank, hf=hf, sg_=sg_, nw=nw:
                               e.tensor_mul(hf[:, :nw], bank[:, :nw],
                                            sg_[:, :nw])),
                              after=[mmtok, s1, war3])
                    sgp.used(v1)
                    qt = emit_qdq(p, hf[:, :nw], hf[:, :nw], nw, qs,
                                  after=[v1])
                    hq, war4 = hqp.acquire()
                    cv = p.op("vector",
                              (lambda e, hf=hf, hq=hq, nw=nw: e.tensor_copy(
                                  out=hq[:, :nw], in_=hf[:, :nw])),
                              after=[qt, war4])
                    hfp.used(cv)
                    last = cv
                    for c in range(nw // 128):
                        tr = p.op("tensor",
                                  (lambda e, hq=hq, c=c: e.transpose(
                                      pb_b3[:], hq[:, c * 128:(c + 1) * 128],
                                      identb[:])),
                                  after=[cv]
                                  + ([prev4[0]] if prev4[0] else []))
                        o_, war5 = evb3.acquire()
                        prev4[0] = p.op("vector",
                                        (lambda e, o_=o_: e.tensor_copy(
                                            out=o_[:], in_=pb_b3[:])),
                                        after=[tr, war5])
                        st1 = p.dma("gpsimd",
                                    hq_fm[n0 + c * 128:n0 + (c + 1) * 128,
                                          tch * 128:(tch + 1) * 128],
                                    o_[:], sem="st", after=[prev4[0]])
                        evb3.used(st1)
                    hqp.used(prev4[0])
                return last

            gemm_tm(p, (lambda k: (wuq[:, k * I:(k + 1) * I], wtok)),
                    lambda k, t0, tw: xq_get(k, t0, tw),
                    up_epi, K=D, NO=I, Tt=T, name="l1fug")
        p.bar()

    with scope(p):
        h_get = load_fm_blocked(p, hq_fm, I, 2048, "l1fhr")
        xrp = Pool(p, 3, [128, 512], F32, tag="l1fxr")
        ofp = Pool(p, 3, [128, 512], F32, tag="l1fof")

        def fin_epi(no, ts0, banks, mmtok):
            last = None
            for (bank, t0, tw) in banks:
                xr, w1 = xrp.acquire()
                tl = p.dma("sync", xr[:],
                           x_in[no * 128:(no + 1) * 128, t0:t0 + tw],
                           after=[w1])
                of, w2 = ofp.acquire()
                last = p.op("vector",
                            (lambda e, of=of, xr=xr, bank=bank: e.tensor_add(
                                of[:], bank[:, :512], xr[:])),
                            after=[tl, mmtok, w2])
                xrp.used(last)
                st1 = p.dma("gpsimd", x_out[no * 128:(no + 1) * 128,
                                            t0:t0 + tw],
                            of[:], sem="st", after=[last])
                ofp.used(st1)
            return last

        gemm_fm(p, wdq_T, lambda k, t0, tw: h_get(k, t0, tw), fin_epi,
                K=I, NO=D, Tt=T, name="l1fd", TBLK=2048, w_bf16=True)
    p.bar()


def l1_ffn_host_inputs(inp):
    import ml_dtypes
    return {"l1f_wg": np.ascontiguousarray(inp["L1_ffn_w_gate"]),
            "l1f_wu": np.ascontiguousarray(inp["L1_ffn_w_up"]),
            "l1f_wd": np.ascontiguousarray(inp["L1_ffn_w_down"]),
            "ident_bf": np.eye(128, dtype=ml_dtypes.bfloat16),
            "ident_f32": np.eye(128, dtype=np.float32)}


def build_l1_ffn_stage():
    p = Prog("l1_ffn")
    io = {"x": p.dram_in("x", [D, T], F32),
          "xo": p.dram_out("xo", [D, T], F32),
          "l1f_wg": p.dram_in("l1f_wg", [2816, 1024], F32),
          "l1f_wu": p.dram_in("l1f_wu", [2816, 1024], F32),
          "l1f_wd": p.dram_in("l1f_wd", [1024, 2816], F32),
          "ident_bf": p.dram_in("ident_bf", [128, 128], BF16),
          "ident_f32": p.dram_in("ident_f32", [128, 128], F32)}
    emit_l1_ffn(p, io)
    return p



def emit_rms_fm(p, x_hbm, nx_bf_hbm, g_col_hbm, eps, name, nx_f32_hbm=None):
    DC = D // 128
    p.bar()
    NBT = T // 512
    with scope(p):
        xp = Pool(p, 3, [128, 512], F32, tag=name + "x")
        x2p = Pool(p, 3, [128, 512], F32, tag=name + "q")
        onec = p.sb([128, 1], F32)
        ones1f = p.sb([1, 128], F32)
        gcol = p.sb([128, DC], F32)
        arow = p.sb([1, T], F32)
        pb = p.psum([1, 512], F32)
        pb_a = p.psum([128, 512], F32)
        obp = Pool(p, 3, [128, 512], BF16, tag=name + "o")
        ofp = Pool(p, 3, [128, 512], F32, tag=name + "of")
        p.op("vector", lambda e: e.memset(onec[:], 1.0))
        p.op("vector", lambda e: e.memset(ones1f[:], 1.0))
        onetok = p.eng_at("vector")
        gt = p.dma("sync", gcol[:], g_col_hbm)
        evtok = None
        for tb in range(NBT):
            for c in range(DC):
                xt, w1 = xp.acquire()
                tl = p.dma("sync", xt[:],
                           x_hbm[c * 128:(c + 1) * 128,
                                 tb * 512:(tb + 1) * 512], after=[w1])
                x2, w2 = x2p.acquire()
                sq = p.op("vector",
                          (lambda e, xt=xt, x2=x2: e.tensor_mul(
                              x2[:], xt[:], xt[:])), after=[tl, w2])
                xp.used(sq)
                mm = p.op("tensor",
                          (lambda e, x2=x2, st=(c == 0), sp=(c == DC - 1):
                           e.matmul(pb[:], onec[:], x2[:], start=st,
                                    stop=sp)),
                          after=[sq, onetok] + ([evtok] if c == 0 and evtok
                                                else []))
                x2p.used(mm)
            evtok = p.op("vector",
                         (lambda e, tb=tb: e.tensor_copy(
                             out=arow[0:1, tb * 512:(tb + 1) * 512],
                             in_=pb[:])),
                         after=[p.eng_at("tensor")])
        v1 = p.op("vector", (lambda e: e.tensor_scalar(
            out=arow[:], in0=arow[:], scalar1=1.0 / D, scalar2=float(eps),
            op0=ALU.mult, op1=ALU.add)), after=[evtok])
        s1 = p.op("scalar", (lambda e: e.activation(out=arow[:], in_=arow[:],
                                                    func=AF.Sqrt)),
                  after=[v1])
        v2 = p.op("vector", (lambda e: e.reciprocal(out=arow[:],
                                                    in_=arow[:])),
                  after=[s1])
        for tb in range(NBT):
            tsl = slice(tb * 512, (tb + 1) * 512)
            mb = p.op("tensor",
                      (lambda e, tsl=tsl: e.matmul(
                          pb_a[:], ones1f[:], arow[0:1, tsl], start=True,
                          stop=True)),
                      after=[v2, p.eng_at("vector")])
            for c in range(DC):
                xt, w1 = xp.acquire()
                tl = p.dma("sync", xt[:], x_hbm[c * 128:(c + 1) * 128, tsl],
                           after=[w1])
                m1 = p.op("vector",
                          (lambda e, xt=xt: e.tensor_mul(xt[:], xt[:],
                                                         pb_a[:])),
                          after=[tl, mb, gt])
                ob, w3 = obp.acquire()
                m2 = p.op("vector",
                          (lambda e, xt=xt, ob=ob, c=c: e.tensor_scalar(
                              out=ob[:], in0=xt[:],
                              scalar1=gcol[:, c:c + 1], scalar2=None,
                              op0=ALU.mult)),
                          after=[m1, w3])
                st1 = p.dma("gpsimd", nx_bf_hbm[c * 128:(c + 1) * 128, tsl],
                            ob[:], sem="st", after=[m2])
                obp.used(st1)
                if nx_f32_hbm is not None:
                    of, w4 = ofp.acquire()
                    m3 = p.op("vector",
                              (lambda e, xt=xt, of=of, c=c: e.tensor_scalar(
                                  out=of[:], in0=xt[:],
                                  scalar1=gcol[:, c:c + 1], scalar2=None,
                                  op0=ALU.mult)),
                              after=[m1, w4])
                    st2 = p.dma("gpsimd",
                                nx_f32_hbm[c * 128:(c + 1) * 128, tsl],
                                of[:], sem="st", after=[m3])
                    ofp.used(st2)
                    xp.used(m3)
                else:
                    xp.used(m2)
    p.bar()



def emit_l3_ffn(p, io):
    x_in, x_out = io["x"], io["xo"]
    xn_f = p.dram_tmp("l3f_xnf", [D, T], F32)
    xn_b = p.dram_tmp("l3f_xnb", [D, T], BF16)
    nx_bf = p.dram_tmp("l3f_nx", [D, T], BF16)
    q_fm = p.dram_tmp("l3f_q", [D, T], BF16)
    kc_fm = p.dram_tmp("l3f_kc", [256, T], BF16)
    vc_tm = p.dram_tmp("l3f_vc", [T, 256], BF16)
    o_fm = p.dram_tmp("l3f_o", [D, T], BF16)
    xmid = p.dram_tmp("l3f_xmid", [D, T], F32)
    nx2_bf = p.dram_tmp("l3f_nx2", [D, T], BF16)
    s1_hb = p.dram_tmp("l3f_s1", [D, T], BF16)
    hs_hb = p.dram_tmp("l3f_hs", [D, T], BF16)
    acc_h = p.dram_tmp("l3f_acc", [D, T], F32)
    comb_hbm = p.dram_tmp("l3f_comb", [8, T], F32)

    emit_ln_fm(p, x_in, xn_b, xn_f, name="l3fln")
    emit_rms_fm(p, xn_f, nx_bf, io["l3f_g1"], 1e-6, "l3frms1")

    with scope(p):
        nx_get, nx_tok, _ = load_fm_resident(p, nx_bf, D, T, "l3fnx")
        obp = Pool(p, 3, [128, 512], BF16, tag="l3fqkv")

        def q_epi(no, ts0, banks, mmtok):
            last = None
            for (bank, t0, tw) in banks:
                ob, war = obp.acquire()
                last = p.op("vector",
                            (lambda e, bank=bank, ob=ob: e.tensor_scalar(
                                out=ob[:, :512], in0=bank[:, :512],
                                scalar1=1.0 / 16.0, scalar2=None,
                                op0=ALU.mult)),
                            after=[mmtok, war])
                stok = p.dma("gpsimd", q_fm[no * 128:(no + 1) * 128,
                                            t0:t0 + tw],
                             ob[:, :tw], sem="st", after=[last])
                obp.used(stok)
            return last

        gemm_fm(p, io["l3f_wqT"], lambda k, t0, tw: nx_get(k, t0, tw), q_epi,
                K=D, NO=D, Tt=T, name="l3fq")

        def kc_epi(no, ts0, banks, mmtok):
            last = None
            for (bank, t0, tw) in banks:
                ob, war = obp.acquire()
                last = p.op("vector",
                            (lambda e, bank=bank, ob=ob: e.tensor_copy(
                                out=ob[:, :512], in_=bank[:, :512])),
                            after=[mmtok, war])
                stok = p.dma("gpsimd", kc_fm[no * 128:(no + 1) * 128,
                                             t0:t0 + tw],
                             ob[:, :tw], sem="st", after=[last])
                obp.used(stok)
            return last

        gemm_fm(p, io["l3f_wkcT"], lambda k, t0, tw: nx_get(k, t0, tw),
                kc_epi, K=D, NO=256, Tt=T, name="l3fkc")

        def vc_epi(tch, banks, mmtok):
            last = None
            for (bank, n0, nw) in banks:
                ob, war = obp.acquire()
                last = p.op("vector",
                            (lambda e, bank=bank, ob=ob, nw=nw: e.tensor_copy(
                                out=ob[:, :nw], in_=bank[:, :nw])),
                            after=[mmtok, war])
                stok = p.dma("gpsimd", vc_tm[tch * 128:(tch + 1) * 128,
                                             n0:n0 + nw],
                             ob[:, :nw], sem="st", after=[last])
                obp.used(stok)
            return last

        wvc_st = p.sb([128, 256], F32)
        wvc_bf = p.sb([128, 8 * 256], BF16)
        for k in range(8):
            tl = p.dma("sync", wvc_st[:],
                       io["l3f_wvcT"][k * 128:(k + 1) * 128, :],
                       after=[p.eng_at("vector")] if k else ())
            wvctok = p.op("vector",
                          (lambda e, k=k: e.tensor_copy(
                              out=wvc_bf[:, k * 256:(k + 1) * 256],
                              in_=wvc_st[:])),
                          after=[tl])
        gemm_tm(p, (lambda k: (wvc_bf[:, k * 256:(k + 1) * 256], wvctok)),
                lambda k, t0, tw: nx_get(k, t0, tw),
                vc_epi, K=D, NO=256, Tt=T, name="l3fvc")
    p.bar()

    with scope(p):
        kc_all = p.sb([128, 2 * T], BF16, name="l3fkcr")
        kct = []
        for c in range(2):
            kct.append(p.dma("sync", kc_all[:, c * T:(c + 1) * T],
                             kc_fm[c * 128:(c + 1) * 128, :]))
        kc_tok = kct[-1]
        vc_all = p.sb([128, 32 * 256], BF16, name="l3fvcr")
        vct = []
        for c in range(32):
            vct.append(p.dma("sync", vc_all[:, c * 256:(c + 1) * 256],
                             vc_tm[c * 128:(c + 1) * 128, :]))
        vc_tok = vct[-1]
        oatt1f = p.sb([1, 128], F32)
        onecb = p.sb([128, 1], BF16)
        onecf = p.sb([128, 1], F32)
        p.op("vector", lambda e: e.memset(oatt1f[:], 1.0))
        p.op("vector", lambda e: e.memset(onecb[:], 1.0))
        p.op("vector", lambda e: e.memset(onecf[:], 1.0))
        otok = p.eng_at("vector")
        E = p.sb([128, 32 * 512], BF16, name="l3fE")
        qblk = p.sb([128, 2 * 512], BF16)
        sqs = p.sb([128, 512], F32)
        nrow = p.sb([1, T], F32)
        numax = p.sb([1, 2], F32)
        nu_col = p.sb([128, 1], F32)
        rb_sb = p.sb([128, 512], F32)
        den_row = p.sb([1, 512], F32)
        pb_n = p.psum([1, 512], F32)
        evt = None
        for tb in range(8):
            for c in range(2):
                sq = p.op("vector",
                          (lambda e, c=c, tb=tb: e.tensor_mul(
                              sqs[:], kc_all[:, c * T + tb * 512:
                                             c * T + (tb + 1) * 512],
                              kc_all[:, c * T + tb * 512:
                                     c * T + (tb + 1) * 512])),
                          after=[kc_tok] + ([evt] if evt else []))
                p.op("tensor",
                     (lambda e, c=c: e.matmul(pb_n[:], onecf[:], sqs[:],
                                              start=(c == 0), stop=(c == 1))),
                     after=[sq, otok])
            evt = p.op("vector",
                       (lambda e, tb=tb: e.tensor_copy(
                           out=nrow[0:1, tb * 512:(tb + 1) * 512],
                           in_=pb_n[:])),
                       after=[p.eng_at("tensor")])
        k1r = p.op("vector", (lambda e: e.tensor_reduce(
            out=numax[0:1, 1:2], in_=nrow[:], axis=AX.X, op=ALU.max)),
            after=[evt])
        k1 = p.op("scalar", (lambda e: e.activation(
            out=numax[0:1, 1:2], in_=numax[0:1, 1:2], func=AF.Sqrt)),
            after=[k1r])
        for h in range(L3F_HEADS):
            with scope(p):
                evq = None
                for tb in range(8):
                    qtl = []
                    for c in range(2):
                        qtl.append(p.dma(
                            "sync", qblk[:, c * 512:(c + 1) * 512],
                            q_fm[(h * 2 + c) * 128:(h * 2 + c + 1) * 128,
                                 tb * 512:(tb + 1) * 512],
                            after=[p.eng_at("tensor")]))
                    for c in range(2):
                        sq = p.op("vector",
                                  (lambda e, c=c: e.tensor_mul(
                                      sqs[:], qblk[:, c * 512:(c + 1) * 512],
                                      qblk[:, c * 512:(c + 1) * 512])),
                                  after=[qtl[c]] + ([evq] if evq else []))
                        p.op("tensor",
                             (lambda e, c=c: e.matmul(
                                 pb_n[:], onecf[:], sqs[:], start=(c == 0),
                                 stop=(c == 1))),
                             after=[sq])
                    evq = p.op("vector",
                               (lambda e, tb=tb: e.tensor_copy(
                                   out=nrow[0:1, tb * 512:(tb + 1) * 512],
                                   in_=pb_n[:])),
                               after=[p.eng_at("tensor")])
                k2 = p.op("vector", (lambda e: e.tensor_reduce(
                    out=numax[0:1, 0:1], in_=nrow[:], axis=AX.X,
                    op=ALU.max)), after=[evq, k1])
                s1 = p.op("scalar", (lambda e: e.activation(
                    out=numax[0:1, 0:1], in_=numax[0:1, 0:1],
                    func=AF.Sqrt)), after=[k2, k1])
                k3 = p.op("vector", (lambda e: e.tensor_scalar(
                    out=numax[0:1, 0:1], in0=numax[0:1, 0:1],
                    scalar1=numax[0:1, 1:2], scalar2=-1.0, op0=ALU.mult,
                    op1=ALU.mult)), after=[s1])
                pb_u = p.psum([128, 1], F32)
                mu = p.op("tensor", (lambda e: e.matmul(
                    pb_u[:], oatt1f[:], numax[0:1, 0:1], start=True,
                    stop=True)), after=[k3])
                nut = p.op("vector", (lambda e: e.tensor_copy(
                    out=nu_col[:], in_=pb_u[:])), after=[mu])
                if DEBUG_TMPS:
                    dbg_nu = p.dram_out(f"dbg_nu{h}", [128, 1], F32)
                    p.dma("gpsimd", dbg_nu, nu_col[:], sem="st",
                          after=[nut])
                    dbg_qn = p.dram_out(f"dbg_qn{h}", [1, T], F32)
                    p.dma("gpsimd", dbg_qn, nrow[:], sem="st", after=[nut])
                for B in range(L3F_BLKS):
                    qt0 = B * 512
                    qtoks = []
                    for c in range(2):
                        qtoks.append(p.dma(
                            "sync", qblk[:, c * 512:(c + 1) * 512],
                            q_fm[(h * 2 + c) * 128:(h * 2 + c + 1) * 128,
                                 qt0:qt0 + 512],
                            after=[p.eng_at("tensor")]))
                    q_tok = qtoks[-1]
                    with scope(p):
                        pb_s = [p.psum([128, 512], F32) for _ in range(2)]
                        pb_den = p.psum([1, 512], F32)
                        etoks = []
                        for ktc in range(32):
                            bank = pb_s[ktc % 2]
                            last = None
                            for c in range(2):
                                last = p.op(
                                    "tensor",
                                    (lambda e, bank=bank, c=c, ktc=ktc:
                                     e.matmul(
                                        bank[:],
                                        kc_all[:, c * T + ktc * 128:
                                               c * T + (ktc + 1) * 128],
                                        qblk[:, c * 512:(c + 1) * 512],
                                        start=(c == 0), stop=(c == 1))),
                                    after=([kc_tok, q_tok, vc_tok, nut]
                                           + ([etoks[ktc - 2]]
                                              if ktc >= 2 else []))
                                    if c == 0 else (), inc=(c == 1))
                            et = p.op("scalar",
                                      (lambda e, bank=bank, ktc=ktc:
                                       e.activation(
                                          out=E[:, ktc * 512:(ktc + 1) * 512],
                                          in_=bank[:], func=AF.Exp,
                                          bias=nu_col[:, 0:1])),
                                      after=[last])
                            etoks.append(et)
                            p.op("tensor",
                                 (lambda e, ktc=ktc, st=(ktc == 0),
                                  sp=(ktc == 31), pb_den=pb_den: e.matmul(
                                     pb_den[:], onecb[:],
                                     E[:, ktc * 512:(ktc + 1) * 512],
                                     start=st, stop=sp)),
                                 after=[et])
                        dent = p.eng_at("tensor")
                        d1 = p.op("vector", (lambda e, pb_den=pb_den:
                                             e.tensor_copy(
                            out=den_row[:], in_=pb_den[:])), after=[dent])
                        d2 = p.op("vector", (lambda e: e.reciprocal(
                            out=den_row[:], in_=den_row[:])), after=[d1])
                        d3 = p.op("vector", (lambda e: e.tensor_copy(
                            out=rb_sb[0:1, 0:8], in_=den_row[0:1, 0:8])),
                            after=[d2])
                        pb_r = p.psum([128, 512], F32)
                        mr = p.op("tensor",
                                  (lambda e, pb_r=pb_r: e.matmul(
                                      pb_r[:], oatt1f[:],
                                      den_row[:], start=True,
                                      stop=True)),
                                  after=[d3])
                        rbt = p.op("vector", (lambda e, pb_r=pb_r:
                                               e.tensor_copy(
                            out=rb_sb[:], in_=pb_r[:])), after=[mr])
                        if DEBUG_TMPS and B == 0 and h == 0:
                            dbg_pbr = p.dram_out("dbg_pbr", [128, 512], F32)
                            dbg_t = p.sb([128, 512], F32, name="dbgt")
                            dc1 = p.op("vector", (lambda e, pb_r=pb_r:
                                                  e.tensor_copy(
                                out=dbg_t[:], in_=pb_r[:])), after=[mr])
                            p.dma("gpsimd", dbg_pbr, dbg_t[:], sem="st",
                                  after=[dc1])
                        if DEBUG_TMPS and B == 0:
                            dbg_rb = p.dram_out(f"dbg_rb{h}", [128, 512],
                                                F32)
                            p.dma("gpsimd", dbg_rb, rb_sb[:], sem="st",
                                  after=[rbt])
                            dbg_dn = p.dram_out(f"dbg_dn{h}", [1, 512], F32)
                            p.dma("gpsimd", dbg_dn, den_row[:], sem="st",
                                  after=[rbt])
                    with scope(p):
                        pb_av = [p.psum([128, 512], F32) for _ in range(2)]
                        obp3 = Pool(p, 3, [128, 512], BF16,
                                    tag=f"l3fo{h}_{B}_")
                        for dv in range(2):
                            avt = None
                            for ktc in range(32):
                                avt = p.op(
                                    "tensor",
                                    (lambda e, dv=dv, ktc=ktc: e.matmul(
                                        pb_av[dv][:],
                                        vc_all[:, ktc * 256 + dv * 128:
                                               ktc * 256 + (dv + 1) * 128],
                                        E[:, ktc * 512:(ktc + 1) * 512],
                                        start=(ktc == 0), stop=(ktc == 31))),
                                    after=[rbt] if (ktc == 0 and dv == 0)
                                    else ())
                            ob, war = obp3.acquire()
                            ot = p.op("vector",
                                      (lambda e, dv=dv, ob=ob,
                                       pb_av=pb_av: e.tensor_mul(
                                          ob[:], pb_av[dv][:], rb_sb[:])),
                                      after=[avt, war, rbt])
                            stok = p.dma(
                                "gpsimd",
                                o_fm[(h * 2 + dv) * 128:
                                     (h * 2 + dv + 1) * 128, qt0:qt0 + 512],
                                ob[:], sem="st", after=[ot])
                            obp3.used(stok)
                    p.bar(("tensor", "vector", "scalar"))
    p.bar()
    if DEBUG_TMPS:
        dbg_E = p.dram_out("dbg_E2", [128, 32 * 512], BF16)
        p.dma("gpsimd", dbg_E, E[:], sem="st")
        dbg_dr = p.dram_out("dbg_dr", [1, 512], F32)
        p.dma("gpsimd", dbg_dr, den_row[:], sem="st")
        dbg_rb2 = p.dram_out("dbg_rb2", [128, 512], F32)
        p.dma("gpsimd", dbg_rb2, rb_sb[:], sem="st")
        p.bar()

    with scope(p):
        o_get, o_tok, _ = load_fm_resident(p, o_fm, D, T, "l3for")
        xrp = Pool(p, 3, [128, 512], F32, tag="l3fxr")
        ofp = Pool(p, 3, [128, 512], F32, tag="l3fof")

        def proj_epi(no, ts0, banks, mmtok):
            last = None
            for (bank, t0, tw) in banks:
                xr, w1 = xrp.acquire()
                tl = p.dma("sync", xr[:],
                           xn_f[no * 128:(no + 1) * 128, t0:t0 + tw],
                           after=[w1])
                of, w2 = ofp.acquire()
                last = p.op("vector",
                            (lambda e, of=of, xr=xr, bank=bank: e.tensor_add(
                                of[:], bank[:, :512], xr[:])),
                            after=[tl, mmtok, w2])
                xrp.used(last)
                st1 = p.dma("gpsimd", xmid[no * 128:(no + 1) * 128,
                                           t0:t0 + tw],
                            of[:], sem="st", after=[last])
                ofp.used(st1)
            return last

        gemm_fm(p, io["l3f_woT"], lambda k, t0, tw: o_get(k, t0, tw),
                proj_epi, K=D, NO=D, Tt=T, name="l3fwo")
    emit_rms_fm(p, xmid, nx2_bf, io["l3f_g2"], 1e-6, "l3frms2")

    with scope(p):
        nx2_get, nx2_tok, _ = load_fm_resident(p, nx2_bf, D, T, "l3fnx2")
        ones1f = p.sb([1, 128], F32)
        p.op("vector", lambda e: e.memset(ones1f[:], 1.0))
        otok = p.eng_at("vector")
        with scope(p):
            wr_st = p.sb([128, 8], F32)
            wr_bf = p.sb([128, 8 * 8], BF16)
            for k in range(8):
                tl = p.dma("sync", wr_st[:],
                           io["l3f_wrT"][k * 128:(k + 1) * 128, :],
                           after=[p.eng_at("vector")] if k else ())
                wrtok = p.op("vector",
                             (lambda e, k=k: e.tensor_copy(
                                 out=wr_bf[:, k * 8:(k + 1) * 8],
                                 in_=wr_st[:])),
                             after=[tl])
            eb_row = p.sb([1, 8], F32)
            eb_bc = p.sb([128, 8], F32)
            pb_b = p.psum([128, 8], F32)
            ebt = p.dma("sync", eb_row[:], io["l3f_ebias"])
            mm = p.op("tensor", lambda e: e.matmul(pb_b[:], ones1f[:],
                                                   eb_row[:], start=True,
                                                   stop=True),
                      after=[ebt, otok])
            ebbc = p.op("vector", lambda e: e.tensor_copy(out=eb_bc[:],
                                                          in_=pb_b[:]),
                        after=[mm])
            pb_g = p.psum([128, 8], F32)
            pb_t = p.psum([8, 128], F32)
            gt = p.sb([128, 8], F32)
            gmax = p.sb([128, 1], F32)
            gsum = p.sb([128, 1], F32)
            m2c = p.sb([128, 8], F32)
            msk_g = p.sb([128, 8], F32)
            er = p.sb([128, 8], F32)
            comb_T = p.sb([8, T], F32)
            identf = p.sb([128, 128], F32)
            idf = p.dma("sync", identf[:], io["ident_f32"])
            prev_evt = None
            for tch in range(32):
                last = None
                for k in range(8):
                    xap, xtok = nx2_get(k, 0, T)
                    last = p.op("tensor",
                                (lambda e, k=k, tch=tch, xap=xap: e.matmul(
                                    pb_g[:],
                                    xap[:, tch * 128:(tch + 1) * 128],
                                    wr_bf[:, k * 8:(k + 1) * 8],
                                    start=(k == 0), stop=(k == 7))),
                                after=[xtok, wrtok, p.eng_at("vector")]
                                if k == 0 else (), inc=(k == 7))
                g1 = p.op("vector", lambda e: e.tensor_add(gt[:], pb_g[:],
                                                           eb_bc[:]),
                          after=[last, ebbc])
                g2 = p.op("vector", (lambda e: e.tensor_reduce(
                    out=gmax[:], in_=gt[:], axis=AX.X, op=ALU.max,
                    negate=True)), after=[g1])
                g3 = p.op("scalar", (lambda e: e.activation(
                    out=gt[:], in_=gt[:], func=AF.Exp,
                    bias=gmax[:, 0:1])), after=[g2])
                g4 = p.op("vector", (lambda e: e.tensor_reduce(
                    out=gsum[:], in_=gt[:], axis=AX.X, op=ALU.add)),
                    after=[g3])
                g5 = p.op("vector", (lambda e: e.reciprocal(
                    out=gsum[:], in_=gsum[:])), after=[g4])
                g6 = p.op("vector", (lambda e: e.tensor_scalar(
                    out=gt[:], in0=gt[:], scalar1=gsum[:, 0:1],
                    scalar2=None, op0=ALU.mult)), after=[g5])
                g7 = p.op("vector", (lambda e: e.max(out=m2c[:], in_=gt[:])),
                          after=[g6])
                g8 = p.op("vector", (lambda e: e.tensor_scalar(
                    out=msk_g[:], in0=gt[:], scalar1=m2c[:, 1:2],
                    scalar2=None, op0=ALU.is_ge)), after=[g7])
                g9 = p.op("scalar", (lambda e: e.activation(
                    out=er[:], in_=gt[:], func=AF.Exp)), after=[g8])
                g10 = p.op("vector", (lambda e: e.tensor_mul(
                    er[:], er[:], msk_g[:])), after=[g9])
                g11 = p.op("vector", (lambda e: e.tensor_reduce(
                    out=gsum[:], in_=er[:], axis=AX.X, op=ALU.add)),
                    after=[g10])
                g12 = p.op("vector", (lambda e: e.reciprocal(
                    out=gsum[:], in_=gsum[:])), after=[g11])
                g13 = p.op("vector", (lambda e: e.tensor_scalar(
                    out=er[:], in0=er[:], scalar1=gsum[:, 0:1],
                    scalar2=None, op0=ALU.mult)), after=[g12])
                g14 = p.op("tensor", (lambda e: e.transpose(
                    pb_t[:], er[:], identf[:])),
                    after=[g13, idf] + ([prev_evt] if prev_evt else []))
                prev_evt = p.op("vector", (lambda e, tch=tch: e.tensor_copy(
                    out=comb_T[:, tch * 128:(tch + 1) * 128], in_=pb_t[:])),
                    after=[g14])
            p.dma("gpsimd", comb_hbm, comb_T[:], sem="st", after=[prev_evt])
        p.bar()

        obp = Pool(p, 3, [128, 512], BF16, tag="l3fs")
        hfp = Pool(p, 3, [128, 512], F32, tag="l3fhf")
        s1p = Pool(p, 3, [128, 512], BF16, tag="l3fs1")

        def s1_epi(no, ts0, banks, mmtok):
            last = None
            for (bank, t0, tw) in banks:
                ob, war = obp.acquire()
                last = p.op("vector",
                            (lambda e, bank=bank, ob=ob: e.tensor_copy(
                                out=ob[:, :512], in_=bank[:, :512])),
                            after=[mmtok, war])
                stok = p.dma("gpsimd", s1_hb[no * 128:(no + 1) * 128,
                                             t0:t0 + tw],
                             ob[:, :tw], sem="st", after=[last])
                obp.used(stok)
            return last

        gemm_fm(p, io["l3f_sw1T"], lambda k, t0, tw: nx2_get(k, t0, tw),
                s1_epi, K=D, NO=D, Tt=T, name="l3fs1g")
        p.bar()

        def s3_epi(no, ts0, banks, mmtok):
            last = None
            for (bank, t0, tw) in banks:
                s1t, war1 = s1p.acquire()
                tl = p.dma("sync", s1t[:],
                           s1_hb[no * 128:(no + 1) * 128, t0:t0 + tw],
                           after=[war1])
                hf, war2 = hfp.acquire()
                sl = p.op("scalar",
                          (lambda e, s1t=s1t, hf=hf: e.activation(
                              out=hf[:], in_=s1t[:], func=AF.Silu)),
                          after=[tl, war2])
                s1p.used(sl)
                ob, war3 = obp.acquire()
                last = p.op("vector",
                            (lambda e, hf=hf, bank=bank, ob=ob: e.tensor_mul(
                                ob[:, :512], bank[:, :512], hf[:])),
                            after=[sl, mmtok, war3])
                hfp.used(last)
                stok = p.dma("gpsimd", hs_hb[no * 128:(no + 1) * 128,
                                             t0:t0 + tw],
                             ob[:, :512], sem="st", after=[last])
                obp.used(stok)
            return last

        gemm_fm(p, io["l3f_sw3T"], lambda k, t0, tw: nx2_get(k, t0, tw),
                s3_epi, K=D, NO=D, Tt=T, name="l3fs3g")
        p.bar()

        ofp = Pool(p, 3, [128, 512], F32, tag="l3fso")
        with scope(p):
            hs_get = load_fm_blocked(p, hs_hb, D, 2048, "l3fhsr")

            def so_epi(no, ts0, banks, mmtok):
                last = None
                for (bank, t0, tw) in banks:
                    of, w2 = ofp.acquire()
                    last = p.op("vector",
                                (lambda e, of=of, bank=bank: e.tensor_copy(
                                    out=of[:], in_=bank[:, :512])),
                                after=[mmtok, w2])
                    st1 = p.dma("gpsimd", acc_h[no * 128:(no + 1) * 128,
                                                t0:t0 + tw],
                                of[:], sem="st", after=[last])
                    ofp.used(st1)
                return last

            gemm_fm(p, io["l3f_sw2T"], lambda k, t0, tw: hs_get(k, t0, tw),
                    so_epi, K=D, NO=D, Tt=T, name="l3fs2g", TBLK=2048)
        p.bar()

        crp = Pool(p, 3, [1, 512], F32, tag="l3fcr")
        cbp = Pool(p, 3, [128, 512], F32, tag="l3fcb")
        acp = Pool(p, 3, [128, 512], F32, tag="l3fac")
        pb_cb = p.psum([128, 512], F32)
        pbcb_reader = [None]
        for ex in range(8):
            def e1_epi(no, ts0, banks, mmtok, ex=ex):
                last = None
                for (bank, t0, tw) in banks:
                    ob, war = obp.acquire()
                    last = p.op("vector",
                                (lambda e, bank=bank, ob=ob: e.tensor_copy(
                                    out=ob[:, :512], in_=bank[:, :512])),
                                after=[mmtok, war])
                    stok = p.dma("gpsimd", s1_hb[no * 128:(no + 1) * 128,
                                                 t0:t0 + tw],
                                 ob[:, :tw], sem="st", after=[last])
                    obp.used(stok)
                return last
            gemm_fm(p, io[f"l3f_ew1T_{ex}"],
                    lambda k, t0, tw: nx2_get(k, t0, tw),
                    e1_epi, K=D, NO=D, Tt=T, name=f"l3fe1_{ex}", TBLK=2048)
            p.bar()
            gemm_fm(p, io[f"l3f_ew3T_{ex}"],
                    lambda k, t0, tw: nx2_get(k, t0, tw),
                    s3_epi, K=D, NO=D, Tt=T, name=f"l3fe3_{ex}", TBLK=2048)
            p.bar()
            with scope(p):
                he_get = load_fm_blocked(p, hs_hb, D, 2048, f"l3fhe{ex}")

                def eo_epi(no, ts0, banks, mmtok, ex=ex):
                    last = None
                    for (bank, t0, tw) in banks:
                        cr, warr = crp.acquire()
                        tlc = p.dma("sync", cr[:],
                                    comb_hbm[ex:ex + 1, t0:t0 + 512],
                                    after=[warr])
                        cb, warc = cbp.acquire()
                        mmc = p.op("tensor",
                                   (lambda e, cr=cr: e.matmul(
                                       pb_cb[:], ones1f[:], cr[:],
                                       start=True, stop=True)),
                                   after=[tlc, warc, mmtok, pbcb_reader[0]])
                        cbt = p.op("vector",
                                   (lambda e, cb=cb: e.tensor_copy(
                                       out=cb[:], in_=pb_cb[:])),
                                   after=[mmc])
                        pbcb_reader[0] = cbt
                        crp.used(mmc)
                        act, wara = acp.acquire()
                        tl = p.dma("sync", act[:],
                                   acc_h[no * 128:(no + 1) * 128,
                                         t0:t0 + tw], after=[wara])
                        v2 = p.op("vector",
                                  (lambda e, bank=bank, cb=cb: e.tensor_mul(
                                      bank[:, :512], bank[:, :512], cb[:])),
                                  after=[cbt, mmtok])
                        cbp.used(v2)
                        last = p.op("vector",
                                    (lambda e, act=act, bank=bank:
                                     e.tensor_add(act[:], act[:],
                                                  bank[:, :512])),
                                    after=[v2, tl])
                        stok = p.dma("gpsimd",
                                     acc_h[no * 128:(no + 1) * 128,
                                           t0:t0 + tw],
                                     act[:], sem="st", after=[last])
                        acp.used(stok)
                    return last

                gemm_fm(p, io[f"l3f_ew2T_{ex}"],
                        lambda k, t0, tw: he_get(k, t0, tw),
                        eo_epi, K=D, NO=D, Tt=T, name=f"l3feo_{ex}",
                        TBLK=2048)
            p.bar()
    p.bar()

    with scope(p):
        ap_ = Pool(p, 3, [128, 512], F32, tag="l3ff1")
        bp_ = Pool(p, 3, [128, 512], F32, tag="l3ff2")
        cp_ = Pool(p, 3, [128, 512], F32, tag="l3ff3")
        for c in range(8):
            for tb in range(8):
                sl = (slice(c * 128, (c + 1) * 128),
                      slice(tb * 512, (tb + 1) * 512))
                a_, w1 = ap_.acquire()
                t1 = p.dma("sync", a_[:], x_in[sl[0], sl[1]], after=[w1])
                b_, w2 = bp_.acquire()
                t2 = p.dma("sync", b_[:], xmid[sl[0], sl[1]], after=[w2])
                c_, w3 = cp_.acquire()
                t3 = p.dma("sync", c_[:], acc_h[sl[0], sl[1]], after=[w3])
                v1 = p.op("vector", (lambda e, a_=a_, b_=b_: e.tensor_add(
                    a_[:], a_[:], b_[:])), after=[t1, t2])
                bp_.used(v1)
                v2 = p.op("vector", (lambda e, a_=a_, c_=c_: e.tensor_add(
                    a_[:], a_[:], c_[:])), after=[t3])
                cp_.used(v2)
                st1 = p.dma("gpsimd", x_out[sl[0], sl[1]], a_[:], sem="st",
                            after=[v2])
                ap_.used(st1)
    p.bar()


def l3_ffn_host_inputs(inp):
    import ml_dtypes
    d = {
        "l3f_g1": np.ascontiguousarray(inp["L3_ffn_g1"].reshape(8, 128).T),
        "l3f_g2": np.ascontiguousarray(inp["L3_ffn_g2"].reshape(8, 128).T),
        "l3f_wqT": np.ascontiguousarray(inp["L3_ffn_Wq"].T),
        "l3f_wkcT": np.ascontiguousarray(inp["L3_ffn_Wkc"].T),
        "l3f_wvcT": np.ascontiguousarray(inp["L3_ffn_Wvc"].T),
        "l3f_woT": np.ascontiguousarray(inp["L3_ffn_Wo"].T),
        "l3f_wrT": np.ascontiguousarray(inp["L3_ffn_Wr"].T),
        "l3f_ebias": np.ascontiguousarray(
            inp["L3_ffn_expert_bias"].reshape(1, 8)),
        "l3f_sw1T": np.ascontiguousarray(inp["L3_ffn_sW1"].T),
        "l3f_sw2T": np.ascontiguousarray(inp["L3_ffn_sW2"].T),
        "l3f_sw3T": np.ascontiguousarray(inp["L3_ffn_sW3"].T),
        "ident_bf": np.eye(128, dtype=ml_dtypes.bfloat16),
        "ident_f32": np.eye(128, dtype=np.float32),
    }
    for ex in range(8):
        d[f"l3f_ew1T_{ex}"] = np.ascontiguousarray(inp["L3_ffn_eW1"][ex].T)
        d[f"l3f_ew2T_{ex}"] = np.ascontiguousarray(inp["L3_ffn_eW2"][ex].T)
        d[f"l3f_ew3T_{ex}"] = np.ascontiguousarray(inp["L3_ffn_eW3"][ex].T)
    return d


def build_l3_ffn_stage():
    p = Prog("l3_ffn")
    io = {"x": p.dram_in("x", [D, T], F32),
          "xo": p.dram_out("xo", [D, T], F32)}
    for nm, shape in (("l3f_g1", [128, 8]), ("l3f_g2", [128, 8]),
                      ("l3f_wqT", [D, D]), ("l3f_wkcT", [D, 256]),
                      ("l3f_wvcT", [D, 256]), ("l3f_woT", [D, D]),
                      ("l3f_wrT", [D, 8]), ("l3f_ebias", [1, 8]),
                      ("l3f_sw1T", [D, D]), ("l3f_sw2T", [D, D]),
                      ("l3f_sw3T", [D, D])):
        io[nm] = p.dram_in(nm, shape, F32)
    io["ident_bf"] = p.dram_in("ident_bf", [128, 128], BF16)
    io["ident_f32"] = p.dram_in("ident_f32", [128, 128], F32)
    for ex in range(8):
        for nm in ("ew1T", "ew2T", "ew3T"):
            io[f"l3f_{nm}_{ex}"] = p.dram_in(f"l3f_{nm}_{ex}", [D, D], F32)
    emit_l3_ffn(p, io)
    return p



def emit_ln_fm2(p, x_load, out_bf_hbm, g_col_hbm=None, b_col_hbm=None,
                Tt=T, eps=1e-5, name="ln2"):
    DC = D // 128
    NBT = Tt // 512
    with scope(p):
        xp = Pool(p, 3, [128, 512], F32)
        x2p = Pool(p, 3, [128, 512], F32)
        obp = Pool(p, 3, [128, 512], BF16)
        nb = p.sb([128, 512], F32)
        onec = p.sb([128, 1], F32)
        ones1f = p.sb([1, 128], F32)
        stats_x = p.sb([1, Tt], F32)
        stats_x2 = p.sb([1, Tt], F32)
        arow = p.sb([1, Tt], F32)
        brow = p.sb([1, Tt], F32)
        mean = p.sb([1, Tt], F32)
        var = p.sb([1, Tt], F32)
        tmpr = p.sb([1, Tt], F32)
        gcol = p.sb([128, DC], F32)
        bcol = p.sb([128, DC], F32)
        pb_sx = p.psum([1, 512], F32)
        pb_sx2 = p.psum([1, 512], F32)
        pb_a = p.psum([128, 512], F32)
        pb_b = p.psum([128, 512], F32)
        p.op("vector", lambda e: e.memset(onec[:], 1.0))
        p.op("vector", lambda e: e.memset(ones1f[:], 1.0))
        onetok = p.eng_at("vector")
        gtok = p.dma("sync", gcol[:], g_col_hbm) if g_col_hbm is not None \
            else None
        btok = p.dma("sync", bcol[:], b_col_hbm) if b_col_hbm is not None \
            else None
        evtok = None
        for tb in range(NBT):
            for c in range(DC):
                xt, w1 = xp.acquire()
                tload = x_load(c, tb, xt, w1)
                x2, w2 = x2p.acquire()
                sq = p.op("vector",
                          (lambda e, xt=xt, x2=x2: e.tensor_mul(
                              x2[:], xt[:], xt[:])), after=[tload, w2])
                p.op("tensor",
                     (lambda e, xt=xt, st=(c == 0): e.matmul(
                         pb_sx[:], onec[:], xt[:], start=st,
                         stop=(c == DC - 1))),
                     after=[tload, onetok] + ([evtok] if c == 0 and evtok
                                              else []),
                     inc=False)
                mm2 = p.op("tensor",
                           (lambda e, x2=x2, st=(c == 0), sp=(c == DC - 1):
                            e.matmul(pb_sx2[:], onec[:], x2[:], start=st,
                                     stop=sp)),
                           after=[sq])
                xp.used(mm2)
                x2p.used(mm2)
            mmt = p.eng_at("tensor")
            p.op("vector",
                 (lambda e, tb=tb: e.tensor_copy(
                     out=stats_x[0:1, tb * 512:(tb + 1) * 512],
                     in_=pb_sx[:])),
                 after=[mmt])
            evtok = p.op("vector",
                         (lambda e, tb=tb: e.tensor_copy(
                             out=stats_x2[0:1, tb * 512:(tb + 1) * 512],
                             in_=pb_sx2[:])))
        stok = p.eng_at("vector")
        p.op("vector", lambda e: e.tensor_scalar_mul(mean[:], stats_x[:],
                                                     1.0 / D), after=[stok])
        p.op("vector", lambda e: e.tensor_scalar_mul(var[:], stats_x2[:],
                                                     1.0 / D))
        vt = p.eng_at("vector")
        sq2 = p.op("scalar", lambda e: e.activation(out=tmpr[:], in_=mean[:],
                                                    func=AF.Square),
                   after=[vt])
        p.op("vector", lambda e: e.tensor_sub(var[:], var[:], tmpr[:]),
             after=[sq2])
        p.op("vector", lambda e: e.tensor_scalar_add(var[:], var[:],
                                                     float(eps)))
        vt = p.eng_at("vector")
        sq3 = p.op("scalar", lambda e: e.activation(out=tmpr[:], in_=var[:],
                                                    func=AF.Sqrt),
                   after=[vt])
        p.op("vector", lambda e: e.reciprocal(out=arow[:], in_=tmpr[:]),
             after=[sq3])
        p.op("vector", lambda e: e.tensor_mul(brow[:], mean[:], arow[:]))
        p.op("vector", lambda e: e.tensor_scalar_mul(brow[:], brow[:], -1.0))
        abtok = p.eng_at("vector")
        for tb in range(NBT):
            tsl = slice(tb * 512, (tb + 1) * 512)
            p.op("tensor",
                 (lambda e, tsl=tsl: e.matmul(pb_a[:], ones1f[:],
                                              arow[:, tsl], start=True,
                                              stop=True)),
                 after=[abtok, p.eng_at("vector")], inc=False)
            bt = p.op("tensor",
                      (lambda e, tsl=tsl: e.matmul(pb_b[:], ones1f[:],
                                                   brow[:, tsl], start=True,
                                                   stop=True)))
            for c in range(DC):
                xt, w1 = xp.acquire()
                tload = x_load(c, tb, xt, w1)
                m1 = p.op("vector",
                          (lambda e, xt=xt: e.tensor_mul(nb[:], xt[:],
                                                         pb_a[:])),
                          after=[tload, bt])
                xp.used(m1)
                m2 = p.op("vector",
                          (lambda e: e.tensor_add(nb[:], nb[:], pb_b[:])))
                ob, w3 = obp.acquire()
                if g_col_hbm is not None:
                    m3 = p.op("vector",
                              (lambda e, c=c: e.tensor_scalar(
                                  out=nb[:], in0=nb[:],
                                  scalar1=gcol[:, c:c + 1], scalar2=None,
                                  op0=ALU.mult)),
                              after=[gtok])
                    m4 = p.op("vector",
                              (lambda e, ob=ob, c=c: e.tensor_scalar(
                                  out=ob[:], in0=nb[:],
                                  scalar1=bcol[:, c:c + 1], scalar2=None,
                                  op0=ALU.add)),
                              after=[btok, w3])
                else:
                    m4 = p.op("vector",
                              (lambda e, ob=ob: e.tensor_copy(out=ob[:],
                                                              in_=nb[:])),
                              after=[w3])
                st1 = p.dma("gpsimd",
                            out_bf_hbm[c * 128:(c + 1) * 128, tsl],
                            ob[:], sem="st", after=[m4])
                obp.used(st1)


def shift_loader(src_hbm):
    def load(p):
        def x_load(c, tb, xt, war):
            if c < 4:
                return p.dma("sync", xt[:],
                             src_hbm[c * 128:(c + 1) * 128,
                                     tb * 512:(tb + 1) * 512], after=[war])
            if tb == 0:
                z = p.op("vector", (lambda e, xt=xt: e.memset(xt[:, 0:1],
                                                              0.0)),
                         after=[war])
                return p.dma("sync", xt[:, 1:512],
                             src_hbm[c * 128:(c + 1) * 128, 0:511],
                             after=[z])
            return p.dma("sync", xt[:],
                         src_hbm[c * 128:(c + 1) * 128,
                                 tb * 512 - 1:(tb + 1) * 512 - 1],
                         after=[war])
        return x_load
    return load



def emit_l0_ffn(p, io):
    x_in, x_out = io["x"], io["xo"]
    HEADS, DH, WIN = 8, 64, 512
    INNER = 512
    HID2 = 2816
    y_h = [p.dram_tmp(f"l0f_y{i}", [D, T], F32) for i in range(3)]
    hsh_bf = p.dram_tmp("l0f_hsh", [D, T], BF16)
    q_hb = p.dram_tmp("l0f_q", [INNER, T], BF16)
    k_hb = p.dram_tmp("l0f_k", [INNER, T], BF16)
    v_hb = p.dram_tmp("l0f_v", [T, INNER], BF16)
    o_hb = p.dram_tmp("l0f_o", [INNER, T], BF16)
    ln_bf = p.dram_tmp("l0f_ln", [D, T], BF16)
    hg_bf = p.dram_tmp("l0f_hg", [HID2, T], BF16)

    xn_b_d = p.dram_tmp("l0f_xnb", [D, T], BF16)
    emit_ln_fm(p, x_in, xn_b_d, y_h[0], name="l0fln0")
    p.bar()

    for L in range(2):
        ysrc = y_h[L]
        ydst = y_h[L + 1]
        with scope(p):
            ldp = Pool(p, 3, [128, 512], F32, tag=f"l0fsh{L}")
            obp = Pool(p, 3, [128, 512], BF16, tag=f"l0fshb{L}")
            xl = shift_loader(ysrc)(p)
            for c in range(8):
                for tb in range(8):
                    xt, war = ldp.acquire()
                    tl = xl(c, tb, xt, war)
                    ob, w2 = obp.acquire()
                    cv = p.op("vector",
                              (lambda e, xt=xt, ob=ob: e.tensor_copy(
                                  out=ob[:], in_=xt[:])), after=[tl, w2])
                    ldp.used(cv)
                    st1 = p.dma("gpsimd",
                                hsh_bf[c * 128:(c + 1) * 128,
                                       tb * 512:(tb + 1) * 512],
                                ob[:], sem="st", after=[cv])
                    obp.used(st1)
        p.bar()

        with scope(p):
            h_get, h_tok, _ = load_fm_resident(p, hsh_bf, D, T, f"l0fhr{L}")
            obp = Pool(p, 3, [128, 512], BF16, tag=f"l0fqk{L}")

            def qk_epi(no, ts0, banks, mmtok):
                last = None
                for (bank, t0, tw) in banks:
                    ob, war = obp.acquire()
                    last = p.op("vector",
                                (lambda e, bank=bank, ob=ob: e.tensor_copy(
                                    out=ob[:, :512], in_=bank[:, :512])),
                                after=[mmtok, war])
                    dst = q_hb if no < 4 else k_hb
                    cc = no % 4
                    stok = p.dma("gpsimd",
                                 dst[cc * 128:(cc + 1) * 128, t0:t0 + tw],
                                 ob[:, :tw], sem="st", after=[last])
                    obp.used(stok)
                return last

            gemm_fm(p, io[f"l0f_wqkv{L}"][:, 0:1024],
                    lambda k, t0, tw: h_get(k, t0, tw),
                    qk_epi, K=D, NO=1024, Tt=T, name=f"l0fqk{L}")

            wv_st = p.sb([128, 512], F32)
            wv_bf = p.sb([128, 8 * 512], BF16)
            for k in range(8):
                tl = p.dma("sync", wv_st[:],
                           io[f"l0f_wqkv{L}"][k * 128:(k + 1) * 128,
                                              1024:1536],
                           after=[p.eng_at("vector")] if k else ())
                wvtok = p.op("vector",
                             (lambda e, k=k: e.tensor_copy(
                                 out=wv_bf[:, k * 512:(k + 1) * 512],
                                 in_=wv_st[:])),
                             after=[tl])

            def v_epi(tch, banks, mmtok):
                last = None
                for (bank, n0, nw) in banks:
                    ob, war = obp.acquire()
                    last = p.op("vector",
                                (lambda e, bank=bank, ob=ob, nw=nw:
                                 e.tensor_copy(out=ob[:, :nw],
                                               in_=bank[:, :nw])),
                                after=[mmtok, war])
                    stok = p.dma("gpsimd",
                                 v_hb[tch * 128:(tch + 1) * 128, n0:n0 + nw],
                                 ob[:, :nw], sem="st", after=[last])
                    obp.used(stok)
                return last

            gemm_tm(p, (lambda k: (wv_bf[:, k * 512:(k + 1) * 512], wvtok)),
                    lambda k, t0, tw: h_get(k, t0, tw),
                    v_epi, K=D, NO=512, Tt=T, name=f"l0fv{L}")
        p.bar()

        with scope(p):
            pb_nA = p.psum([1, 512], F32)
            pb_nB = p.psum([1, 512], F32)
            pb_bc = p.psum([128, 512], F32)
            onesA = p.sb([128, 1], F32)
            onesB = p.sb([128, 1], F32)
            ones64 = p.sb([1, 64], F32)
            hnA = p.sb([1, 4 * T], F32)
            hnB = p.sb([1, 4 * T], F32)
            qsc = p.sb([128, 4], F32)
            ksc = p.sb([128, 4], F32)
            p.op("vector", lambda e: e.memset(onesA[:], 0.0))
            p.op("vector", lambda e: e.memset(onesA[0:64, :], 1.0))
            p.op("vector", lambda e: e.memset(onesB[:], 0.0))
            p.op("vector", lambda e: e.memset(onesB[64:128, :], 1.0))
            p.op("vector", lambda e: e.memset(ones64[:], 1.0))
            otok = p.eng_at("vector")
            t1 = p.dma("sync", qsc[:], io[f"l0f_qs{L}"])
            t2 = p.dma("sync", ksc[:], io[f"l0f_ks{L}"])
            ldp = Pool(p, 3, [128, 512], BF16, tag=f"l0fnr{L}")
            sqp = Pool(p, 2, [128, 512], F32, tag=f"l0fnq{L}")
            rt1 = Pool(p, 2, [128, 512], F32, tag=f"l0ft1{L}")
            rt2 = Pool(p, 2, [128, 512], F32, tag=f"l0ft2{L}")
            rt3 = Pool(p, 2, [128, 512], F32, tag=f"l0ft3{L}")
            obp = Pool(p, 3, [128, 512], BF16, tag=f"l0fno{L}")
            for (src, dst, sc_col, cs_t, ss_t, xscale) in (
                    (q_hb, q_hb, qsc, io["l0f_csq"], io["l0f_ssq"], 8.0),
                    (k_hb, k_hb, ksc, io["l0f_csk"], io["l0f_ssk"], None)):
                evt = None
                for c in range(4):
                    for tb in range(8):
                        xt, war = ldp.acquire()
                        tl = p.dma("sync", xt[:],
                                   src[c * 128:(c + 1) * 128,
                                       tb * 512:(tb + 1) * 512], after=[war])
                        sq_, w2 = sqp.acquire()
                        sq = p.op("vector",
                                  (lambda e, xt=xt, sq_=sq_: e.tensor_mul(
                                      sq_[:], xt[:], xt[:])),
                                  after=[tl, w2])
                        p.op("tensor",
                             (lambda e, sq_=sq_: e.matmul(
                                 pb_nA[:], onesA[:], sq_[:],
                                 start=True, stop=True)),
                             after=[sq, otok] + ([evt] if evt else []),
                             inc=False)
                        mm = p.op("tensor",
                                  (lambda e, sq_=sq_: e.matmul(
                                      pb_nB[:], onesB[:], sq_[:],
                                      start=True, stop=True)))
                        sqp.used(mm)
                        p.op("vector",
                             (lambda e, c=c, tb=tb: e.tensor_copy(
                                 out=hnA[0:1, c * T + tb * 512:
                                         c * T + (tb + 1) * 512],
                                 in_=pb_nA[:])),
                             after=[mm])
                        evt = p.op("vector",
                                   (lambda e, c=c, tb=tb: e.tensor_copy(
                                       out=hnB[0:1, c * T + tb * 512:
                                               c * T + (tb + 1) * 512],
                                       in_=pb_nB[:])))
                        ldp.used(sq)
                v1 = p.op("vector", (lambda e: e.tensor_scalar_max(
                    hnA[:], hnA[:], 1e-24)), after=[evt])
                p.op("vector", (lambda e: e.tensor_scalar_max(
                    hnB[:], hnB[:], 1e-24)))
                vt_ = p.eng_at("vector")
                s1 = p.op("scalar", (lambda e: e.activation(
                    out=hnA[:], in_=hnA[:], func=AF.Sqrt)), after=[vt_])
                s1b = p.op("scalar", (lambda e: e.activation(
                    out=hnB[:], in_=hnB[:], func=AF.Sqrt)))
                v2 = p.op("vector", (lambda e: e.reciprocal(
                    out=hnA[:], in_=hnA[:])), after=[s1, s1b])
                v2 = p.op("vector", (lambda e: e.reciprocal(
                    out=hnB[:], in_=hnB[:])))
                evt2 = None
                for c in range(4):
                    for tb in range(8):
                        tsl = slice(tb * 512, (tb + 1) * 512)
                        m1 = p.op("tensor",
                                  (lambda e, c=c, tsl=tsl: e.matmul(
                                      pb_bc[0:64, :], ones64[:],
                                      hnA[0:1, c * T + tsl.start:
                                          c * T + tsl.stop],
                                      start=True, stop=True)),
                                  after=[v2, otok]
                                  + ([evt2] if evt2 else []), inc=False)
                        m2 = p.op("tensor",
                                  (lambda e, c=c, tsl=tsl: e.matmul(
                                      pb_bc[64:128, :], ones64[:],
                                      hnB[0:1, c * T + tsl.start:
                                          c * T + tsl.stop],
                                      start=True, stop=True)))
                        xt, war = ldp.acquire()
                        tl = p.dma("sync", xt[:],
                                   src[c * 128:(c + 1) * 128, tsl],
                                   after=[war])
                        qn_, w2_ = rt1.acquire()
                        n1 = p.op("vector",
                                  (lambda e, xt=xt, qn_=qn_: e.tensor_mul(
                                      qn_[:], xt[:], pb_bc[:])),
                                  after=[tl, m2, w2_])
                        ldp.used(n1)
                        evt2 = n1
                        n2 = p.op("vector",
                                  (lambda e, qn_=qn_, c=c, sc_col=sc_col:
                                   e.tensor_scalar(
                                      out=qn_[:], in0=qn_[:],
                                      scalar1=sc_col[:, c:c + 1],
                                      scalar2=None, op0=ALU.mult)),
                                  after=[t1, t2])
                        rh, w3 = rt2.acquire()
                        d1 = p.dma("gpsimd", rh[0:32, :], qn_[32:64, :],
                                   after=[n2, w3])
                        d2 = p.dma("gpsimd", rh[32:64, :], qn_[0:32, :])
                        d3 = p.dma("gpsimd", rh[64:96, :], qn_[96:128, :])
                        d4 = p.dma("gpsimd", rh[96:128, :], qn_[64:96, :])
                        ct, w4 = rt3.acquire()
                        c1 = p.dma("sync", ct[:],
                                   cs_t[0:128, tsl], after=[w4])
                        r1 = p.op("vector",
                                  (lambda e, qn_=qn_, ct=ct: e.tensor_mul(
                                      qn_[:], qn_[:], ct[:])),
                                  after=[c1, d4])
                        c2 = p.dma("sync", ct[:], ss_t[0:128, tsl],
                                   after=[r1])
                        r2 = p.op("vector",
                                  (lambda e, rh=rh, ct=ct: e.tensor_mul(
                                      rh[:], rh[:], ct[:])),
                                  after=[c2])
                        rt3.used(r2)
                        ob, w5 = obp.acquire()
                        if xscale is None:
                            r3 = p.op("vector",
                                      (lambda e, qn_=qn_, rh=rh, ob=ob:
                                       e.tensor_tensor(
                                          out=ob[:], in0=qn_[:], in1=rh[:],
                                          op=ALU.add)),
                                      after=[r2, w5])
                        else:
                            r3a = p.op("vector",
                                       (lambda e, qn_=qn_, rh=rh:
                                        e.tensor_add(qn_[:], qn_[:],
                                                     rh[:])),
                                       after=[r2])
                            r3 = p.op("vector",
                                      (lambda e, qn_=qn_, ob=ob, xscale=xscale:
                                       e.tensor_scalar(
                                          out=ob[:], in0=qn_[:],
                                          scalar1=float(xscale),
                                          scalar2=None, op0=ALU.mult)),
                                      after=[w5])
                        rt1.used(r3)
                        rt2.used(r3)
                        st1 = p.dma("gpsimd",
                                    dst[c * 128:(c + 1) * 128, tsl],
                                    ob[:], sem="st", after=[r3])
                        obp.used(st1)
                p.bar(("tensor", "vector", "scalar"))
        p.bar()

        with scope(p):
            ident = p.sb([128, 128], BF16)
            itok = p.dma("sync", ident[:], io["ident_bf"])
            k_res = p.sb([128, 4 * T], BF16, name=f"l0fkres{L}")
            ktoks = []
            for c in range(4):
                ktoks.append(p.dma("sync", k_res[:, c * T:(c + 1) * T],
                                   k_hb[c * 128:(c + 1) * 128, :]))
            k_tok = ktoks[-1]
            v_res = p.sb([128, 32 * 512], BF16, name=f"l0fvres{L}")
            vtoks = []
            for c in range(32):
                vtoks.append(p.dma("sync", v_res[:, c * 512:(c + 1) * 512],
                                   v_hb[c * 128:(c + 1) * 128, :]))
            v_tok = vtoks[-1]
            mask_sb = p.sb([128, 8 * 1024], F32, name=f"l0fmask{L}")
            mtok = p.dma("sync", mask_sb[:], io["l0f_mask"])
            qblk = p.sb([128, 4 * 512], BF16)
            Ssb = p.sb([128, 1024], F32)
            nmx = p.sb([128, 1], F32)
            rden = p.sb([128, 1], F32)
            Pbf = p.sb([128, 1024], BF16)
            PT = p.sb([128, 8 * 512], BF16, name=f"l0fPT{L}")
            for B in range(8):
                qt0 = B * 512
                qtoks = []
                for c in range(4):
                    qtoks.append(p.dma("sync",
                                       qblk[:, c * 512:(c + 1) * 512],
                                       q_hb[c * 128:(c + 1) * 128,
                                            qt0:qt0 + 512],
                                       after=[p.eng_at("tensor")]))
                q_tok = qtoks[-1]
                for h in range(HEADS):
                    hc, ho = h // 2, (h % 2) * 64
                    for qt in range(4):
                        with scope(p):
                            pb_s = [p.psum([128, 512], F32) for _ in range(2)]
                            pb_tr = [p.psum([128, 512], BF16)
                                     for _ in range(2)]
                            pb_av = p.psum([64, 512], F32)
                            ktc0 = B * 4 - 4
                            evs = []
                            for kk in range(8):
                                ktc = ktc0 + kk
                                bank = pb_s[kk % 2]
                                if ktc < 0:
                                    ev = p.op("vector",
                                              (lambda e, kk=kk: e.memset(
                                                  Ssb[:, kk * 128:
                                                      (kk + 1) * 128], 0.0)),
                                              after=[mtok])
                                    evs.append(ev)
                                    continue
                                mm = p.op(
                                    "tensor",
                                    (lambda e, bank=bank, hc=hc, ho=ho,
                                     qt=qt, ktc=ktc: e.matmul(
                                        bank[0:128, 0:128],
                                        qblk[hc * 512 + ho + (0):
                                             hc * 512 + ho + 64,
                                             qt * 128:(qt + 1) * 128]
                                        if False else
                                        qblk[:, hc * 512:(hc + 1) * 512]
                                        [ho:ho + 64,
                                         qt * 128:(qt + 1) * 128],
                                        k_res[:, hc * T:(hc + 1) * T]
                                        [ho:ho + 64,
                                         ktc * 128:(ktc + 1) * 128],
                                        start=True, stop=True)),
                                    after=[k_tok, q_tok, itok, v_tok, mtok]
                                    + ([evs[kk - 2]] if kk >= 2 else []))
                                ev = p.op("vector",
                                          (lambda e, bank=bank, kk=kk:
                                           e.tensor_copy(
                                              out=Ssb[:, kk * 128:
                                                      (kk + 1) * 128],
                                              in_=bank[0:128, 0:128])),
                                          after=[mm])
                                evs.append(ev)
                            mi = qt + (4 if B == 0 else 0)
                            a1 = p.op("vector",
                                      (lambda e, mi=mi: e.tensor_add(
                                          Ssb[:], Ssb[:],
                                          mask_sb[:, mi * 1024:
                                                  (mi + 1) * 1024])),
                                      after=[evs[-1]])
                            mx = p.op("vector",
                                      (lambda e: e.tensor_reduce(
                                          out=nmx[:], in_=Ssb[:], axis=AX.X,
                                          op=ALU.max, negate=True)),
                                      after=[a1])
                            ex = p.op("scalar",
                                      (lambda e: e.activation(
                                          out=Ssb[:], in_=Ssb[:],
                                          func=AF.Exp,
                                          bias=nmx[:, 0:1])),
                                      after=[mx])
                            rs = p.op("vector",
                                      (lambda e: e.tensor_reduce(
                                          out=rden[:], in_=Ssb[:],
                                          axis=AX.X, op=ALU.add)),
                                      after=[ex])
                            rc = p.op("vector",
                                      (lambda e: e.reciprocal(
                                          out=rden[:], in_=rden[:])),
                                      after=[rs])
                            pw = p.op("vector",
                                      (lambda e: e.tensor_scalar(
                                          out=Pbf[:], in0=Ssb[:],
                                          scalar1=rden[:, 0:1],
                                          scalar2=None, op0=ALU.mult)),
                                      after=[rc])
                            cpt = []
                            for kk in range(8):
                                trb = pb_tr[kk % 2]
                                tt = p.op("tensor",
                                          (lambda e, kk=kk, trb=trb:
                                           e.transpose(
                                              trb[:, 0:128],
                                              Pbf[:, kk * 128:
                                                  (kk + 1) * 128],
                                              ident[:])),
                                          after=[pw]
                                          + ([cpt[kk - 2]] if kk >= 2
                                             else []))
                                cpt.append(p.op(
                                    "vector",
                                    (lambda e, kk=kk, qt=qt, trb=trb:
                                     e.tensor_copy(
                                        out=PT[:, kk * 512 + qt * 128:
                                               kk * 512 + (qt + 1) * 128],
                                        in_=trb[:, 0:128])),
                                    after=[tt]))
                            avt = None
                            for kk in range(8):
                                ktc = ktc0 + kk
                                if ktc < 0:
                                    continue
                                avt = p.op(
                                    "tensor",
                                    (lambda e, kk=kk, ktc=ktc, hc=hc, ho=ho,
                                     qt=qt, st=avt is None:
                                     e.matmul(
                                        pb_av[:, 0:128],
                                        v_res[:, ktc * 512 + hc * 128 + ho:
                                              ktc * 512 + hc * 128 + ho
                                              + 64],
                                        PT[:, kk * 512 + qt * 128:
                                           kk * 512 + (qt + 1) * 128],
                                        start=st, stop=(kk == 7))),
                                    after=[cpt[kk]])
                            ob = p.sb([64, 128], BF16)
                            ot = p.op("vector",
                                      (lambda e, ob=ob: e.tensor_copy(
                                          out=ob[:], in_=pb_av[:, 0:128])),
                                      after=[avt])
                            p.dma("gpsimd",
                                  o_hb[hc * 128 + ho:hc * 128 + ho + 64,
                                       qt0 + qt * 128:qt0 + (qt + 1) * 128],
                                  ob[:], sem="st", after=[ot])
                        p.bar(("tensor", "vector", "scalar"))
        p.bar()

        with scope(p):
            o_get, o_tok, _ = load_fm_resident(p, o_hb, INNER, T,
                                               f"l0for{L}")
            xrp = Pool(p, 3, [128, 512], F32, tag=f"l0fxr{L}")
            ofp = Pool(p, 3, [128, 512], F32, tag=f"l0fof{L}")

            def wo_epi(no, ts0, banks, mmtok):
                last = None
                for (bank, t0, tw) in banks:
                    xr, w1 = xrp.acquire()
                    tl = p.dma("sync", xr[:],
                               ysrc[no * 128:(no + 1) * 128, t0:t0 + tw],
                               after=[w1])
                    of, w2 = ofp.acquire()
                    last = p.op("vector",
                                (lambda e, of=of, xr=xr, bank=bank:
                                 e.tensor_add(of[:], bank[:, :512], xr[:])),
                                after=[tl, mmtok, w2])
                    xrp.used(last)
                    st1 = p.dma("gpsimd",
                                y_h[2][no * 128:(no + 1) * 128, t0:t0 + tw],
                                of[:], sem="st", after=[last])
                    ofp.used(st1)
                return last

            gemm_fm(p, io[f"l0f_wo{L}"], lambda k, t0, tw: o_get(k, t0, tw),
                    wo_epi, K=INNER, NO=D, Tt=T, name=f"l0fwo{L}")
        p.bar()
        ymid = y_h[2]

        emit_ln_fm2(p, shift_loader(ymid)(p), ln_bf, io[f"l0f_lg{L}"],
                    io[f"l0f_lb{L}"], name=f"l0fln{L}")
        p.bar()

        with scope(p):
            ln_get, ln_tok, _ = load_fm_resident(p, ln_bf, D, T,
                                                 f"l0flnr{L}")
            b1c = p.sb([128, 44], F32)
            bt1 = p.dma("sync", b1c[:], io[f"l0f_b1{L}"])
            gp_ = Pool(p, 10, [128, 512], F32, tag=f"l0fg{L}")
            obp = Pool(p, 3, [128, 512], BF16, tag=f"l0fgo{L}")
            gel = Pool(p, 3, [128, 512], F32, tag=f"l0fge{L}")

            apend = {}

            def h1_epi(no, ts0, banks, mmtok):
                last = None
                for (bank, t0, tw) in banks:
                    if no % 2 == 0:
                        at, war = gp_.acquire()
                        last = p.op("vector",
                                    (lambda e, bank=bank, at=at, no=no:
                                     e.tensor_scalar(
                                        out=at[:], in0=bank[:, :512],
                                        scalar1=b1c[:, no:no + 1],
                                        scalar2=None, op0=ALU.add)),
                                    after=[mmtok, bt1, war])
                        apend[t0] = (at, last)
                    else:
                        at, atok = apend[t0]
                        ge, w2 = gel.acquire()
                        s1 = p.op("scalar",
                                  (lambda e, bank=bank, ge=ge, no=no:
                                   e.activation(
                                      out=ge[:], in_=bank[:, :512],
                                      func=AF.Gelu,
                                      bias=b1c[:, no:no + 1])),
                                  after=[mmtok, bt1, w2])
                        ob, w3 = obp.acquire()
                        last = p.op("vector",
                                    (lambda e, at=at, ge=ge, ob=ob:
                                     e.tensor_mul(ob[:, :512], at[:],
                                                  ge[:])),
                                    after=[s1, atok, w3])
                        gp_.used(last)
                        gel.used(last)
                        stok = p.dma("gpsimd",
                                     hg_bf[(no // 2) * 128:
                                           (no // 2 + 1) * 128,
                                           t0:t0 + tw],
                                     ob[:, :tw], sem="st", after=[last])
                        obp.used(stok)
                return last

            gemm_fm(p, io[f"l0f_w1{L}"], lambda k, t0, tw: ln_get(k, t0, tw),
                    h1_epi, K=D, NO=44 * 128, Tt=T, name=f"l0fh1{L}",
                    TBLK=2048)
        p.bar()

        with scope(p):
            h_get2 = load_fm_blocked(p, hg_bf, HID2, 2048, f"l0fh2r{L}")
            b2c = p.sb([128, 8], F32)
            bt2 = p.dma("sync", b2c[:], io[f"l0f_b2{L}"])
            xrp = Pool(p, 3, [128, 512], F32, tag=f"l0fx2{L}")
            xip = Pool(p, 3, [128, 512], F32, tag=f"l0fx3{L}")
            ofp = Pool(p, 3, [128, 512], F32, tag=f"l0fo2{L}")
            final = (L == 1)

            def w2_epi(no, ts0, banks, mmtok):
                last = None
                for (bank, t0, tw) in banks:
                    xr, w1 = xrp.acquire()
                    tl = p.dma("sync", xr[:],
                               ymid[no * 128:(no + 1) * 128, t0:t0 + tw],
                               after=[w1])
                    of, w2 = ofp.acquire()
                    v1 = p.op("vector",
                              (lambda e, of=of, bank=bank, no=no:
                               e.tensor_scalar(
                                  out=of[:], in0=bank[:, :512],
                                  scalar1=b2c[:, no:no + 1], scalar2=None,
                                  op0=ALU.add)),
                              after=[mmtok, bt2, w2])
                    last = p.op("vector",
                                (lambda e, of=of, xr=xr: e.tensor_add(
                                    of[:], of[:], xr[:])),
                                after=[tl, v1])
                    xrp.used(last)
                    if final:
                        xi, w3 = xip.acquire()
                        tl2 = p.dma("sync", xi[:],
                                    x_in[no * 128:(no + 1) * 128,
                                         t0:t0 + tw], after=[w3])
                        last = p.op("vector",
                                    (lambda e, of=of, xi=xi: e.tensor_add(
                                        of[:], of[:], xi[:])),
                                    after=[tl2, last])
                        xip.used(last)
                    st1 = p.dma("gpsimd",
                                (x_out if final else ydst)
                                [no * 128:(no + 1) * 128, t0:t0 + tw],
                                of[:], sem="st", after=[last])
                    ofp.used(st1)
                return last

            gemm_fm(p, io[f"l0f_w2{L}"], lambda k, t0, tw: h_get2(k, t0, tw),
                    w2_epi, K=HID2, NO=D, Tt=T, name=f"l0fw2{L}",
                    TBLK=2048)
        p.bar()


def l0_ffn_host_inputs(inp):
    import ml_dtypes
    DH, WIN = 64, 512
    half = np.arange(0, DH, 2, dtype=np.float32)
    inv_freq = 1.0 / 10000.0 ** (half / DH)
    t = np.arange(T, dtype=np.float32)
    freqs = t[:, None] * inv_freq[None, :]
    freqs = np.concatenate([freqs, freqs], -1)
    base = ((half + 0.4 * DH) / (1.4 * DH)).astype(np.float32)
    power = (t - T // 2) / float(WIN // 2)
    scale = base[None, :] ** power[:, None]
    scale = np.concatenate([scale, scale], -1)
    cos, sin = np.cos(freqs), np.sin(freqs)
    sgn = np.concatenate([-np.ones(DH // 2), np.ones(DH // 2)])[None, :]
    csq = (cos * scale).T
    ssq = (sin * scale * sgn).T
    csk = (cos / scale).T
    ssk = (sin / scale * sgn).T
    mk2 = lambda a: np.ascontiguousarray(
        np.concatenate([a, a], axis=0).astype(np.float32))
    m = np.zeros((8, 128, 1024), np.float32)
    for mi in range(8):
        qt = mi % 4
        b0 = mi >= 4
        r = np.arange(128)[:, None]
        kt = np.arange(1024)[None, :]
        valid = (kt >= qt * 128 + r) & (kt <= qt * 128 + r + 512)
        if b0:
            valid &= (kt >= 512)
        m[mi] = np.where(valid, 0.0, -1e30)
    masks = np.ascontiguousarray(m.transpose(1, 0, 2).reshape(128, 8 * 1024))
    d = {"ident_bf": np.eye(128, dtype=ml_dtypes.bfloat16),
         "l0f_csq": mk2(csq), "l0f_ssq": mk2(ssq),
         "l0f_csk": mk2(csk), "l0f_ssk": mk2(ssk),
         "l0f_mask": masks}
    HID = 2730
    for L in range(2):
        d[f"l0f_wqkv{L}"] = np.ascontiguousarray(inp["L0_ffn_wqkv"][L])
        d[f"l0f_wo{L}"] = np.ascontiguousarray(inp["L0_ffn_wo"][L])
        d[f"l0f_qs{L}"] = np.ascontiguousarray(
            np.tile(inp["L0_ffn_q_scale"][L], 8).reshape(4, 128).T)
        d[f"l0f_ks{L}"] = np.ascontiguousarray(
            np.tile(inp["L0_ffn_k_scale"][L], 8).reshape(4, 128).T)
        d[f"l0f_lg{L}"] = np.ascontiguousarray(
            inp["L0_ffn_ln_g"][L].reshape(8, 128).T)
        d[f"l0f_lb{L}"] = np.ascontiguousarray(
            inp["L0_ffn_ln_b"][L].reshape(8, 128).T)
        w1 = inp["L0_ffn_w1"][L]
        a = np.zeros((1024, 2816), np.float32); a[:, :HID] = w1[:, :HID]
        g = np.zeros((1024, 2816), np.float32); g[:, :HID] = w1[:, HID:]
        ab = a.reshape(1024, 22, 128)
        gb = g.reshape(1024, 22, 128)
        w1i = np.stack([ab, gb], axis=2).reshape(1024, 44 * 128)
        d[f"l0f_w1{L}"] = np.ascontiguousarray(w1i)
        b1 = inp["L0_ffn_b1"][L]
        ba = np.zeros(2816, np.float32); ba[:HID] = b1[:HID]
        bg = np.zeros(2816, np.float32); bg[:HID] = b1[HID:]
        b1i = np.stack([ba.reshape(22, 128), bg.reshape(22, 128)],
                       axis=1).reshape(44, 128)
        d[f"l0f_b1{L}"] = np.ascontiguousarray(b1i.T)
        w2 = np.zeros((2816, 1024), np.float32)
        w2[:HID] = inp["L0_ffn_w2"][L]
        d[f"l0f_w2{L}"] = w2
        d[f"l0f_b2{L}"] = np.ascontiguousarray(
            inp["L0_ffn_b2"][L].reshape(8, 128).T)
    return d


def build_l0_ffn_stage():
    p = Prog("l0_ffn")
    io = {"x": p.dram_in("x", [D, T], F32),
          "xo": p.dram_out("xo", [D, T], F32),
          "ident_bf": p.dram_in("ident_bf", [128, 128], BF16),
          "l0f_csq": p.dram_in("l0f_csq", [128, T], F32),
          "l0f_ssq": p.dram_in("l0f_ssq", [128, T], F32),
          "l0f_csk": p.dram_in("l0f_csk", [128, T], F32),
          "l0f_ssk": p.dram_in("l0f_ssk", [128, T], F32),
          "l0f_mask": p.dram_in("l0f_mask", [128, 8 * 1024], F32)}
    for L in range(2):
        io[f"l0f_wqkv{L}"] = p.dram_in(f"l0f_wqkv{L}", [1024, 1536], F32)
        io[f"l0f_wo{L}"] = p.dram_in(f"l0f_wo{L}", [512, 1024], F32)
        io[f"l0f_qs{L}"] = p.dram_in(f"l0f_qs{L}", [128, 4], F32)
        io[f"l0f_ks{L}"] = p.dram_in(f"l0f_ks{L}", [128, 4], F32)
        io[f"l0f_lg{L}"] = p.dram_in(f"l0f_lg{L}", [128, 8], F32)
        io[f"l0f_lb{L}"] = p.dram_in(f"l0f_lb{L}", [128, 8], F32)
        io[f"l0f_w1{L}"] = p.dram_in(f"l0f_w1{L}", [1024, 44 * 128], F32)
        io[f"l0f_b1{L}"] = p.dram_in(f"l0f_b1{L}", [128, 44], F32)
        io[f"l0f_w2{L}"] = p.dram_in(f"l0f_w2{L}", [2816, 1024], F32)
        io[f"l0f_b2{L}"] = p.dram_in(f"l0f_b2{L}", [128, 8], F32)
    emit_l0_ffn(p, io)
    return p



def emit_mamba(p, io, h_bf, out_fm, add_fm, name):
    DI = 2048
    xmpre = p.dram_tmp(name + "_xmp", [DI, T], F32)
    xm_f = p.dram_tmp(name + "_xmf", [DI, T], F32)
    xm_b = p.dram_tmp(name + "_xmb", [DI, T], BF16)
    sres = p.dram_tmp(name + "_sres", [DI, T], F32)
    xdbl_h = p.dram_tmp(name + "_xdbl", [96, T], F32)
    delta_h = p.dram_tmp(name + "_delta", [DI, T], F32)
    du_h = p.dram_tmp(name + "_du", [DI, T], F32)
    y_bf = p.dram_tmp(name + "_y", [DI, T], BF16)

    with scope(p):
        h_get, h_tok, _ = load_fm_resident(p, h_bf, D, T, name + "hr")
        bcol = p.sb([128, 32], F32)
        btok = p.dma("sync", bcol[:], io["l3a_inb"])
        ofp = Pool(p, 3, [128, 512], F32, tag=name + "xz")

        def xz_epi(no, ts0, banks, mmtok):
            last = None
            for (bank, t0, tw) in banks:
                of, war = ofp.acquire()
                if no < 16:
                    last = p.op("vector",
                                (lambda e, bank=bank, of=of, no=no,
                                 bcol=bcol: e.tensor_scalar(
                                    out=of[:], in0=bank[:, :512],
                                    scalar1=bcol[:, no:no + 1],
                                    scalar2=None, op0=ALU.add)),
                                after=[mmtok, btok, war])
                    dst = xmpre
                    cc = no
                else:
                    last = p.op("scalar",
                                (lambda e, bank=bank, of=of, no=no,
                                 bcol=bcol: e.activation(
                                    out=of[:], in_=bank[:, :512],
                                    func=AF.Silu,
                                    bias=bcol[:, no:no + 1])),
                                after=[mmtok, btok, war])
                    dst = sres
                    cc = no - 16
                st1 = p.dma("gpsimd",
                            dst[cc * 128:(cc + 1) * 128, t0:t0 + tw],
                            of[:], sem="st", after=[last])
                ofp.used(st1)
            return last

        gemm_fm(p, io["l3a_inwT"], lambda k, t0, tw: h_get(k, t0, tw),
                xz_epi, K=D, NO=4096, Tt=T, name=name + "xz", TBLK=2048)
    p.bar()

    with scope(p):
        wcol = p.sb([128, 16 * 4], F32)
        bcv = p.sb([128, 16], F32)
        t1 = p.dma("sync", wcol[:], io["l3a_convw"])
        t2 = p.dma("sync", bcv[:], io["l3a_convb"])
        ldp = Pool(p, 3, [128, 516], F32, tag=name + "cv")
        acp = Pool(p, 3, [128, 512], F32, tag=name + "ca")
        obp = Pool(p, 3, [128, 512], BF16, tag=name + "cb")
        ofp = Pool(p, 3, [128, 512], F32, tag=name + "cf")
        for c in range(16):
            for tb in range(8):
                xt, war = ldp.acquire()
                if tb == 0:
                    z = p.op("vector", (lambda e, xt=xt: e.memset(
                        xt[:, 0:3], 0.0)), after=[war])
                    tl = p.dma("sync", xt[:, 3:515],
                               xmpre[c * 128:(c + 1) * 128, 0:512],
                               after=[z])
                else:
                    tl = p.dma("sync", xt[:, 0:515],
                               xmpre[c * 128:(c + 1) * 128,
                                     tb * 512 - 3:(tb + 1) * 512],
                               after=[war])
                ac, w2 = acp.acquire()
                v0 = p.op("vector",
                          (lambda e, xt=xt, ac=ac, c=c: e.tensor_scalar(
                              out=ac[:], in0=xt[:, 0:512],
                              scalar1=wcol[:, c * 4:c * 4 + 1],
                              scalar2=None, op0=ALU.mult)),
                          after=[tl, t1, w2])
                for j in range(1, 4):
                    v0 = p.op("vector",
                              (lambda e, xt=xt, ac=ac, c=c, j=j:
                               e.scalar_tensor_tensor(
                                  out=ac[:], in0=xt[:, j:j + 512],
                                  scalar=wcol[:, c * 4 + j:c * 4 + j + 1],
                                  in1=ac[:], op0=ALU.mult, op1=ALU.add)),
                              after=[v0])
                ldp.used(v0)
                of, w3 = ofp.acquire()
                s1 = p.op("scalar",
                          (lambda e, ac=ac, of=of, c=c: e.activation(
                              out=of[:], in_=ac[:], func=AF.Silu,
                              bias=bcv[:, c:c + 1])),
                          after=[v0, t2, w3])
                acp.used(s1)
                ob, w4 = obp.acquire()
                cv = p.op("vector", (lambda e, of=of, ob=ob: e.tensor_copy(
                    out=ob[:], in_=of[:])), after=[s1, w4])
                st1 = p.dma("gpsimd",
                            xm_f[c * 128:(c + 1) * 128,
                                 tb * 512:(tb + 1) * 512],
                            of[:], sem="st", after=[cv])
                ofp.used(st1)
                st2 = p.dma("gpsimd",
                            xm_b[c * 128:(c + 1) * 128,
                                 tb * 512:(tb + 1) * 512],
                            ob[:], sem="st", after=[cv])
                obp.used(st2)
    p.bar()

    with scope(p):
        xm_get, xm_tok, _ = load_fm_resident(p, xm_b, DI, T, name + "xmr")
        wst = p.sb([128, 96], F32)
        wbf = p.sb([128, 16 * 96], BF16)
        for k in range(16):
            tl = p.dma("sync", wst[:],
                       io["l3a_xpwT"][k * 128:(k + 1) * 128, :],
                       after=[p.eng_at("vector")] if k else ())
            wtok = p.op("vector",
                        (lambda e, k=k, wbf=wbf, wst=wst: e.tensor_copy(
                            out=wbf[:, k * 96:(k + 1) * 96], in_=wst[:])),
                        after=[tl])
        pb = p.psum([96, 512], F32)
        evp = Pool(p, 3, [96, 512], F32, tag=name + "xd")
        for tb in range(8):
            last = None
            for k in range(16):
                xap = xm_get(k, 0, T)[0]
                last = p.op("tensor",
                            (lambda e, k=k, tb=tb, wbf=wbf, pb=pb, xap=xap:
                             e.matmul(
                                pb[:],
                                wbf[:, k * 96:(k + 1) * 96],
                                xap[:, tb * 512:(tb + 1) * 512],
                                start=(k == 0), stop=(k == 15))),
                            after=[xm_tok, wtok, p.eng_at("vector")]
                            if k == 0 else (), inc=(k == 15))
            ev, war = evp.acquire()
            e1 = p.op("vector", (lambda e, ev=ev, pb=pb: e.tensor_copy(
                out=ev[:], in_=pb[:])), after=[last, war])
            st1 = p.dma("gpsimd", xdbl_h[:, tb * 512:(tb + 1) * 512],
                        ev[:], sem="st", after=[e1])
            evp.used(st1)
    p.bar()

    with scope(p):
        xd = p.sb([64, T], F32)
        xd_b = p.sb([64, T], BF16)
        tl = p.dma("sync", xd[:], xdbl_h[0:64, :])
        cv = p.op("vector", lambda e: e.tensor_copy(out=xd_b[:], in_=xd[:]),
                  after=[tl])
        wst = p.sb([64, 128], F32)
        wbf = p.sb([64, 16 * 128], BF16)
        for o in range(16):
            t2 = p.dma("sync", wst[:],
                       io["l3a_dtwT"][:, o * 128:(o + 1) * 128],
                       after=[p.eng_at("vector")] if o else ())
            wtok = p.op("vector",
                        (lambda e, o=o, wbf=wbf, wst=wst: e.tensor_copy(
                            out=wbf[:, o * 128:(o + 1) * 128], in_=wst[:])),
                        after=[t2])
        bcol = p.sb([128, 16], F32)
        t3 = p.dma("sync", bcol[:], io["l3a_dtb"])
        pb = p.psum([128, 512], F32)
        sp1 = Pool(p, 2, [128, 512], F32, tag=name + "d1")
        sp2 = Pool(p, 2, [128, 512], F32, tag=name + "d2")
        sp3 = Pool(p, 2, [128, 512], F32, tag=name + "d3")
        xmp = Pool(p, 2, [128, 512], F32, tag=name + "d4")
        prev_ev = None
        for o in range(16):
            for tb in range(8):
                mm = p.op("tensor",
                          (lambda e, o=o, tb=tb, pb=pb, wbf=wbf: e.matmul(
                              pb[:], wbf[:, o * 128:(o + 1) * 128],
                              xd_b[:, tb * 512:(tb + 1) * 512],
                              start=True, stop=True)),
                          after=[cv, wtok, t3]
                          + ([prev_ev] if prev_ev else []))
                a_, w1 = sp1.acquire()
                z1 = p.op("vector",
                          (lambda e, a_=a_, o=o, pb=pb, bcol=bcol:
                           e.tensor_scalar(
                              out=a_[:], in0=pb[:],
                              scalar1=bcol[:, o:o + 1], scalar2=None,
                              op0=ALU.add)),
                          after=[mm, w1])
                prev_ev = z1
                b_, w2 = sp2.acquire()
                s1 = p.op("scalar",
                          (lambda e, a_=a_, b_=b_: e.activation(
                              out=b_[:], in_=a_[:], func=AF.Abs)),
                          after=[z1, w2])
                s2 = p.op("scalar",
                          (lambda e, b_=b_: e.activation(
                              out=b_[:], in_=b_[:], func=AF.Exp,
                              scale=-1.0)),
                          after=[s1])
                s3 = p.op("scalar",
                          (lambda e, b_=b_: e.activation(
                              out=b_[:], in_=b_[:], func=AF.Ln, bias=1.0)),
                          after=[s2])
                r1 = p.op("vector",
                          (lambda e, a_=a_: e.tensor_scalar_max(
                              a_[:], a_[:], 0.0)),
                          after=[s1])
                r2 = p.op("vector",
                          (lambda e, a_=a_, b_=b_: e.tensor_add(
                              a_[:], a_[:], b_[:])),
                          after=[r1, s3])
                sp2.used(r2)
                st1 = p.dma("gpsimd",
                            delta_h[o * 128:(o + 1) * 128,
                                    tb * 512:(tb + 1) * 512],
                            a_[:], sem="st", after=[r2])
                xm_, w3 = xmp.acquire()
                t4 = p.dma("sync", xm_[:],
                           xm_f[o * 128:(o + 1) * 128,
                                tb * 512:(tb + 1) * 512], after=[w3])
                c_, w4 = sp3.acquire()
                m1 = p.op("vector",
                          (lambda e, a_=a_, xm_=xm_, c_=c_: e.tensor_mul(
                              c_[:], a_[:], xm_[:])),
                          after=[r2, t4, w4])
                sp1.used(m1)
                xmp.used(m1)
                st2 = p.dma("gpsimd",
                            du_h[o * 128:(o + 1) * 128,
                                 tb * 512:(tb + 1) * 512],
                            c_[:], sem="st", after=[m1])
                sp3.used(st2)
    p.bar()

    with scope(p):
        A_sb = p.sb([128, 16 * 16], F32, name=name + "A")
        t1 = p.dma("sync", A_sb[:], io["l3a_alog"])
        s1 = p.op("scalar", lambda e: e.activation(out=A_sb[:], in_=A_sb[:],
                                                   func=AF.Exp), after=[t1])
        atok = p.op("vector", lambda e: e.tensor_scalar_mul(
            A_sb[:], A_sb[:], -1.0), after=[s1])

        carry = p.sb([128, 256], F32, name=name + "carry")
        ztok = p.op("vector", lambda e: e.memset(carry[:], 0.0))
        Brow = p.sb([1, 512], F32)
        Crow = p.sb([1, 512], F32)
        Bb = p.sb([128, 512], F32)
        Cb = p.sb([128, 512], F32)
        ones1f = p.sb([1, 128], F32)
        p.op("vector", lambda e: e.memset(ones1f[:], 1.0))
        otok = p.eng_at("vector")
        pb_b = p.psum([128, 512], F32)
        pb_c = p.psum([128, 512], F32)
        delta_sb = p.sb([128, 16 * 512], F32, name=name + "dsb")
        du_sb = p.sb([128, 16 * 512], F32, name=name + "dusb")
        y_acc = p.sb([128, 16 * 512], F32, name=name + "yacc")
        dA = p.sb([128, 512], F32)
        bt = p.sb([128, 512], F32)
        ht = p.sb([128, 512], F32)
        zt = p.sb([128, 512], F32)
        Dcol = p.sb([128, 16], F32)
        dtok = p.dma("sync", Dcol[:], io["l3a_dssm"])
        xm2 = Pool(p, 2, [128, 512], F32, tag=name + "sx")
        sr2 = Pool(p, 2, [128, 512], F32, tag=name + "ss")
        ob2 = Pool(p, 2, [128, 512], BF16, tag=name + "so")
        for tc in range(8):
            tsl = slice(tc * 512, (tc + 1) * 512)
            ltoks = []
            for c in range(16):
                ltoks.append(p.dma("sync",
                                   delta_sb[:, c * 512:(c + 1) * 512],
                                   delta_h[c * 128:(c + 1) * 128, tsl],
                                   after=[p.eng_at("scalar"),
                                          p.eng_at("vector")]
                                   if c == 0 else ()))
                ltoks.append(p.dma("sync", du_sb[:, c * 512:(c + 1) * 512],
                                   du_h[c * 128:(c + 1) * 128, tsl]))
            ltok = ltoks[-1]
            prev_n = None
            for n in range(16):
                t1 = p.dma("sync", Brow[:],
                           xdbl_h[64 + n:65 + n, tsl],
                           after=[prev_n] if prev_n else [])
                t2 = p.dma("sync", Crow[:], xdbl_h[80 + n:81 + n, tsl])
                m1 = p.op("tensor", (lambda e: e.matmul(
                    pb_b[:], ones1f[:], Brow[:], start=True, stop=True)),
                    after=[t1, otok] + ([prev_n] if prev_n else []),
                    inc=False)
                m2 = p.op("tensor", (lambda e: e.matmul(
                    pb_c[:], ones1f[:], Crow[:], start=True, stop=True)),
                    after=[t2])
                bb1 = p.op("vector", lambda e: e.tensor_copy(out=Bb[:],
                                                             in_=pb_b[:]),
                           after=[m2])
                cb1 = p.op("vector", lambda e: e.tensor_copy(out=Cb[:],
                                                             in_=pb_c[:]))
                last_n = None
                for dc in range(16):
                    d1 = p.op("scalar",
                              (lambda e, dc=dc, n=n: e.activation(
                                  out=dA[:], in_=delta_sb[:, dc * 512:
                                                          (dc + 1) * 512],
                                  func=AF.Exp,
                                  scale=A_sb[:, dc * 16 + n:
                                             dc * 16 + n + 1])),
                              after=[ltok, atok, cb1]
                              + ([last_n] if last_n else []))
                    v1 = p.op("vector",
                              (lambda e, dc=dc: e.tensor_mul(
                                  bt[:], du_sb[:, dc * 512:(dc + 1) * 512],
                                  Bb[:])),
                              after=[d1, bb1])
                    v2 = p.op("vector",
                              (lambda e, dc=dc, n=n: e.tensor_tensor_scan(
                                  out=ht[:], data0=dA[:], data1=bt[:],
                                  initial=carry[:, dc * 16 + n:
                                                dc * 16 + n + 1],
                                  op0=ALU.mult, op1=ALU.add)),
                              after=[v1, ztok])
                    v3 = p.op("vector",
                              (lambda e, dc=dc, n=n: e.tensor_copy(
                                  out=carry[:, dc * 16 + n:dc * 16 + n + 1],
                                  in_=ht[:, 511:512])),
                              after=[v2])
                    v4 = p.op("vector",
                              (lambda e: e.tensor_mul(zt[:], ht[:], Cb[:])),
                              after=[v2])
                    if n == 0:
                        last_n = p.op("vector",
                                      (lambda e, dc=dc: e.tensor_copy(
                                          out=y_acc[:, dc * 512:
                                                    (dc + 1) * 512],
                                          in_=zt[:])),
                                      after=[v4])
                    else:
                        last_n = p.op("vector",
                                      (lambda e, dc=dc: e.tensor_add(
                                          y_acc[:, dc * 512:(dc + 1) * 512],
                                          y_acc[:, dc * 512:(dc + 1) * 512],
                                          zt[:])),
                                      after=[v4])
                prev_n = last_n
            for dc in range(16):
                xm_, w1 = xm2.acquire()
                t5 = p.dma("sync", xm_[:],
                           xm_f[dc * 128:(dc + 1) * 128, tsl], after=[w1])
                sr_, w2 = sr2.acquire()
                t6 = p.dma("sync", sr_[:],
                           sres[dc * 128:(dc + 1) * 128, tsl], after=[w2])
                y1 = p.op("vector",
                          (lambda e, xm_=xm_, dc=dc: e.scalar_tensor_tensor(
                              out=y_acc[:, dc * 512:(dc + 1) * 512],
                              in0=xm_[:],
                              scalar=Dcol[:, dc:dc + 1],
                              in1=y_acc[:, dc * 512:(dc + 1) * 512],
                              op0=ALU.mult, op1=ALU.add)),
                          after=[t5, dtok, prev_n])
                xm2.used(y1)
                ob, w3 = ob2.acquire()
                y2 = p.op("vector",
                          (lambda e, sr_=sr_, ob=ob, dc=dc: e.tensor_tensor(
                              out=ob[:], in0=y_acc[:, dc * 512:
                                                   (dc + 1) * 512],
                              in1=sr_[:], op=ALU.mult)),
                          after=[y1, t6, w3])
                sr2.used(y2)
                st1 = p.dma("gpsimd",
                            y_bf[dc * 128:(dc + 1) * 128, tsl],
                            ob[:], sem="st", after=[y2])
                ob2.used(st1)
            p.bar(("tensor", "vector", "scalar"))
    p.bar()

    with scope(p):
        y_get = load_fm_blocked(p, y_bf, DI, 2048, name + "yr")
        bcol = p.sb([128, 8], F32)
        btok = p.dma("sync", bcol[:], io["l3a_outb"])
        xrp = Pool(p, 3, [128, 512], F32, tag=name + "or")
        ofp = Pool(p, 3, [128, 512], F32, tag=name + "oo")

        def out_epi(no, ts0, banks, mmtok):
            last = None
            for (bank, t0, tw) in banks:
                of, w2 = ofp.acquire()
                last = p.op("vector",
                            (lambda e, bank=bank, of=of, no=no, bcol=bcol:
                             e.tensor_scalar(
                                out=of[:], in0=bank[:, :512],
                                scalar1=bcol[:, no:no + 1], scalar2=None,
                                op0=ALU.add)),
                            after=[mmtok, btok, w2])
                if add_fm is not None:
                    xr, w1 = xrp.acquire()
                    tl = p.dma("sync", xr[:],
                               add_fm[no * 128:(no + 1) * 128, t0:t0 + tw],
                               after=[w1])
                    last = p.op("vector",
                                (lambda e, of=of, xr=xr: e.tensor_add(
                                    of[:], of[:], xr[:])),
                                after=[tl, last])
                    xrp.used(last)
                st1 = p.dma("gpsimd",
                            out_fm[no * 128:(no + 1) * 128, t0:t0 + tw],
                            of[:], sem="st", after=[last])
                ofp.used(st1)
            return last

        gemm_fm(p, io["l3a_outwT"], lambda k, t0, tw: y_get(k, t0, tw),
                out_epi, K=DI, NO=D, Tt=T, name=name + "og", TBLK=2048)
    p.bar()


def copy_hbm(p, src, dst):
    with scope(p):
        cv = Pool(p, 3, [128, 512], F32, tag="cph")
        for c in range(8):
            for tb in range(8):
                t_, war = cv.acquire()
                tl = p.dma("sync", t_[:],
                           src[c * 128:(c + 1) * 128,
                               tb * 512:(tb + 1) * 512], after=[war])
                st1 = p.dma("gpsimd",
                            dst[c * 128:(c + 1) * 128,
                                tb * 512:(tb + 1) * 512],
                            t_[:], sem="st", after=[tl])
                cv.used(st1)
    p.bar()


def emit_l3_attn(p, io, upto=99):
    x_in, x_out = io["x"], io["xo"]
    xn_f = p.dram_tmp("l3a_xnf", [D, T], F32)
    xn_bd = p.dram_tmp("l3a_xnb", [D, T], BF16)
    h1_bf = p.dram_tmp("l3a_h1", [D, T], BF16)
    h2_bf = p.dram_tmp("l3a_h2", [D, T], BF16)
    x1_h = p.dram_tmp("l3a_x1", [D, T], F32)
    out2_h = p.dram_tmp("l3a_out2", [D, T], F32)
    out2_bf = p.dram_tmp("l3a_out2b", [D, T], BF16)
    rksc_h = p.dram_tmp("l3a_rksc", [D, T], BF16)
    wksc_h = p.dram_tmp("l3a_wksc", [D, T], BF16)
    wk_bf = p.dram_tmp("l3a_wkb", [D, T], BF16)
    wk_tm = p.dram_tmp("l3a_wktm", [T, D], BF16)
    v_tm = p.dram_tmp("l3a_vtm", [T, D], BF16)
    vsc_tm = p.dram_tmp("l3a_vsctm", [T, D], BF16)
    reads_h = p.dram_tmp("l3a_reads", [D, T], BF16)

    emit_ln_fm(p, x_in, xn_bd, xn_f, name="l3aln")
    emit_rms_fm(p, xn_f, h1_bf, io["l3a_n1"], 1e-5, "l3arms1")
    emit_mamba(p, io, h1_bf, x1_h, xn_f, "l3amb1")
    if upto <= 1:
        copy_hbm(p, x1_h, x_out)
        return
    emit_rms_fm(p, x1_h, h2_bf, io["l3a_n2"], 1e-5, "l3arms2")
    emit_mamba(p, io, h2_bf, out2_h, None, "l3amb2")
    if upto <= 2:
        copy_hbm(p, out2_h, x_out)
        return

    with scope(p):
        cvp = Pool(p, 3, [128, 512], F32, tag="l3acv")
        cvo = Pool(p, 3, [128, 512], BF16, tag="l3acvo")
        for c in range(8):
            for tb in range(8):
                t_, war = cvp.acquire()
                tl = p.dma("sync", t_[:],
                           out2_h[c * 128:(c + 1) * 128,
                                  tb * 512:(tb + 1) * 512], after=[war])
                o_, w2 = cvo.acquire()
                cv = p.op("vector", (lambda e, t_=t_, o_=o_: e.tensor_copy(
                    out=o_[:], in_=t_[:])), after=[tl, w2])
                cvp.used(cv)
                st1 = p.dma("gpsimd",
                            out2_bf[c * 128:(c + 1) * 128,
                                    tb * 512:(tb + 1) * 512],
                            o_[:], sem="st", after=[cv])
                cvo.used(st1)
    p.bar()

    with scope(p):
        dec = p.sb([1, 1], F32)
        lgam = p.sb([1, 2], F32)
        frow = p.sb([1, T], F32)
        frown = p.sb([1, T], F32)
        fcoln = p.sb([128, 32], F32)
        ir = p.sb([1, T], F32)
        ic = p.sb([128, 32], F32)
        t1 = p.dma("sync", dec[:], io["l3a_decay"])
        t2 = p.dma("sync", ir[:], io["l3a_iotar"])
        t3 = p.dma("sync", ic[:], io["l3a_iotac"])
        s1 = p.op("scalar", lambda e: e.activation(out=lgam[0:1, 0:1],
                                                   in_=dec[:],
                                                   func=AF.Sigmoid),
                  after=[t1])
        s2 = p.op("scalar", lambda e: e.activation(out=lgam[0:1, 0:1],
                                                   in_=lgam[0:1, 0:1],
                                                   func=AF.Ln), after=[s1])
        v1 = p.op("vector", lambda e: e.tensor_scalar(
            out=lgam[0:1, 1:2], in0=lgam[0:1, 0:1], scalar1=-1.0,
            scalar2=None, op0=ALU.mult), after=[s2])
        s3 = p.op("scalar", lambda e: e.activation(
            out=frow[:], in_=ir[:], func=AF.Exp,
            scale=lgam[0:1, 0:1]), after=[v1, t2])
        s4 = p.op("scalar", lambda e: e.activation(
            out=frown[:], in_=ir[:], func=AF.Exp,
            scale=lgam[0:1, 1:2], bias=lgam[0:1, 1:2]), after=[v1, t2])
        ones1f = p.sb([1, 128], F32)
        p.op("vector", lambda e: e.memset(ones1f[:], 1.0))
        otok = p.eng_at("vector")
        lgcol = p.sb([128, 1], F32)
        bscope0 = ExitStack()
        _old = p.stack
        p.stack = bscope0
        pb_1 = p.psum([128, 1], F32)
        p.stack = _old
        mg = p.op("tensor", lambda e: e.matmul(
            pb_1[:], ones1f[:], lgam[0:1, 1:2], start=True, stop=True),
            after=[v1, otok])
        lgc = p.op("vector", lambda e: e.tensor_copy(out=lgcol[:],
                                                     in_=pb_1[:]),
                   after=[mg])
        s5 = p.op("scalar", lambda e: e.activation(
            out=fcoln[:], in_=ic[:], func=AF.Exp,
            scale=lgcol[:, 0:1], bias=lgcol[:, 0:1]), after=[lgc, t3])
        ftok = p.eng_at("scalar")
        fb = p.sb([128, 512], F32)
        ldp = Pool(p, 3, [128, 512], F32, tag="l3am1")
        op1 = Pool(p, 3, [128, 512], BF16, tag="l3am2")
        op2 = Pool(p, 3, [128, 512], BF16, tag="l3am3")
        op3 = Pool(p, 3, [128, 512], BF16, tag="l3am4")
        sh = shift_loader(out2_h)(p)

        def shall(c, tb, xt, war):
            if tb == 0:
                z = p.op("vector", (lambda e, xt=xt: e.memset(xt[:, 0:1],
                                                              0.0)),
                         after=[war])
                return p.dma("sync", xt[:, 1:512],
                             out2_h[c * 128:(c + 1) * 128, 0:511], after=[z])
            return p.dma("sync", xt[:],
                         out2_h[c * 128:(c + 1) * 128,
                                tb * 512 - 1:(tb + 1) * 512 - 1],
                         after=[war])

        bscope = ExitStack()
        old_stack = p.stack
        p.stack = bscope
        pb_f = p.psum([128, 512], F32)
        p.stack = old_stack
        prev_bc = None
        for (f_src, x_mode, dst) in ((frow, None, rksc_h),
                                     (frown, "shift", wksc_h)):
            for tb in range(8):
                tsl = slice(tb * 512, (tb + 1) * 512)
                mm = p.op("tensor", (lambda e, tsl=tsl, f_src=f_src:
                                     e.matmul(pb_f[:], ones1f[:],
                                              f_src[0:1, tsl], start=True,
                                              stop=True)),
                          after=[ftok, otok] + ([prev_bc] if prev_bc
                                                else []))
                bc = p.op("vector", (lambda e: e.tensor_copy(out=fb[:],
                                                             in_=pb_f[:])),
                          after=[mm])
                prev_bc = bc
                for c in range(8):
                    xt, war = ldp.acquire()
                    if x_mode is None:
                        tl = p.dma("sync", xt[:],
                                   out2_h[c * 128:(c + 1) * 128, tsl],
                                   after=[war])
                    else:
                        tl = shall(c, tb, xt, war)
                    ob, w2 = op1.acquire()
                    m2 = p.op("vector",
                              (lambda e, xt=xt, ob=ob: e.tensor_tensor(
                                  out=ob[:], in0=xt[:], in1=fb[:],
                                  op=ALU.mult)),
                              after=[tl, bc, w2])
                    ldp.used(m2)
                    prev_bc = m2
                    st1 = p.dma("gpsimd",
                                dst[c * 128:(c + 1) * 128, tsl],
                                ob[:], sem="st", after=[m2])
                    op1.used(st1)
                    if x_mode == "shift":
                        ob2, w3 = op2.acquire()
                        cv = p.op("vector",
                                  (lambda e, xt=xt, ob2=ob2: e.tensor_copy(
                                      out=ob2[:], in_=xt[:])),
                                  after=[tl, w3])
                        st2 = p.dma("gpsimd",
                                    wk_bf[c * 128:(c + 1) * 128, tsl],
                                    ob2[:], sem="st", after=[cv])
                        op2.used(st2)
        p.bar()
        bscope.close()
        bscope0.close()
        if upto == 21:
            return
        ident = p.sb([128, 128], BF16, name="l3aidm")
        itok = p.dma("sync", ident[:], io["ident_bf"])
        emit_transpose_hbm(p, wk_bf, wk_tm, D, T, BF16, "l3awkT", ident,
                           itok)
        p.bar()

        with scope(p):
            o2_get, o2_tok, _ = load_fm_resident(p, out2_bf, D, T, "l3ao2r")
            wv_st = p.sb([128, 1024], F32)
            wv_bf = p.sb([128, 8 * 1024], BF16)
            for k in range(8):
                tl = p.dma("sync", wv_st[:],
                           io["l3a_wwT"][k * 128:(k + 1) * 128, :],
                           after=[p.eng_at("vector")] if k else ())
                wvtok = p.op("vector",
                             (lambda e, k=k: e.tensor_copy(
                                 out=wv_bf[:, k * 1024:(k + 1) * 1024],
                                 in_=wv_st[:])),
                             after=[tl])

            def v_epi(tch, banks, mmtok):
                last = None
                for (bank, n0, nw) in banks:
                    ob, war = op1.acquire()
                    c1 = p.op("vector",
                              (lambda e, bank=bank, ob=ob, nw=nw:
                               e.tensor_copy(out=ob[:, :nw],
                                             in_=bank[:, :nw])),
                              after=[mmtok, war])
                    st1 = p.dma("gpsimd",
                                v_tm[tch * 128:(tch + 1) * 128, n0:n0 + nw],
                                ob[:, :nw], sem="st", after=[c1])
                    op1.used(st1)
                    ob2, w2 = op2.acquire()
                    last = p.op("vector",
                                (lambda e, bank=bank, ob2=ob2, nw=nw,
                                 tch=tch: e.tensor_scalar(
                                    out=ob2[:, :nw], in0=bank[:, :nw],
                                    scalar1=fcoln[:, tch:tch + 1],
                                    scalar2=None, op0=ALU.mult)),
                                after=[mmtok, ftok, w2])
                    st2 = p.dma("gpsimd",
                                vsc_tm[tch * 128:(tch + 1) * 128,
                                       n0:n0 + nw],
                                ob2[:, :nw], sem="st", after=[last])
                    op2.used(st2)
                return last

            gemm_tm(p, (lambda k: (wv_bf[:, k * 1024:(k + 1) * 1024],
                                   wvtok)),
                    lambda k, t0, tw: o2_get(k, t0, tw),
                    v_epi, K=D, NO=D, Tt=T, name="l3av")
        p.bar()
        if upto == 22:
            return

        with scope(p):
            rkg = p.sb([128, 8 * 256], BF16, name="l3arkg")
            wksg = p.sb([128, 8 * 256], BF16, name="l3awksg")
            What = p.sb([128, 8 * 1024], F32, name="l3aWhat")
            Wbf = p.sb([128, 8 * 1024], BF16, name="l3aWbf")
            mskg = p.sb([128, 2 * 256], F32)
            mtok = p.dma("sync", mskg[:], io["l3a_gmask"])
            zt1 = p.op("vector", lambda e: e.memset(What[:], 0.0))
            zt2 = p.op("vector", lambda e: e.memset(Wbf[:], 0.0))
            S_sb = p.sb([128, 2 * 256], BF16)
            vg = Pool(p, 3, [128, 1024], BF16, tag="l3avg")
            wkg = Pool(p, 3, [128, 1024], BF16, tag="l3awg")
            vsg = Pool(p, 3, [128, 1024], BF16, tag="l3avsg")
            rdp = Pool(p, 3, [128, 256], BF16, tag="l3ard")
            wuptok = [p.eng_at("vector")]
            for G in range(16):
                g0 = G * 256
                gtoks = []
                for dcc in range(8):
                    gtoks.append(p.dma(
                        "sync", rkg[:, dcc * 256:(dcc + 1) * 256],
                        rksc_h[dcc * 128:(dcc + 1) * 128, g0:g0 + 256],
                        after=[p.eng_at("tensor")] if dcc == 0 else ()))
                    gtoks.append(p.dma(
                        "sync", wksg[:, dcc * 256:(dcc + 1) * 256],
                        wksc_h[dcc * 128:(dcc + 1) * 128, g0:g0 + 256]))
                gtok = gtoks[-1]
                rk_get = lambda dcc, a, b: (rkg[:, dcc * 256:(dcc + 1) * 256]
                                            .rearrange("p (a c) -> p a c",
                                                       a=1)[:, 0], gtok)                     if False else None
                with scope(p):
                    pb_s = [p.psum([128, 256], F32) for _ in range(2)]
                    stoks = []
                    for jc in range(2):
                        last = None
                        for dcc in range(8):
                            last = p.op(
                                "tensor",
                                (lambda e, jc=jc, dcc=dcc: e.matmul(
                                    pb_s[jc][:],
                                    wksg[:, dcc * 256 + jc * 128:
                                         dcc * 256 + (jc + 1) * 128],
                                    rkg[:, dcc * 256:(dcc + 1) * 256],
                                    start=(dcc == 0), stop=(dcc == 7))),
                                after=[gtok, mtok, zt2, wuptok[0]]
                                if dcc == 0 else (), inc=(dcc == 7))
                        st_ = p.op("vector",
                                   (lambda e, jc=jc: e.tensor_tensor(
                                       out=S_sb[:, jc * 256:(jc + 1) * 256],
                                       in0=pb_s[jc][:],
                                       in1=mskg[:, jc * 256:(jc + 1) * 256],
                                       op=ALU.mult)),
                                   after=[last])
                        stoks.append(st_)
                    stok = stoks[-1]
                with scope(p):
                    pb_r = [p.psum([128, 256], F32) for _ in range(8)]
                    vts = []
                    for jc in range(2):
                        vt_, war = vg.acquire()
                        vts.append((vt_, p.dma(
                            "sync", vt_[:],
                            v_tm[g0 + jc * 128:g0 + (jc + 1) * 128, :],
                            after=[war, stok])))
                    last = None
                    do_inter = (upto >= 24)
                    for d2 in range(8):
                        for jc in range(2):
                            vt_, vtok_ = vts[jc]
                            last = p.op(
                                "tensor",
                                (lambda e, d2=d2, jc=jc, vt_=vt_,
                                 sp=(jc == 1 and not do_inter): e.matmul(
                                    pb_r[d2][:],
                                    vt_[:, d2 * 128:(d2 + 1) * 128],
                                    S_sb[:, jc * 256:(jc + 1) * 256],
                                    start=(jc == 0), stop=sp)),
                                after=[vts[0][1], vts[1][1], stok]
                                if (d2 == 0 and jc == 0) else (),
                                inc=(jc == 1 and not do_inter))
                        if do_inter:
                            for ec in range(8):
                                last = p.op(
                                    "tensor",
                                    (lambda e, d2=d2, ec=ec: e.matmul(
                                        pb_r[d2][:],
                                        Wbf[:, ec * 1024 + d2 * 128:
                                            ec * 1024 + (d2 + 1) * 128],
                                        rkg[:, ec * 256:(ec + 1) * 256],
                                        start=False, stop=(ec == 7))),
                                    inc=(ec == 7))
                    vg.used(last)
                    vg.used(last)
                    mmt = p.eng_at("tensor")
                    evtok = None
                    for d2 in range(8):
                        rd, war = rdp.acquire()
                        ev = p.op("vector",
                                  (lambda e, d2=d2, rd=rd: e.tensor_copy(
                                      out=rd[:], in_=pb_r[d2][:])),
                                  after=[mmt, war])
                        st1 = p.dma("gpsimd",
                                    reads_h[d2 * 128:(d2 + 1) * 128,
                                            g0:g0 + 256],
                                    rd[:], sem="st", after=[ev])
                        rdp.used(st1)
                        evtok = ev
                if upto == 23 or upto == 24:
                    wuptok[0] = evtok
                    continue
                with scope(p):
                    pb_w = [p.psum([128, 512], F32) for _ in range(8)]
                    wkts = []
                    vsts = []
                    for jc in range(2):
                        wk_, war = wkg.acquire()
                        wkts.append((wk_, p.dma(
                            "sync", wk_[:],
                            wk_tm[g0 + jc * 128:g0 + (jc + 1) * 128, :],
                            after=[war, evtok])))
                        vs_, war2 = vsg.acquire()
                        vsts.append((vs_, p.dma(
                            "sync", vs_[:],
                            vsc_tm[g0 + jc * 128:g0 + (jc + 1) * 128, :],
                            after=[war2])))
                    uts = []
                    for half in range(2):
                        last = None
                        for ec in range(8):
                            for jc in range(2):
                                wk_, wtok_ = wkts[jc]
                                vs_, vtok_ = vsts[jc]
                                last = p.op(
                                    "tensor",
                                    (lambda e, ec=ec, jc=jc, half=half,
                                     wk_=wk_, vs_=vs_: e.matmul(
                                        pb_w[ec][:],
                                        wk_[:, ec * 128:(ec + 1) * 128],
                                        vs_[:, half * 512:
                                            (half + 1) * 512],
                                        start=(jc == 0), stop=(jc == 1))),
                                    after=([wkts[0][1], wkts[1][1],
                                            vsts[0][1], vsts[1][1], evtok]
                                           if (ec == 0 and jc == 0
                                               and half == 0) else
                                           ([uts[-1]] if (ec == 0 and jc == 0)
                                            else ())), inc=(jc == 1))
                        mmt2 = p.eng_at("tensor")
                        for ec in range(8):
                            u1 = p.op(
                                "vector",
                                (lambda e, ec=ec, half=half: e.tensor_add(
                                    What[:, ec * 1024 + half * 512:
                                         ec * 1024 + (half + 1) * 512],
                                    What[:, ec * 1024 + half * 512:
                                         ec * 1024 + (half + 1) * 512],
                                    pb_w[ec][:])),
                                after=[mmt2, zt1])
                            uts.append(p.op(
                                "vector",
                                (lambda e, ec=ec, half=half: e.tensor_copy(
                                    out=Wbf[:, ec * 1024 + half * 512:
                                            ec * 1024 + (half + 1) * 512],
                                    in_=What[:, ec * 1024 + half * 512:
                                             ec * 1024 + (half + 1)
                                             * 512])),
                                after=[u1]))
                    wkg.used(uts[-1])
                    wkg.used(uts[-1])
                    vsg.used(uts[-1])
                    vsg.used(uts[-1])
                    wuptok[0] = uts[-1]
    p.bar()
    if upto <= 3:
        with scope(p):
            cv = Pool(p, 3, [128, 512], BF16, tag="cpr")
            co = Pool(p, 3, [128, 512], F32, tag="cpo")
            for c in range(8):
                for tb in range(8):
                    t_, war = cv.acquire()
                    tl = p.dma("sync", t_[:],
                               reads_h[c * 128:(c + 1) * 128,
                                       tb * 512:(tb + 1) * 512], after=[war])
                    o_, w2 = co.acquire()
                    cc = p.op("vector", (lambda e, t_=t_, o_=o_:
                                         e.tensor_copy(out=o_[:],
                                                       in_=t_[:])),
                              after=[tl, w2])
                    cv.used(cc)
                    st1 = p.dma("gpsimd",
                                x_out[c * 128:(c + 1) * 128,
                                      tb * 512:(tb + 1) * 512],
                                o_[:], sem="st", after=[cc])
                    co.used(st1)
        p.bar()
        return

    with scope(p):
        r_get = load_fm_blocked(p, reads_h, D, 2048, "l3ardr")
        ap_ = Pool(p, 3, [128, 512], F32, tag="l3af1")
        bp_ = Pool(p, 3, [128, 512], F32, tag="l3af2")
        cp_ = Pool(p, 3, [128, 512], F32, tag="l3af3")
        ofp = Pool(p, 3, [128, 512], F32, tag="l3af4")

        def fin_epi(no, ts0, banks, mmtok):
            last = None
            for (bank, t0, tw) in banks:
                sl0 = slice(no * 128, (no + 1) * 128)
                sl1 = slice(t0, t0 + tw)
                a_, w1 = ap_.acquire()
                t1 = p.dma("sync", a_[:], x_in[sl0, sl1], after=[w1])
                b_, w2 = bp_.acquire()
                t2 = p.dma("sync", b_[:], x1_h[sl0, sl1], after=[w2])
                c_, w3 = cp_.acquire()
                t3 = p.dma("sync", c_[:], out2_h[sl0, sl1], after=[w3])
                of, w4 = ofp.acquire()
                v1 = p.op("vector",
                          (lambda e, of=of, bank=bank: e.tensor_scalar(
                              out=of[:], in0=bank[:, :512], scalar1=0.1,
                              scalar2=None, op0=ALU.mult)),
                          after=[mmtok, w4])
                v2 = p.op("vector", (lambda e, of=of, a_=a_: e.tensor_add(
                    of[:], of[:], a_[:])), after=[t1, v1])
                ap_.used(v2)
                v3 = p.op("vector", (lambda e, of=of, b_=b_: e.tensor_add(
                    of[:], of[:], b_[:])), after=[t2, v2])
                bp_.used(v3)
                last = p.op("vector", (lambda e, of=of, c_=c_: e.tensor_add(
                    of[:], of[:], c_[:])), after=[t3, v3])
                cp_.used(last)
                st1 = p.dma("gpsimd", x_out[sl0, sl1], of[:], sem="st",
                            after=[last])
                ofp.used(st1)
            return last

        gemm_fm(p, io["l3a_wrT"], lambda k, t0, tw: r_get(k, t0, tw),
                fin_epi, K=D, NO=D, Tt=T, name="l3afin", TBLK=2048)
    p.bar()


def l3_attn_host_inputs(inp):
    import ml_dtypes
    iota_r = np.arange(T, dtype=np.float32).reshape(1, T)
    iota_c = np.ascontiguousarray(
        np.arange(T, dtype=np.float32).reshape(32, 128).T)
    gm = np.zeros((2, 128, 256), np.float32)
    for jc in range(2):
        j = jc * 128 + np.arange(128)[:, None]
        i = np.arange(256)[None, :]
        gm[jc] = (j < i).astype(np.float32)
    gmask = np.ascontiguousarray(gm.transpose(1, 0, 2).reshape(128, 512))
    d = {
        "l3a_n1": np.ascontiguousarray(
            inp["L3_attn_norm1_w"].reshape(8, 128).T),
        "l3a_n2": np.ascontiguousarray(
            inp["L3_attn_norm2_w"].reshape(8, 128).T),
        "l3a_inwT": np.ascontiguousarray(inp["L3_attn_in_proj_w"].T),
        "l3a_inb": np.ascontiguousarray(
            inp["L3_attn_in_proj_b"].reshape(32, 128).T),
        "l3a_convw": np.ascontiguousarray(
            inp["L3_attn_conv_w"][:, 0, :].reshape(16, 128, 4)
            .transpose(1, 0, 2).reshape(128, 64)),
        "l3a_convb": np.ascontiguousarray(
            inp["L3_attn_conv_b"].reshape(16, 128).T),
        "l3a_xpwT": np.ascontiguousarray(inp["L3_attn_x_proj_w"].T),
        "l3a_dtwT": np.ascontiguousarray(inp["L3_attn_dt_proj_w"].T),
        "l3a_dtb": np.ascontiguousarray(
            inp["L3_attn_dt_proj_b"].reshape(16, 128).T),
        "l3a_alog": np.ascontiguousarray(
            inp["L3_attn_A_log"].reshape(16, 128, 16)
            .transpose(1, 0, 2).reshape(128, 256)),
        "l3a_dssm": np.ascontiguousarray(
            inp["L3_attn_D_ssm"].reshape(16, 128).T),
        "l3a_outwT": np.ascontiguousarray(inp["L3_attn_out_proj_w"].T),
        "l3a_outb": np.ascontiguousarray(
            inp["L3_attn_out_proj_b"].reshape(8, 128).T),
        "l3a_wwT": np.ascontiguousarray(inp["L3_attn_proj_write_w"].T),
        "l3a_wrT": np.ascontiguousarray(inp["L3_attn_proj_read_w"].T),
        "l3a_decay": np.asarray(inp["L3_attn_decay"],
                                np.float32).reshape(1, 1),
        "l3a_iotar": iota_r, "l3a_iotac": iota_c,
        "l3a_gmask": gmask,
        "ident_bf": np.eye(128, dtype=ml_dtypes.bfloat16),
    }
    return d


def build_l3_attn_stage(upto=99):
    p = Prog("l3_attn")
    io = {"x": p.dram_in("x", [D, T], F32),
          "xo": p.dram_out("xo", [D, T], F32)}
    shapes = {"l3a_n1": [128, 8], "l3a_n2": [128, 8],
              "l3a_inwT": [1024, 4096], "l3a_inb": [128, 32],
              "l3a_convw": [128, 64], "l3a_convb": [128, 16],
              "l3a_xpwT": [2048, 96], "l3a_dtwT": [64, 2048],
              "l3a_dtb": [128, 16], "l3a_alog": [128, 256],
              "l3a_dssm": [128, 16], "l3a_outwT": [2048, 1024],
              "l3a_outb": [128, 8], "l3a_wwT": [1024, 1024],
              "l3a_wrT": [1024, 1024], "l3a_decay": [1, 1],
              "l3a_iotar": [1, T], "l3a_iotac": [128, 32],
              "l3a_gmask": [128, 512]}
    for nm, sh in shapes.items():
        io[nm] = p.dram_in(nm, sh, F32)
    io["ident_bf"] = p.dram_in("ident_bf", [128, 128], BF16)
    emit_l3_attn(p, io, upto)
    return p



_STAGE_DEFS = [
    ("l0_attn", build_l0_attn_stage, l0_attn_host_inputs),
    ("l0_ffn", build_l0_ffn_stage, l0_ffn_host_inputs),
    ("l1_attn", build_l1_attn_stage, l1_attn_host_inputs),
    ("l1_ffn", build_l1_ffn_stage, l1_ffn_host_inputs),
    ("l2_attn", build_l2_attn_stage, l2_attn_host_inputs),
    ("l2_ffn", build_l2_ffn_stage, l2_ffn_host_inputs),
    ("l3_attn", build_l3_attn_stage, l3_attn_host_inputs),
    ("l3_ffn", build_l3_ffn_stage, l3_ffn_host_inputs),
]

_CACHE = {}


def _get_stage(name):
    if name not in _CACHE:
        for nm, bf, hf in _STAGE_DEFS:
            if nm == name:
                p = bf()
                nc = p.finalize()
                _CACHE[name] = (p, nc, hf)
    return _CACHE[name]


def kernel(**inputs):
    x = np.ascontiguousarray(np.asarray(inputs["L0_attn_x"], np.float32))
    B = x.shape[0]
    x_fm = [np.ascontiguousarray(x[b].T) for b in range(B)]
    for name, _, _ in _STAGE_DEFS:
        p, nc, hf = _get_stage(name)
        shared = hf(inputs)
        shared.update(p.host_consts)
        in_maps = []
        for b in range(B):
            m = dict(shared)
            m["x"] = x_fm[b]
            in_maps.append(m)
        res = run_bass_kernel_spmd(nc, in_maps, core_ids=list(range(B)))
        x_fm = [np.ascontiguousarray(res.results[b]["xo"]) for b in range(B)]
    out = np.stack([x_fm[b].T for b in range(B)], axis=0)
    return out.astype(np.float32)
